# version 17
# baseline (speedup 1.0000x reference)
import contextlib
import numpy as np
import concourse.bass as bass
import concourse.mybir as mybir
from concourse.bass_utils import run_bass_kernel_spmd

F32 = mybir.dt.float32
BF16 = mybir.dt.bfloat16
U8 = mybir.dt.uint8
AF = mybir.ActivationFunctionType
ALU = mybir.AluOpType
AX = mybir.AxisListType

ENGINES = ("sync", "gpsimd", "tensor", "vector", "scalar")

NEG = -1.0e30
LN_EPS = 1e-5
L0A_EPS = 1e-15


class S:
    def __init__(self, name):
        self.name = name
        self.n = 0

    def bump(self, a=1):
        self.n += a
        return self.n


class Prog:
    def __init__(self):
        self.marks = []
        self.nmm = 0
        self.ops = {e: [] for e in ENGINES}
        self.lastwait = {e: {} for e in ENGINES}
        self.sems = {}
        for nm in ("ld", "cst", "pe", "ve", "ac", "bar", "sto", "po", "wa", "wb",
                   "wc", "wd", "w0", "w1", "w2", "w3", "w4", "w5", "w6", "w7"):
            self.sems[nm] = S(nm)
        self.ld = self.sems["ld"]
        self.cst = self.sems["cst"]
        self.pe = self.sems["pe"]
        self.ve = self.sems["ve"]
        self.ac = self.sems["ac"]
        self.bar = self.sems["bar"]
        self.sto = self.sems["sto"]
        self.w = [self.sems[f"w{i}"] for i in range(8)]
        self.po = self.sems["po"]

    def emit(self, eng, fn=None, waits=(), inc=None, amt=1):
        lw = self.lastwait[eng]
        w2 = []
        for (s, v) in waits:
            if v is None or v <= 0:
                continue
            if lw.get(s.name, -1) >= v:
                continue
            lw[s.name] = v
            w2.append((s, v))
        self.ops[eng].append((fn, tuple(w2), inc, amt))
        return inc.bump(amt) if inc is not None else None

    def sy(self, fn=None, waits=(), inc=None, amt=16):
        return self.emit("sync", fn, waits, inc, amt)

    def gp(self, fn=None, waits=(), inc=None, amt=16):
        return self.emit("gpsimd", fn, waits, inc, amt)

    def t(self, fn=None, waits=(), inc=None, amt=1):
        if fn is not None:
            self.nmm += 1
        return self.emit("tensor", fn, waits, inc, amt)

    def mark(self, name):
        self.marks.append((name, self.nmm))

    def v(self, fn=None, waits=(), inc=None, amt=1):
        return self.emit("vector", fn, waits, inc, amt)

    def s(self, fn=None, waits=(), inc=None, amt=1):
        return self.emit("scalar", fn, waits, inc, amt)

    def barrier(self):
        target = self.bar.n + 5
        for e in ENGINES:
            self.emit(e, None, inc=self.bar, amt=1)
        for e in ENGINES:
            self.emit(e, None, waits=[(self.bar, target)])
        return target

    def replay(self, nc, semh):
        with nc.Block() as block:
            for ename in ENGINES:
                ops = self.ops[ename]

                def runner(eng, ops=ops):
                    for fn, waits, inc, amt in ops:
                        for (s_, v_) in waits:
                            eng.wait_ge(semh[s_.name], v_)
                        ins = fn(eng) if fn is not None else None
                        if inc is not None:
                            if ins is None:
                                ins = eng.nop(nofuse=True)
                            ins.then_inc(semh[inc.name], amt)

                getattr(block, ename)(runner)


def build_nc(T=2048, blocks=("l0a", "l0f", "l1a", "l1f")):
    NT = T // 128
    ST = min(512, T)
    NS = T // ST
    STT = ST // 128
    C = 1024
    KC = 8
    F = 4096
    FC = 32

    nc = bass.Bass("TRN2", target_bir_lowering=False)

    xin = nc.dram_tensor("x", [T, C], F32, kind="ExternalInput").ap()
    wqkvT = nc.dram_tensor("wqkvT", [C, 3 * C], F32, kind="ExternalInput").ap()
    wprojT = nc.dram_tensor("wprojT", [C, C], F32, kind="ExternalInput").ap()
    bpj_in = nc.dram_tensor("bpj_d", [128, C], F32, kind="ExternalInput").ap()
    k0_in = nc.dram_tensor("k0", [2 * C, F], U8, kind="ExternalInput").ap()
    k1_in = nc.dram_tensor("k1", [F, F], U8, kind="ExternalInput").ap()
    k2_in = nc.dram_tensor("k2", [F, C], U8, kind="ExternalInput").ap()
    smallv = nc.dram_tensor("smallv_d", [128, 8], F32, kind="ExternalInput").ap()
    kTm = {w: nc.dram_tensor(f"{w}KT", [C, C], F32, kind="ExternalInput").ap() for w in ("q", "k", "v", "p")}
    vVm = {w: nc.dram_tensor(f"{w}V", [C, C], F32, kind="ExternalInput").ap() for w in ("q", "k", "v", "p")}
    fKT = nc.dram_tensor("fKT", [C, 4 * C], F32, kind="ExternalInput").ap()
    fV = nc.dram_tensor("fV", [4 * C, C], F32, kind="ExternalInput").ap()
    w1T = nc.dram_tensor("w1T", [C, F], F32, kind="ExternalInput").ap()
    w2T = nc.dram_tensor("w2T", [C, F], F32, kind="ExternalInput").ap()
    h1T = nc.dram_tensor("h1T", [F, F], F32, kind="ExternalInput").ap()
    h2T = nc.dram_tensor("h2T", [F, F], F32, kind="ExternalInput").ap()
    w3T = nc.dram_tensor("w3T", [F, C], F32, kind="ExternalInput").ap()
    sc32 = nc.dram_tensor("sc32_d", [128, 128], F32, kind="ExternalInput").ap()
    w3sb_in = nc.dram_tensor("w3sb_d", [128, C], F32, kind="ExternalInput").ap()
    idn_in = nc.dram_tensor("idn_d", [128, 128], F32, kind="ExternalInput").ap()
    bdiag_in = nc.dram_tensor("bdiag_d", [128, 128], F32, kind="ExternalInput").ap()
    sel4_in = nc.dram_tensor("sel4_d", [4, 128], F32, kind="ExternalInput").ap()
    cmask_in = nc.dram_tensor("cmask_d", [128, 128], F32, kind="ExternalInput").ap()
    nh_in = nc.dram_tensor("nh_d", [32, 128], F32, kind="ExternalInput").ap()

    yout = nc.dram_tensor("y", [T, C], F32, kind="ExternalOutput").ap()

    w0s = nc.dram_tensor("w0s", [C, F], BF16, kind="Internal").ap()
    w1s_d = nc.dram_tensor("w1s_d", [32 * F, 128], U8, kind="Internal").ap()
    fKb = nc.dram_tensor("fKb", [C, 4 * C], BF16, kind="Internal").ap()
    fVb = nc.dram_tensor("fVb", [4 * C, C], BF16, kind="Internal").ap()
    kTb = {w: nc.dram_tensor(f"{w}KTb", [C, C], BF16, kind="Internal").ap() for w in ("q", "k", "v", "p")}
    vVb = {w: nc.dram_tensor(f"{w}Vb", [C, C], BF16, kind="Internal").ap() for w in ("q", "k", "v", "p")}
    h1s_t = nc.dram_tensor("h1s_t", [32 * F, 128], BF16, kind="Internal").ap()
    h2s_t = nc.dram_tensor("h2s_t", [32 * F, 128], BF16, kind="Internal").ap()
    w1b = nc.dram_tensor("w1b", [C, F], BF16, kind="Internal").ap()
    w2b = nc.dram_tensor("w2b", [C, F], BF16, kind="Internal").ap()
    w3b = nc.dram_tensor("w3b", [F, C], BF16, kind="Internal").ap()
    xstash = nc.dram_tensor("xstash", [T, C], F32, kind="Internal").ap()

    P = Prog()
    ld, cst, pe, ve, ac, sto = P.ld, P.cst, P.pe, P.ve, P.ac, P.sto

    es = contextlib.ExitStack()
    with es:
        stat = es.enter_context(nc.sbuf_tensor("stat", [128, 512], F32))
        sq = es.enter_context(nc.sbuf_tensor("sq", [128, 1024], F32))
        nrm = es.enter_context(nc.sbuf_tensor("nrm", [128, 2048], BF16))
        ident = es.enter_context(nc.sbuf_tensor("ident", [128, 128], BF16))
        bdiag = es.enter_context(nc.sbuf_tensor("bdiag", [128, 128], F32))
        sel4 = es.enter_context(nc.sbuf_tensor("sel4", [4, 128], BF16))
        cmask = es.enter_context(nc.sbuf_tensor("cmask", [128, 128], F32))
        scv = es.enter_context(nc.sbuf_tensor("scv", [128, 128], F32))
        smv = es.enter_context(nc.sbuf_tensor("smv", [128, 16], F32))
        onec = es.enter_context(nc.sbuf_tensor("onec", [128, 32], BF16))
        nhalf = es.enter_context(nc.sbuf_tensor("nhalf", [32, 128], BF16))
        wq = es.enter_context(nc.sbuf_tensor("wq", [128, 16384], BF16))
        ysc = es.enter_context(nc.sbuf_tensor("ysc", [128, 1024], F32))

        SU, SQ_, MU, VP, RC, MB = 0, 16, 32, 48, 64, 80
        MU1, R1, MB1 = 96, 112, 128
        SMX = 160

        P.gp(lambda e: e.dma_start(out=ident[:, :], in_=idn_in[:, :]), inc=cst)
        P.gp(lambda e: e.dma_start(out=sel4[:, :], in_=sel4_in[:, :]), inc=cst)
        P.gp(lambda e: e.dma_start(out=nhalf[:, :], in_=nh_in[:, :]), inc=cst)
        consts_cst = cst.n
        P.sy(lambda e: e.dma_start(out=bdiag[:, :], in_=bdiag_in[:, :]), inc=ld)
        P.sy(lambda e: e.dma_start(out=cmask[:, :], in_=cmask_in[:, :]), inc=ld)
        P.sy(lambda e: e.dma_start(out=scv[:, :], in_=sc32[:, :]), inc=ld)
        P.sy(lambda e: e.dma_start(out=smv[:, 0:8], in_=smallv[:, :]), inc=ld)
        P.v(lambda e: e.memset(onec[:, :], 1.0), inc=ve)
        es1 = es.enter_context(contextlib.ExitStack())
        xr = es1.enter_context(nc.sbuf_tensor("xr", [128, NT * 1024], F32, side="right"))
        x3 = xin.rearrange("(t p) c -> p t c", p=128)
        P.sy(lambda e: e.dma_start(out=xr[:, :].rearrange("p (t c) -> p t c", c=1024), in_=x3),
             inc=ld)
        x_ld = init_ld = ld.n
        P.v(lambda e: e.tensor_scalar(smv[:, 4:5], smv[:, 0:1], 4.0, None, ALU.mult),
            waits=[(ld, init_ld)], inc=ve)
        P.v(lambda e: e.tensor_scalar(smv[:, 5:6], smv[:, 1:2], 2.0, None, ALU.mult), inc=ve)
        P.v(lambda e: e.tensor_scalar(smv[:, 7:8], smv[:, 1:2], 4.0, None, ALU.mult), inc=ve)
        P.v(lambda e: e.tensor_scalar(smv[:, 8:9], smv[:, 2:3], 2.0, None, ALU.mult), inc=ve)
        consts_ve = ve.n

        def ln_tile_stats(src_ap, col):
            vv = P.v(lambda e, a=src_ap, c=col: e.reduce_sum(stat[:, SU + c:SU + c + 1], a, axis=AX.X),
                     waits=[(ve, ve.n)], inc=ve)
            aa = P.s(lambda e, a=src_ap, c=col: e.activation(sq[:, :], a, AF.Square,
                                                             accum_out=stat[:, SQ_ + c:SQ_ + c + 1]),
                     waits=[(ac, ac.n)], inc=ac)
            return vv, aa

        def ln_finalize(nt, wv, wa, mu_c=MU, r_c=RC, mb_c=MB):
            P.v(lambda e, n=nt: e.tensor_scalar(stat[:, mu_c:mu_c + n], stat[:, SU:SU + n],
                                                1.0 / C, None, ALU.mult),
                waits=[(ve, wv)], inc=ve)
            P.v(lambda e, n=nt: e.tensor_scalar(stat[:, VP:VP + n], stat[:, SQ_:SQ_ + n],
                                                1.0 / C, LN_EPS, ALU.mult, ALU.add),
                waits=[(ac, wa)], inc=ve)
            P.v(lambda e, n=nt: e.tensor_mul(stat[:, SU:SU + n], stat[:, mu_c:mu_c + n],
                                             stat[:, mu_c:mu_c + n]),
                waits=[(ve, ve.n)], inc=ve)
            vsub = P.v(lambda e, n=nt: e.tensor_sub(stat[:, VP:VP + n], stat[:, VP:VP + n],
                                                    stat[:, SU:SU + n]),
                       waits=[(ve, ve.n)], inc=ve)
            a1 = P.s(lambda e, n=nt: e.activation(stat[:, VP:VP + n], stat[:, VP:VP + n], AF.Ln),
                     waits=[(ve, vsub), (ac, ac.n)], inc=ac)
            a2 = P.s(lambda e, n=nt, r_c=r_c: e.activation(stat[:, r_c:r_c + n], stat[:, VP:VP + n],
                                                           AF.Exp, scale=-0.5),
                     waits=[(ac, a1)], inc=ac)
            vmb = P.v(lambda e, n=nt, mu_c=mu_c, r_c=r_c, mb_c=mb_c: e.scalar_tensor_tensor(
                stat[:, mb_c:mb_c + n], stat[:, mu_c:mu_c + n], -1.0, stat[:, r_c:r_c + n],
                ALU.mult, ALU.mult),
                waits=[(ac, a2), (ve, ve.n)], inc=ve)
            return vmb, a2

        def ln_small(src_ap, base, w_extra=()):
            sv = P.v(lambda e, a=src_ap: e.reduce_sum(stat[:, base:base + 1], a, axis=AX.X),
                     waits=[(ve, ve.n)] + list(w_extra), inc=ve)
            sa = P.s(lambda e, a=src_ap: e.activation(sq[:, :], a, AF.Square,
                                                      accum_out=stat[:, base + 1:base + 2]),
                     waits=[(ac, ac.n)] + list(w_extra), inc=ac)
            v1 = P.v(lambda e: e.tensor_scalar(stat[:, base:base + 1], stat[:, base:base + 1],
                                               1.0 / C, None, ALU.mult),
                     waits=[(ve, sv)], inc=ve)
            v2 = P.v(lambda e: e.tensor_scalar(stat[:, base + 1:base + 2], stat[:, base + 1:base + 2],
                                               1.0 / C, LN_EPS, ALU.mult, ALU.add),
                     waits=[(ac, sa), (ve, v1)], inc=ve)
            v3 = P.v(lambda e: e.scalar_tensor_tensor(stat[:, base + 2:base + 3], stat[:, base:base + 1],
                                                      -1.0, stat[:, base:base + 1], ALU.mult, ALU.mult),
                     waits=[(ve, v2)], inc=ve)
            v4 = P.v(lambda e: e.tensor_add(stat[:, base + 1:base + 2], stat[:, base + 1:base + 2],
                                            stat[:, base + 2:base + 3]),
                     waits=[(ve, v3)], inc=ve)
            a1 = P.s(lambda e: e.activation(stat[:, base + 1:base + 2], stat[:, base + 1:base + 2], AF.Ln),
                     waits=[(ve, v4), (ac, ac.n)], inc=ac)
            a2 = P.s(lambda e: e.activation(stat[:, base + 3:base + 4], stat[:, base + 1:base + 2],
                                            AF.Exp, scale=-0.5),
                     waits=[(ac, a1)], inc=ac)
            v5 = P.v(lambda e: e.scalar_tensor_tensor(stat[:, base + 4:base + 5], stat[:, base:base + 1],
                                                      -1.0, stat[:, base + 3:base + 4], ALU.mult, ALU.mult),
                     waits=[(ac, a2), (ve, ve.n)], inc=ve)
            return v5, a2

        tp_state = {}

        def transpose_run(tiles, pst, nslots, evac_fn, extra_first=()):
            st = tp_state.setdefault(id(pst), {})
            n = len(tiles)
            ngroups = (n + 3) // 4
            g0 = st.get("g", 0)
            for gg in range(ngroups):
                g = g0 + gg
                slot = g % nslots
                lo, hi = gg * 4, min(n, gg * 4 + 4)
                w = list(extra_first) if gg == 0 else []
                if slot in st:
                    w.append(st[slot])
                for i in range(lo, hi):
                    P.t(lambda e, i=i, lo=lo, slot=slot: e.matmul(
                        pst[:, slot * 512 + (i - lo) * 128: slot * 512 + (i - lo + 1) * 128],
                        tiles[i], ident[:, :], start=True, stop=True),
                        waits=(w if i == lo else []),
                        inc=(pe if i == hi - 1 else None))
                st[slot] = evac_fn(g, pst[:, slot * 512: slot * 512 + (hi - lo) * 128], [(pe, pe.n)])
            st["g"] = g0 + ngroups
            return pe.n

        if "l0a" in blocks:
            es_a = contextlib.ExitStack()
            with es_a:
                xh = es_a.enter_context(nc.sbuf_tensor("xh", [128, KC * T], BF16))
                bpj = es_a.enter_context(nc.sbuf_tensor("bpj", [128, C], F32))
                P.sy(lambda e: e.dma_start(out=bpj[:, :], in_=bpj_in[:, :]), inc=P.sems["wa"])
                bpj_ld = P.sems["wa"].n
                ktm = es_a.enter_context(nc.sbuf_tensor("ktm", [128, STT * 1024], BF16))
                vtm = es_a.enter_context(nc.sbuf_tensor("vtm", [128, STT * 1024], BF16))
                qf = es_a.enter_context(nc.sbuf_tensor("qf", [128, KC * ST], BF16))
                qt = qf
                af = es_a.enter_context(nc.sbuf_tensor("af", [128, KC * ST], BF16))
                kvq = es_a.enter_context(nc.sbuf_tensor("kvq", [128, 2 * ST], BF16))
                s1t = es_a.enter_context(nc.sbuf_tensor("s1t", [128, KC * 128], BF16))
                dstt = es_a.enter_context(nc.sbuf_tensor("dstt", [128, KC * 4], BF16))
                dtmp = es_a.enter_context(nc.sbuf_tensor("dtmp", [4, 512], F32))
                dnvb = es_a.enter_context(nc.sbuf_tensor("dnvb", [4, KC * ST], BF16))
                ps_mm = es_a.enter_context(nc.psum_tensor("ps_mm", [128, 1024], F32))
                ps_t = es_a.enter_context(nc.psum_tensor("ps_t", [128, 1024], F32))
                ps_s1 = es_a.enter_context(nc.psum_tensor("ps_s1", [128, 1536], F32))
                ps_x = es_a.enter_context(nc.psum_tensor("ps_x", [128, 512], F32))

                def s1slice(c, a, b):
                    base = (c // 3) * 512 + (c % 3) * 160
                    return ps_s1[:, base + a: base + b]

                def xh_ap(kc, t0, w):
                    return xh[:, kc * T + t0: kc * T + t0 + w]

                w0 = P.w[0]
                for j in range(4):
                    P.gp(lambda e, j=j: e.dma_start(
                        out=wq[:, j * 4096:(j + 1) * 4096].rearrange("p (k c) -> p k c", c=512),
                        in_=wqkvT[:, 1024 + 512 * j: 1024 + 512 * (j + 1)].rearrange(
                            "(k p) c -> p k c", p=128)),
                        inc=w0)
                kv_total = w0.n
                kv_slab_ld = [kv_total] * 4
                if "l0f" in blocks:
                    for r in range(FC):
                        for gq in range(4):
                            P.sy(lambda e, r=r, gq=gq: e.dma_start(
                                out=w1s_d.rearrange("(m k) c -> k m c", k=F)[
                                    128 * r:128 * (r + 1), 8 * gq:8 * gq + 8, :],
                                in_=k1_in[128 * r:128 * (r + 1), :].rearrange(
                                    "p (m c) -> p m c", c=128)[:, 8 * gq:8 * gq + 8, :]),
                                inc=sto)
                k1t_total = sto.n
                k0_ld = (P.w[2], P.w[3])
                k0_st = (P.w[6], P.w[7])
                k0_cv = {}
                k0_sts = {}
                k0_i = [0]

                K0SW = min(2048, KC * ST // 2)
                K0N = (F // K0SW) * 8
                def emit_k0_chunk():
                    i = k0_i[0]
                    if i >= K0N or "l0f" not in blocks:
                        return
                    k0_i[0] += 1
                    nsub = F // K0SW
                    r, wdx = i // nsub, i % nsub
                    p = i % 2
                    aap = qf[:, p * K0SW:(p + 1) * K0SW]
                    bap = af[:, p * K0SW:(p + 1) * K0SW]
                    w = [(k0_st[p], k0_sts.get(p, 0))]
                    P.gp(lambda e, r=r, wdx=wdx, aap=aap: e.dma_start(
                        out=aap, in_=k0_in[128 * r:128 * (r + 1), K0SW * wdx:K0SW * (wdx + 1)]),
                        waits=w, inc=k0_ld[p])
                    P.gp(lambda e, r=r, wdx=wdx, bap=bap: e.dma_start(
                        out=bap, in_=k0_in[1024 + 128 * r:1024 + 128 * (r + 1), K0SW * wdx:K0SW * (wdx + 1)]),
                        inc=k0_ld[p])
                    k0_cv[p] = P.v(lambda e, aap=aap, bap=bap: e.tensor_sub(aap, aap, bap),
                                   waits=[(k0_ld[p], k0_ld[p].n), (ve, ve.n)], inc=ve)
                    P.sy(lambda e, r=r, wdx=wdx, aap=aap: e.dma_start(
                        out=w0s[128 * r:128 * (r + 1), K0SW * wdx:K0SW * (wdx + 1)], in_=aap),
                        waits=[(ve, k0_cv[p])], inc=k0_st[p])
                    k0_sts[p] = k0_st[p].n

                P.mark("l0a.A1")
                P.v(None, waits=[(ld, x_ld)])
                P.s(None, waits=[(ld, x_ld)])
                wv = wa = 0
                for tt in range(NT):
                    wv, wa = ln_tile_stats(xr[:, tt * 1024:(tt + 1) * 1024], tt)
                ln_finalize(NT, wv, wa)
                mbv, rca = ve.n, ac.n
                tp_after = {}
                for tt in range(NT):
                    half = tt % 2
                    w = [(ve, mbv), (ac, rca), (ac, ac.n)]
                    if tt >= 2:
                        w.append((pe, tp_after[tt - 2]))
                    aa = P.s(lambda e, tt=tt, half=half: e.activation(
                        nrm[:, half * 1024:(half + 1) * 1024], xr[:, tt * 1024:(tt + 1) * 1024],
                        AF.Identity, scale=stat[:, RC + tt:RC + tt + 1], bias=stat[:, MB + tt:MB + tt + 1]),
                        waits=w, inc=ac)

                    def evac(g, psap, ws, tt=tt):
                        dap = xh[:, :].rearrange("p (c t2) -> p c t2", t2=T)[:, 4 * (g % 2):4 * (g % 2) + 4,
                                                                            tt * 128:(tt + 1) * 128]
                        return (ve, P.v(lambda e, psap=psap, dap=dap: e.tensor_copy(dap, psap),
                                        waits=ws, inc=ve))

                    tp_after[tt] = transpose_run(
                        [nrm[:, half * 1024 + c * 128: half * 1024 + (c + 1) * 128] for c in range(KC)],
                        ps_t, 2, evac, extra_first=[(ac, aa), (cst, consts_cst)])
                xh_ve = ve.n

                P.mark("l0a.A2")
                kv_ev = {}
                kv_tp = {}
                s1_started = set()
                for sidx in range(NS):
                    for hi, (half, slabs) in enumerate((("k", (0, 1)), ("v", (2, 3)))):
                        for j2, j in enumerate(slabs):
                            if T <= 512 and j2 > 0:
                                pass
                            for oc4 in range(4):
                                bank = oc4 % 2
                                for kc in range(KC):
                                    w = []
                                    if kc == 0:
                                        w = [(P.w[0], kv_slab_ld[j]), (ve, xh_ve)]
                                        if bank in kv_ev:
                                            w.append((ac, kv_ev[bank]))
                                    P.t(lambda e, j=j, kc=kc, oc4=oc4, bank=bank, sidx=sidx: e.matmul(
                                        ps_mm[:, bank * 512: bank * 512 + ST],
                                        wq[:, j * 4096 + kc * 512 + oc4 * 128: j * 4096 + kc * 512 + (oc4 + 1) * 128],
                                        xh_ap(kc, sidx * ST, ST),
                                        start=(kc == 0), stop=(kc == KC - 1)),
                                        waits=w, inc=(pe if kc == KC - 1 else None))
                                mmc = pe.n
                                buf = oc4 % 2
                                fn = AF.Relu if half == "k" else AF.Copy
                                w = [(pe, mmc), (ac, ac.n)]
                                if buf in kv_tp:
                                    w.append((pe, kv_tp[buf]))
                                kv_ev[bank] = P.s(lambda e, buf=buf, bank=bank, fn=fn: e.activation(
                                    kvq[:, buf * ST: buf * ST + ST],
                                    ps_mm[:, bank * 512: bank * 512 + ST], fn),
                                    waits=w, inc=ac)
                                dstb = ktm if half == "k" else vtm
                                oco = j2 * 4 + oc4

                                def evac2(g, psap, ws, dstb=dstb, oco=oco):
                                    dap = dstb[:, :].rearrange("p (t2 c) -> p t2 c", c=1024)[
                                        :, :, oco * 128:(oco + 1) * 128]
                                    return (ve, P.v(lambda e, psap=psap, dap=dap: e.tensor_copy(dap, psap),
                                                    waits=ws, inc=ve))

                                kv_tp[buf] = transpose_run(
                                    [kvq[:, buf * ST + t2 * 128: buf * ST + (t2 + 1) * 128]
                                     for t2 in range(STT)],
                                    ps_t, 2, evac2, extra_first=[(ac, kv_ev[bank])])
                    for _ in range(K0N // NS + 1):
                        emit_k0_chunk()
                    ktmv = ve.n
                    last_s = (sidx == NS - 1)
                    for c in range(KC):
                        for t2 in range(STT):
                            st_first = (sidx == 0 and t2 == 0 and c % 3 == 0)
                            fin = last_s and t2 == STT - 1
                            w = [(ve, ktmv), (ve, consts_ve)] if t2 == 0 else []
                            P.t(lambda e, c=c, t2=t2, st_first=st_first, fin=fin: e.matmul(
                                s1slice(c, 0, 128),
                                ktm[:, t2 * 1024 + c * 128: t2 * 1024 + (c + 1) * 128],
                                vtm[:, t2 * 1024 + c * 128: t2 * 1024 + (c + 1) * 128],
                                start=st_first, stop=fin, skip_group_check=True),
                                waits=w)
                            P.t(lambda e, c=c, t2=t2, fin=fin: e.matmul(
                                s1slice(c, 128, 129),
                                ktm[:, t2 * 1024 + c * 128: t2 * 1024 + (c + 1) * 128],
                                onec[:, 0:1],
                                start=False, stop=fin, skip_group_check=True),
                                inc=(pe if (fin and c == KC - 1) else None))
                s1_pe = pe.n

                for c in range(KC):
                    P.v(lambda e, c=c: e.tensor_mul(s1t[:, c * 128:(c + 1) * 128],
                                                    s1slice(c, 0, 128), bdiag[:, :]),
                        waits=[(pe, s1_pe), (ld, init_ld), (ve, ve.n)], inc=ve)
                P.v(lambda e: e.memset(dstt[:, :], 0.0), waits=[(ve, ve.n)], inc=ve)
                mset = ve.n
                for c in range(KC):
                    for g in range(4):
                        P.v(lambda e, c=c, g=g: e.tensor_copy(
                            dstt[32 * g:32 * (g + 1), c * 4 + g: c * 4 + g + 1],
                            s1slice(c, 128, 129)[32 * g:32 * (g + 1), :]),
                            waits=[(ve, mset)], inc=ve)
                dst_ve = ve.n

                P.mark("l0a.A3")
                q_slab_ld = {}
                prj_ld = None
                prev_dv = 0
                prev_proj_free = 0
                prev_qt_all = 0
                for sidx in range(NS):
                    if sidx == 0:
                        w1 = P.w[1]
                        for j in range(2):
                            P.gp(lambda e, j=j: e.dma_start(
                                out=wq[:, j * 4096:(j + 1) * 4096].rearrange("p (k c) -> p k c", c=512),
                                in_=wqkvT[:, 512 * j: 512 * (j + 1)].rearrange("(k p) c -> p k c", p=128)),
                                waits=([(pe, s1_pe)] if j == 0 else []), inc=w1)
                        P.gp(lambda e: e.dma_start(
                            out=wq[:, 8192:16384].rearrange("p (k c) -> p k c", c=1024),
                            in_=wprojT[:, :].rearrange("(k p) c -> p k c", p=128)), inc=w1)
                        q_slab_ld[0] = q_slab_ld[1] = prj_ld = w1.n
                    q_ev = {}
                    for j in range(2):
                        for oc4 in range(4):
                            bank = oc4 % 2
                            cchunk = j * 4 + oc4
                            for kc in range(KC):
                                w = []
                                if kc == 0:
                                    w = [(P.w[1], q_slab_ld[j]), (ve, xh_ve), (ve, prev_proj_free),
                                         (ve, prev_qt_all)]
                                    if bank in q_ev:
                                        w.append((ac, q_ev[bank]))
                                    elif bank in kv_ev:
                                        w.append((ac, kv_ev[bank]))
                                P.t(lambda e, j=j, kc=kc, oc4=oc4, bank=bank, sidx=sidx: e.matmul(
                                    ps_mm[:, bank * 512: bank * 512 + ST],
                                    wq[:, j * 4096 + kc * 512 + oc4 * 128: j * 4096 + kc * 512 + (oc4 + 1) * 128],
                                    xh_ap(kc, sidx * ST, ST),
                                    start=(kc == 0), stop=(kc == KC - 1)),
                                    waits=w, inc=(pe if kc == KC - 1 else None))
                            mmc = pe.n
                            w = [(pe, mmc), (ac, ac.n),
                                 (k0_st[0], k0_sts.get(0, 0)), (k0_st[1], k0_sts.get(1, 0))]
                            q_ev[bank] = P.s(lambda e, cchunk=cchunk, bank=bank: e.activation(
                                qf[:, cchunk * ST: cchunk * ST + ST],
                                ps_mm[:, bank * 512: bank * 512 + ST], AF.Relu),
                                waits=w, inc=ac)
                    qf_ac = ac.n
                    den_ev = {}
                    for c in range(KC):
                        bp = (c % 2) * 32
                        w = [(ve, ve.n if c == 0 else 0)]
                        if c == 0:
                            w = [(ve, dst_ve), (ac, qf_ac), (ve, prev_dv)]
                        elif (c % 2) in den_ev:
                            w = [(ve, den_ev[c % 2])]
                        P.t(lambda e, c=c, bp=bp: e.matmul(ps_x[bp:bp + 4, 0:ST],
                                                           dstt[:, c * 4:(c + 1) * 4],
                                                           qf[:, c * ST: c * ST + ST],
                                                           start=True, stop=True),
                            waits=w, inc=pe)
                        dmp = pe.n
                        e1 = P.v(lambda e, c=c, bp=bp: e.tensor_scalar(
                            dtmp[0:4, 0:ST], ps_x[bp:bp + 4, 0:ST],
                            L0A_EPS, None, ALU.add),
                            waits=[(pe, dmp), (ve, ve.n)], inc=ve)
                        e2 = P.v(lambda e: e.reciprocal(dtmp[0:4, 0:ST], dtmp[0:4, 0:ST]),
                                 waits=[(ve, e1)], inc=ve)
                        den_ev[c % 2] = P.v(lambda e, c=c: e.tensor_copy(
                            dnvb[0:4, c * ST:(c + 1) * ST], dtmp[0:4, 0:ST]),
                            waits=[(ve, e2)], inc=ve)
                    prev_dv = dv2 = ve.n
                    qt_done = []
                    bc_ev = {}
                    for c in range(KC):
                        bank = c % 2
                        w = [(ve, dv2)]
                        if bank in bc_ev:
                            w.append((ve, bc_ev[bank]))
                        P.t(lambda e, c=c, bank=bank: e.matmul(
                            ps_t[:, bank * 512: bank * 512 + ST], sel4[0:4, :],
                            dnvb[0:4, c * ST:(c + 1) * ST], start=True, stop=True),
                            waits=w, inc=pe)
                        bcp = pe.n
                        bc_ev[bank] = P.v(lambda e, c=c, bank=bank: e.tensor_mul(
                            qt[:, c * ST: c * ST + ST], qf[:, c * ST: c * ST + ST],
                            ps_t[:, bank * 512: bank * 512 + ST]),
                            waits=[(pe, bcp), (ac, qf_ac), (ve, ve.n)], inc=ve)
                        qt_done.append(bc_ev[bank])
                    qt_all = ve.n
                    prev_qt_all = qt_all
                    af_ev = {}
                    for c in range(KC):
                        bank = c % 2
                        w = [(ve, qt_all)]
                        if bank in af_ev:
                            w.append((ac, af_ev[bank]))
                        P.t(lambda e, c=c, bank=bank: e.matmul(
                            ps_mm[:, bank * 512: bank * 512 + ST], s1t[:, c * 128:(c + 1) * 128],
                            qt[:, c * ST: c * ST + ST], start=True, stop=True),
                            waits=w, inc=pe)
                        nmp = pe.n
                        af_ev[bank] = P.s(lambda e, c=c, bank=bank: e.activation(
                            af[:, c * ST: c * ST + ST], ps_mm[:, bank * 512: bank * 512 + ST], AF.Copy),
                            waits=[(pe, nmp), (ac, ac.n),
                                   (k0_st[0], k0_sts.get(0, 0)), (k0_st[1], k0_sts.get(1, 0))], inc=ac)
                    af_all = ac.n
                    for t2 in range(STT):
                        for kc in range(KC):
                            for n2 in range(2):
                                w = []
                                if kc == 0 and n2 == 0:
                                    w = [(ac, af_all), (P.w[1], prj_ld), (ve, prev_proj_free)]
                                P.t(lambda e, kc=kc, n2=n2, t2=t2: e.matmul(
                                    ps_mm[:, n2 * 512:(n2 + 1) * 512],
                                    af[:, kc * ST + t2 * 128: kc * ST + (t2 + 1) * 128],
                                    wq[:, 8192 + kc * 1024 + n2 * 512: 8192 + kc * 1024 + (n2 + 1) * 512],
                                    start=(kc == 0), stop=(kc == KC - 1)),
                                    waits=w, inc=(pe if (kc == KC - 1 and n2 == 1) else None))
                        prp = pe.n
                        gt = sidx * STT + t2
                        ya = P.v(lambda e: e.tensor_add(ysc[:, :], ps_mm[:, :], bpj[:, :]),
                                 waits=[(pe, prp), (P.sems["wa"], bpj_ld), (ve, ve.n)], inc=ve)
                        prev_proj_free = ya
                        vr, ar = ln_small(ysc[:, :], SMX, [(ve, ya)])
                        v6 = P.v(lambda e, gt=gt: e.tensor_scalar(
                            xr[:, gt * 1024:(gt + 1) * 1024], xr[:, gt * 1024:(gt + 1) * 1024],
                            stat[:, SMX + 4:SMX + 5], None, ALU.add),
                            waits=[(ve, vr)], inc=ve)
                        P.v(lambda e, gt=gt: e.scalar_tensor_tensor(
                            xr[:, gt * 1024:(gt + 1) * 1024], ysc[:, :], stat[:, SMX + 3:SMX + 4],
                            xr[:, gt * 1024:(gt + 1) * 1024], ALU.mult, ALU.add),
                            waits=[(ve, v6)], inc=ve)
                P.barrier()


        if "l0f" in blocks:
            P.mark("l0f")
            es_f = contextlib.ExitStack()
            with es_f:
                xhs = es_f.enter_context(nc.sbuf_tensor("xhs", [128, 2 * KC * ST], BF16))
                Ab = es_f.enter_context(nc.sbuf_tensor("Ab", [128, FC * ST], BF16))
                Bb = es_f.enter_context(nc.sbuf_tensor("Bb", [128, FC * ST], BF16))
                wfc = es_f.enter_context(nc.sbuf_tensor("wfc", [128, 2 * 1024], BF16))
                rshl = es_f.enter_context(nc.sbuf_tensor("rshl", [32, 2 * 512], BF16))
                ps_t2 = es_f.enter_context(nc.psum_tensor("ps_t2", [128, 1024], F32))
                ps_m = es_f.enter_context(nc.psum_tensor("ps_m", [128, 2048], F32))
                ps_o = es_f.enter_context(nc.psum_tensor("ps_o", [128, 1024], F32))
                wl1 = (P.w[6], P.w[7])
                wl2 = (P.w[4], P.w[5])
                wl3 = (P.w[0], P.w[1])
                mm1_free = {}
                mm2_free = {}
                mm3_free = {}
                mm1_ld = {}
                mm2_ld = {}
                mm3_ld = {}

                def load_mm1_slab(gi, j):
                    par = gi % 2
                    w = [(pe, mm1_free.get(par, 0))]
                    if gi < 2:
                        w += [(k0_st[0], k0_sts.get(0, 0)), (k0_st[1], k0_sts.get(1, 0))]
                    for kc in range(KC):
                        P.sy(lambda e, j=j, kc=kc, par=par: e.dma_start(
                            out=wq[:, par * 4096 + kc * 512: par * 4096 + (kc + 1) * 512],
                            in_=w0s[128 * kc:128 * (kc + 1), 512 * j:512 * (j + 1)]),
                            waits=(w if kc == 0 else []), inc=wl1[par])
                    mm1_ld[gi] = (wl1[par], wl1[par].n)
                    return par

                def load_mm2_slab(gi, mc):
                    par = gi % 2
                    w = [(pe, mm2_free.get(par, 0))]
                    if gi < 2:
                        w += [(sto, k1t_total)]
                    P.gp(lambda e, mc=mc, par=par: e.dma_start(
                        out=wq[:, 8192 + par * 4096: 8192 + (par + 1) * 4096].rearrange(
                            "p (k c) -> p k c", c=128),
                        in_=w1s_d[mc * F: (mc + 1) * F, :].rearrange("(k p) c -> p k c", p=128)),
                        waits=w, inc=wl2[par])
                    mm2_ld[gi] = (wl2[par], wl2[par].n)
                    return par

                def load_mm3_slab(gi, kc2):
                    par = gi % 2
                    w = [(pe, mm3_free.get(par, 0))]
                    P.gp(lambda e, kc2=kc2, par=par: e.dma_start(
                        out=wfc[:, par * 1024:(par + 1) * 1024],
                        in_=k2_in[128 * kc2:128 * (kc2 + 1), :]),
                        waits=w, inc=wl3[par])
                    mm3_ld[gi] = (wl3[par], wl3[par].n)
                    return par

                g1 = g2 = g3 = 0
                prev_eva = 0

                def l0f_ln(sidx):
                    par = sidx % 2
                    wv = wa = 0
                    for t2 in range(STT):
                        gt = sidx * STT + t2
                        wv, wa = ln_tile_stats(xr[:, gt * 1024:(gt + 1) * 1024], t2)
                    ln_finalize(STT, wv, wa)
                    mbv, rca = ve.n, ac.n
                    tpa = {}
                    for t2 in range(STT):
                        half = t2 % 2
                        gt = sidx * STT + t2
                        w = [(ve, mbv), (ac, rca), (ac, ac.n)]
                        if t2 >= 2:
                            w.append((pe, tpa[t2 - 2]))
                        aa = P.s(lambda e, gt=gt, half=half, t2=t2: e.activation(
                            nrm[:, half * 1024:(half + 1) * 1024], xr[:, gt * 1024:(gt + 1) * 1024],
                            AF.Identity, scale=stat[:, RC + t2:RC + t2 + 1],
                            bias=stat[:, MB + t2:MB + t2 + 1]),
                            waits=w, inc=ac)

                        def evacx(g, psap, ws, t2=t2, par=par):
                            dap = xhs[:, par * KC * ST:(par + 1) * KC * ST].rearrange(
                                "p (c t3) -> p c t3", t3=ST)[
                                :, 4 * (g % 2):4 * (g % 2) + 4, t2 * 128:(t2 + 1) * 128]
                            return (ve, P.v(lambda e, psap=psap, dap=dap: e.tensor_copy(dap, psap),
                                            waits=ws, inc=ve))

                        tpa[t2] = transpose_run(
                            [nrm[:, half * 1024 + c * 128: half * 1024 + (c + 1) * 128] for c in range(KC)],
                            ps_t2, 2, evacx, extra_first=[(ac, aa)])
                    return ve.n

                p2cv = []
                if "l1a" in blocks:
                    for wnm in ("q", "k", "v", "p"):
                        for srcw, dstw in ((kTm[wnm], kTb[wnm]), (vVm[wnm], vVb[wnm])):
                            for r in range(KC):
                                p2cv.append(lambda e, r=r, srcw=srcw, dstw=dstw: e.dma_start(
                                    out=dstw[128 * r:128 * (r + 1), :],
                                    in_=srcw[128 * r:128 * (r + 1), :]))
                p2cv_i = [0]

                def drain_p2cv(k):
                    while k > 0 and p2cv_i[0] < len(p2cv):
                        P.gp(p2cv[p2cv_i[0]], inc=P.w[2 + p2cv_i[0] % 2])
                        p2cv_i[0] += 1
                        k -= 1

                xhs_ve_n = {0: l0f_ln(0)}
                prev_rc_ve = 0
                prev_mm2_ac = 0
                prev_mm3_ac = 0
                for sidx in range(NS):
                    par0 = sidx % 2
                    xhs_ve = xhs_ve_n[sidx]

                    def xs_ap(kc, par0=par0):
                        return xhs[:, par0 * KC * ST + kc * ST: par0 * KC * ST + (kc + 1) * ST]
                    ev1 = {}
                    for j in range(8):
                        if sidx > 0:
                            drain_p2cv(3)
                        par = load_mm1_slab(g1, j)
                        for m4 in range(4):
                            mc = j * 4 + m4
                            bank = mc % 4
                            for kc in range(KC):
                                w = []
                                if kc == 0:
                                    w = [mm1_ld[g1], (ve, xhs_ve), (ve, prev_rc_ve),
                                         (ac, prev_mm2_ac), (ac, prev_mm3_ac)]
                                    if bank in ev1:
                                        w.append((ac, ev1[bank]))
                                P.t(lambda e, par=par, kc=kc, m4=m4, bank=bank, xs_ap=xs_ap: e.matmul(
                                    ps_m[:, bank * 512: bank * 512 + ST],
                                    wq[:, par * 4096 + kc * 512 + m4 * 128: par * 4096 + kc * 512 + (m4 + 1) * 128],
                                    xs_ap(kc),
                                    start=(kc == 0), stop=(kc == KC - 1)),
                                    waits=w, inc=(pe if kc == KC - 1 else None))
                            if m4 == 3:
                                mm1_free[par] = pe.n
                            ev1[bank] = P.s(lambda e, mc=mc, bank=bank: e.activation(
                                Ab[:, mc * ST:(mc + 1) * ST], ps_m[:, bank * 512: bank * 512 + ST],
                                AF.Relu, scale=smv[:, 4:5]),
                                waits=[(pe, pe.n), (ac, ac.n)], inc=ac)
                        g1 += 1
                    mm1_ac = ac.n
                    if sidx + 1 < NS:
                        xhs_ve_n[sidx + 1] = l0f_ln(sidx + 1)
                    for kc2 in range(FC):
                        P.t(lambda e, kc2=kc2: e.matmul(
                            ps_o[0:32, 0:ST], onec[:, 0:32], Ab[:, kc2 * ST:(kc2 + 1) * ST],
                            start=(kc2 == 0), stop=(kc2 == FC - 1)),
                            waits=([(ac, mm1_ac)] if kc2 == 0 else []),
                            inc=(pe if kc2 == FC - 1 else None))
                    rs_pe = pe.n
                    rv1 = P.v(lambda e: e.tensor_copy(rshl[0:32, 0:ST], ps_o[0:32, 0:ST]),
                              waits=[(pe, rs_pe), (ve, ve.n)], inc=ve)
                    rv2 = P.v(lambda e: e.tensor_sub(rshl[0:32, 512:512 + ST], ps_o[0:32, 0:ST],
                                                     rshl[0:32, 0:ST]),
                              waits=[(ve, rv1)], inc=ve)
                    ev2 = {}
                    for mc in range(FC):
                        par = load_mm2_slab(g2, mc)
                        bank = mc % 4
                        for kc2 in range(FC):
                            w = []
                            if kc2 == 0:
                                w = [mm2_ld[g2], (ac, mm1_ac)]
                                if bank in ev2:
                                    w.append((ac, ev2[bank]))
                            P.t(lambda e, par=par, kc2=kc2, bank=bank: e.matmul(
                                ps_m[:, bank * 512: bank * 512 + ST],
                                wq[:, 8192 + par * 4096 + kc2 * 128: 8192 + par * 4096 + (kc2 + 1) * 128],
                                Ab[:, kc2 * ST: (kc2 + 1) * ST],
                                start=(kc2 == 0), stop=False),
                                waits=w)
                        for hl in range(2):
                            P.t(lambda e, hl=hl, bank=bank: e.matmul(
                                ps_m[:, bank * 512: bank * 512 + ST],
                                nhalf[0:32, :], rshl[0:32, hl * 512: hl * 512 + ST],
                                start=False, stop=(hl == 1)),
                                waits=([(ve, rv2)] if hl == 0 else []),
                                inc=(pe if hl == 1 else None))
                        mm2_free[par] = pe.n
                        ev2[bank] = P.s(lambda e, mc=mc, bank=bank: e.activation(
                            Bb[:, mc * ST:(mc + 1) * ST], ps_m[:, bank * 512: bank * 512 + ST],
                            AF.Relu, scale=smv[:, 7:8]),
                            waits=[(pe, pe.n), (ac, ac.n)], inc=ac)
                        g2 += 1
                    mm2_ac = ac.n
                    prev_mm2_ac = mm2_ac
                    for hf in range(STT // 2):
                        for kc2 in range(FC):
                            par = load_mm3_slab(g3, kc2)
                            for ti in range(2):
                                t2 = hf * 2 + ti
                                for n2 in range(2):
                                    w = []
                                    if kc2 == 0 and ti == 0 and n2 == 0:
                                        w = [(ac, mm2_ac), (ve, prev_eva), (ac, prev_mm3_ac)]
                                    if ti == 0 and n2 == 0:
                                        w = w + [mm3_ld[g3]]
                                    P.t(lambda e, par=par, kc2=kc2, t2=t2, ti=ti, n2=n2: e.matmul(
                                        ps_m[:, ti * 1024 + n2 * 512: ti * 1024 + (n2 + 1) * 512],
                                        Bb[:, kc2 * ST + t2 * 128: kc2 * ST + (t2 + 1) * 128],
                                        wfc[:, par * 1024 + n2 * 512: par * 1024 + (n2 + 1) * 512],
                                        start=(kc2 == 0), stop=(kc2 == FC - 1)),
                                        waits=w)
                                P.t(lambda e, kc2=kc2, t2=t2, ti=ti: e.matmul(
                                    ps_o[:, ti * 512: ti * 512 + 1],
                                    Bb[:, kc2 * ST + t2 * 128: kc2 * ST + (t2 + 1) * 128],
                                    onec[:, 0:1],
                                    start=(kc2 == 0), stop=(kc2 == FC - 1)),
                                    inc=(pe if ti == 1 else None))
                            mm3_free[par] = pe.n
                            g3 += 1
                        mmp = pe.n
                        for ti in range(2):
                            t2 = hf * 2 + ti
                            gt = sidx * STT + t2
                            rc1 = P.v(lambda e, ti=ti: e.tensor_scalar(
                                stat[:, SMX + 6:SMX + 7], ps_o[:, ti * 512: ti * 512 + 1],
                                smv[:, 2:3], -1.0, ALU.mult, ALU.mult),
                                waits=[(pe, mmp), (ve, ve.n)], inc=ve)
                            prev_rc_ve = rc1
                            ya = P.s(lambda e, ti=ti: e.activation(
                                ysc[:, :], ps_m[:, ti * 1024:(ti + 1) * 1024],
                                AF.Identity, scale=smv[:, 8:9], bias=stat[:, SMX + 6:SMX + 7]),
                                waits=[(pe, mmp), (ac, ac.n), (ve, prev_eva), (ve, rc1)], inc=ac)
                            prev_mm3_ac = ya
                            vr, ar = ln_small(ysc[:, :], SMX, [(ac, ya)])
                            v6 = P.v(lambda e, gt=gt: e.tensor_scalar(
                                xr[:, gt * 1024:(gt + 1) * 1024], xr[:, gt * 1024:(gt + 1) * 1024],
                                stat[:, SMX + 4:SMX + 5], None, ALU.add),
                                waits=[(ve, vr)], inc=ve)
                            prev_eva = P.v(lambda e, gt=gt: e.scalar_tensor_tensor(
                                xr[:, gt * 1024:(gt + 1) * 1024], ysc[:, :], stat[:, SMX + 3:SMX + 4],
                                xr[:, gt * 1024:(gt + 1) * 1024], ALU.mult, ALU.add),
                                waits=[(ve, v6)], inc=ve)
                drain_p2cv(len(p2cv))
                es.p2cv_done = (P.w[2].n, P.w[3].n)
            P.barrier()


        if "l1a" in blocks:
            P.mark("l1a.N1")
            H1, D1 = 16, 64
            es_h = contextlib.ExitStack()
            hfm = es_h.enter_context(nc.sbuf_tensor("hfm", [128, KC * T], BF16))
            es_n1 = contextlib.ExitStack()
            with es_n1:
                ps_n1 = es_n1.enter_context(nc.psum_tensor("ps_n1", [128, 1024], F32))
                wv = wa = 0
                for tt in range(NT):
                    wv, wa = ln_tile_stats(xr[:, tt * 1024:(tt + 1) * 1024], tt)
                ln_finalize(NT, wv, wa, mu_c=MU1, r_c=R1, mb_c=MB1)
                mbv, rca = ve.n, ac.n
                tpa = {}
                prev_ai = 0
                for tt in range(NT):
                    half = tt % 2
                    w = [(ve, mbv), (ac, rca), (ac, ac.n), (ve, prev_ai)]
                    ai = P.s(lambda e, tt=tt: e.activation(
                        ysc[:, :], xr[:, tt * 1024:(tt + 1) * 1024], AF.Identity,
                        scale=stat[:, R1 + tt:R1 + tt + 1], bias=stat[:, MB1 + tt:MB1 + tt + 1]),
                        waits=w, inc=ac)
                    vr, ar = ln_small(ysc[:, :], SMX, [(ac, ai)])
                    w2_ = [(ve, vr), (ac, ar), (ac, ac.n)]
                    if tt >= 2:
                        w2_.append((pe, tpa[tt - 2]))
                    aa = P.s(lambda e, half=half: e.activation(
                        nrm[:, half * 1024:(half + 1) * 1024], ysc[:, :],
                        AF.Identity, scale=stat[:, SMX + 3:SMX + 4], bias=stat[:, SMX + 4:SMX + 5]),
                        waits=w2_, inc=ac)

                    def evach(g, psap, ws, tt=tt):
                        dap = hfm[:, :].rearrange("p (c t2) -> p c t2", t2=T)[
                            :, 4 * (g % 2):4 * (g % 2) + 4, tt * 128:(tt + 1) * 128]
                        return (ve, P.v(lambda e, psap=psap, dap=dap: e.tensor_copy(dap, psap),
                                        waits=ws, inc=ve))

                    tpa[tt] = transpose_run(
                        [nrm[:, half * 1024 + c * 128: half * 1024 + (c + 1) * 128] for c in range(KC)],
                        ps_n1, 2, evach, extra_first=[(ac, aa)])
                    prev_ai = ve.n
                    P.sy(lambda e, tt=tt: e.dma_start(out=xstash[128 * tt:128 * (tt + 1), :],
                                                      in_=xr[:, tt * 1024:(tt + 1) * 1024]),
                         inc=sto)
                hfm_ve = ve.n
                stash_total = sto.n
            es1.close()
            P.barrier()
            P.s(None, waits=[(sto, stash_total)])
            P.v(None, waits=[(sto, stash_total)])
            P.t(None, waits=[(sto, stash_total)])

            P.mark("l1a.P2")
            es2 = contextlib.ExitStack()
            qfl = es2.enter_context(nc.sbuf_tensor("qfl", [128, KC * T], BF16, side="right"))
            kfl = es2.enter_context(nc.sbuf_tensor("kfl", [128, KC * T], BF16, side="right"))
            vtl = es2.enter_context(nc.sbuf_tensor("vtl", [128, NT * 1024], BF16, side="right"))
            es_p2 = contextlib.ExitStack()
            with es_p2:
                ptile = es_p2.enter_context(nc.sbuf_tensor("ptile", [128, STT * 1024], BF16))
                pf = es_p2.enter_context(nc.sbuf_tensor("pf", [128, KC * ST], BF16))
                ps2 = es_p2.enter_context(nc.psum_tensor("ps2", [128, 4096], F32))
                L2 = SMX + 16
                wlg = (P.sems["wa"], P.sems["wb"])
                wlg2 = (P.sems["wc"], P.sems["wd"])
                kt_ld = {}
                v_ld = {}
                exp_done = {}
                reg_free = {}
                prev_scores_pe = 0
                prev_av_pe = 0
                for widx, wnm in enumerate(("q", "k", "v")):
                    par = widx % 2
                    w = [(pe, prev_scores_pe)]
                    if widx < 2:
                        w += [(P.w[2], es.p2cv_done[0]), (P.w[3], es.p2cv_done[1])]
                    for kc in range(KC):
                        P.sy(lambda e, wnm=wnm, kc=kc: e.dma_start(
                            out=wq[:, kc * 1024:(kc + 1) * 1024],
                            in_=kTb[wnm][128 * kc:128 * (kc + 1), :]),
                            waits=(w if kc == 0 else []), inc=wlg[par])
                    kt_ld[widx] = (wlg[par], wlg[par].n)
                    w = [(pe, prev_av_pe)]
                    if widx < 2:
                        w += [(P.w[2], es.p2cv_done[0]), (P.w[3], es.p2cv_done[1])]
                    for nc_ in range(KC):
                        P.sy(lambda e, wnm=wnm, nc_=nc_: e.dma_start(
                            out=wq[:, 8192 + nc_ * 1024: 8192 + (nc_ + 1) * 1024],
                            in_=vVb[wnm][128 * nc_:128 * (nc_ + 1), :]),
                            waits=(w if nc_ == 0 else []), inc=wlg2[par])
                    v_ld[widx] = (wlg2[par], wlg2[par].n)
                    if widx == 2:
                        cp = es.conv_pool = []
                        for r in range(KC):
                            cp.append(lambda e, r=r: e.dma_start(
                                out=fKb[128 * r:128 * (r + 1), :],
                                in_=fKT[128 * r:128 * (r + 1), :]))
                        for r in range(FC):
                            cp.append(lambda e, r=r: e.dma_start(
                                out=fVb[128 * r:128 * (r + 1), :],
                                in_=fV[128 * r:128 * (r + 1), :]))
                        if "l1f" in blocks:
                            for srcw, dstw in ((w1T, w1b), (w2T, w2b)):
                                for r in range(KC):
                                    cp.append(lambda e, r=r, srcw=srcw, dstw=dstw: e.dma_start(
                                        out=dstw[128 * r:128 * (r + 1), :],
                                        in_=srcw[128 * r:128 * (r + 1), :]))
                            for r in range(FC):
                                cp.append(lambda e, r=r: e.dma_start(
                                    out=w3b[128 * r:128 * (r + 1), :],
                                    in_=w3T[128 * r:128 * (r + 1), :]))
                            for srcw, dstw in ((h1T, h1s_t), (h2T, h2s_t)):
                                for r in range(FC):
                                    for gq in range(4):
                                        cp.append(lambda e, r=r, gq=gq, srcw=srcw, dstw=dstw: e.dma_start(
                                            out=dstw.rearrange("(m k) c -> k m c", k=F)[
                                                128 * r:128 * (r + 1), 8 * gq:8 * gq + 8, :],
                                            in_=srcw[128 * r:128 * (r + 1), :].rearrange(
                                                "p (m c) -> p m c", c=128)[:, 8 * gq:8 * gq + 8, :]))
                    for sidx in range(NS):
                        sc_done = {}
                        for t2 in range(STT):
                            for nh in range(2):
                                for kc in range(KC):
                                    w = []
                                    if nh == 0 and kc == 0:
                                        w = [kt_ld[widx], (ve, hfm_ve),
                                             exp_done.get(t2, (ac, 0)),
                                             reg_free.get(t2, (ac, 0))]
                                    P.t(lambda e, kc=kc, nh=nh, sidx=sidx, t2=t2: e.matmul(
                                        ps2[:, t2 * 1024 + nh * 512: t2 * 1024 + (nh + 1) * 512],
                                        hfm[:, kc * T + sidx * ST + t2 * 128: kc * T + sidx * ST + (t2 + 1) * 128],
                                        wq[:, kc * 1024 + nh * 512: kc * 1024 + (nh + 1) * 512],
                                        start=(kc == 0), stop=(kc == KC - 1)),
                                        waits=w, inc=(pe if (nh == 1 and kc == KC - 1) else None))
                            sc_done[t2] = pe.n
                        for t2 in range(STT):
                            rm = P.v(lambda e, t2=t2: e.reduce_max(
                                stat[:, SMX:SMX + 1], ps2[:, t2 * 1024:(t2 + 1) * 1024], axis=AX.X),
                                waits=[(pe, sc_done[t2]), (ve, ve.n)], inc=ve)
                            rb = P.v(lambda e: e.tensor_scalar(
                                stat[:, SMX + 1:SMX + 2], stat[:, SMX:SMX + 1],
                                -1.0 / 32.0, None, ALU.mult),
                                waits=[(ve, rm)], inc=ve)
                            exq = P.s(lambda e, t2=t2: e.activation(
                                ptile[:, t2 * 1024:(t2 + 1) * 1024], ps2[:, t2 * 1024:(t2 + 1) * 1024],
                                AF.Exp, scale=1.0 / 32.0, bias=stat[:, SMX + 1:SMX + 2],
                                accum_out=stat[:, SMX + 2:SMX + 3]),
                                waits=[(ve, rb), (ac, ac.n)], inc=ac)
                            exp_done[t2] = (ac, exq)
                            rc = P.v(lambda e, t2=t2: e.reciprocal(stat[:, L2 + t2:L2 + t2 + 1],
                                                                   stat[:, SMX + 2:SMX + 3]),
                                     waits=[(ac, exq), (ve, ve.n)], inc=ve)
                            nmv = rc
                            if wnm in ("q", "k"):
                                nmv = P.v(lambda e, t2=t2: e.tensor_scalar_mul(
                                    ptile[:, t2 * 1024:(t2 + 1) * 1024],
                                    ptile[:, t2 * 1024:(t2 + 1) * 1024],
                                    stat[:, L2 + t2:L2 + t2 + 1]),
                                    waits=[(ve, rc), (ac, exq)], inc=ve)
                            for g2_ in range(2):
                                w = [(ve, nmv), (ac, exq)]
                                for jj in range(4):
                                    P.t(lambda e, t2=t2, g2_=g2_, jj=jj: e.matmul(
                                        ps2[:, t2 * 1024 + g2_ * 512 + jj * 128:
                                            t2 * 1024 + g2_ * 512 + (jj + 1) * 128],
                                        ptile[:, t2 * 1024 + (g2_ * 4 + jj) * 128:
                                              t2 * 1024 + (g2_ * 4 + jj + 1) * 128],
                                        ident[:, :], start=True, stop=True),
                                        waits=(w if jj == 0 else []),
                                        inc=(pe if jj == 3 else None))
                                if g2_ == 0:
                                    P.v(lambda e, t2=t2: e.tensor_copy(
                                        pf[:, :].rearrange("p (c t3) -> p c t3", t3=ST)[
                                            :, 0:4, t2 * 128:(t2 + 1) * 128],
                                        ps2[:, t2 * 1024: t2 * 1024 + 512]),
                                        waits=[(pe, pe.n), (ve, ve.n)], inc=ve)
                                else:
                                    P.s(lambda e, t2=t2: e.copy(
                                        pf[:, :].rearrange("p (c t3) -> p c t3", t3=ST)[
                                            :, 4:8, t2 * 128:(t2 + 1) * 128],
                                        ps2[:, t2 * 1024 + 512: t2 * 1024 + 1024]),
                                        waits=[(pe, pe.n), (ac, ac.n)], inc=ac)
                        pf_ve, pf_ac = ve.n, ac.n
                        if sidx == NS - 1:
                            prev_scores_pe = pe.n
                        if wnm in ("q", "k"):
                            dstb = qfl if wnm == "q" else kfl
                            ev = {}
                            for cc in range(KC):
                                reg = (cc % 4) * 512
                                for nc_ in range(KC):
                                    w = []
                                    if nc_ == 0:
                                        w = [(ve, pf_ve), (ac, pf_ac), v_ld[widx]]
                                        if (cc % 4) in ev:
                                            w.append((ac, ev[cc % 4]))
                                    P.t(lambda e, cc=cc, nc_=nc_, reg=reg: e.matmul(
                                        ps2[:, reg: reg + ST],
                                        wq[:, 8192 + nc_ * 1024 + cc * 128: 8192 + nc_ * 1024 + (cc + 1) * 128],
                                        pf[:, nc_ * ST:(nc_ + 1) * ST],
                                        start=(nc_ == 0), stop=(nc_ == KC - 1)),
                                        waits=w, inc=(pe if nc_ == KC - 1 else None))
                                avp2 = pe.n
                                ev[cc % 4] = P.s(lambda e, cc=cc, reg=reg, dstb=dstb, sidx=sidx: e.activation(
                                    dstb[:, cc * T + sidx * ST: cc * T + sidx * ST + ST],
                                    ps2[:, reg: reg + ST], AF.Copy),
                                    waits=[(pe, avp2), (ac, ac.n)], inc=ac)
                            for t2 in range(STT):
                                reg_free[t2] = (ac, ac.n)
                            if sidx == NS - 1:
                                prev_av_pe = pe.n
                        else:
                            ev = {}
                            for t2 in range(STT):
                                reg = t2 * 1024
                                for nc_ in range(KC):
                                    for nh in range(2):
                                        w = []
                                        if nc_ == 0 and nh == 0:
                                            w = [(ve, pf_ve), (ac, pf_ac), v_ld[widx]]
                                        P.t(lambda e, t2=t2, nc_=nc_, nh=nh, reg=reg: e.matmul(
                                            ps2[:, reg + nh * 512: reg + (nh + 1) * 512],
                                            pf[:, nc_ * ST + t2 * 128: nc_ * ST + (t2 + 1) * 128],
                                            wq[:, 8192 + nc_ * 1024 + nh * 512: 8192 + nc_ * 1024 + (nh + 1) * 512],
                                            start=(nc_ == 0), stop=(nc_ == KC - 1)),
                                            waits=w, inc=(pe if (nc_ == KC - 1 and nh == 1) else None))
                                avp2 = pe.n
                                gt = sidx * STT + t2
                                ev[t2] = P.s(lambda e, t2=t2, reg=reg, gt=gt: e.activation(
                                    vtl[:, gt * 1024:(gt + 1) * 1024], ps2[:, reg:reg + 1024],
                                    AF.Identity, scale=stat[:, L2 + t2:L2 + t2 + 1]),
                                    waits=[(pe, avp2), (ac, ac.n)], inc=ac)
                                reg_free[t2] = (ac, ac.n)
                            if sidx == NS - 1:
                                prev_av_pe = pe.n
                qkv_ac = ac.n
            P.barrier()

            P.mark("l1a.P3")
            es_h.close()
            es3 = contextlib.ExitStack()
            ofm = es3.enter_context(nc.sbuf_tensor("ofm", [128, KC * T], BF16))
            es_p3 = contextlib.ExitStack()
            with es_p3:
                sraw = es_p3.enter_context(nc.sbuf_tensor("sraw", [128, 3968], F32))
                prow = es_p3.enter_context(nc.sbuf_tensor("prow", [128, 3968], BF16))
                ptr = es_p3.enter_context(nc.sbuf_tensor("ptr", [128, 2048], BF16))
                ofin = es_p3.enter_context(nc.sbuf_tensor("ofin", [128, 2 * 64], BF16))
                ps3 = es_p3.enter_context(nc.psum_tensor("ps3", [128, 4096], F32))
                SROFF = (0, 1920)
                po = P.po
                CM0, CB0, CL0, CLI0, CW0 = SMX, SMX + 2, SMX + 4, SMX + 6, SMX + 8
                CS0 = SMX + 16
                rows = [(hh, qi) for hh in range(H1) for qi in range(NT)]
                nrows = len(rows)
                slot_free = {}
                sraw_free = {}
                prow_free = {}
                tp_free = {}
                pv_free = {}
                ptr_busy = 0
                ofin_busy = {}
                wmax_done = {}
                gwa = 0

                def passA(r):
                    nonlocal gwa
                    hh, qi = rows[r]
                    hp = (hh % 2) * 64
                    qc = hh // 2
                    ncols = (qi + 1) * 128
                    rb = r % 2
                    nw = (ncols + 511) // 512
                    for wd in range(nw):
                        c0 = wd * 512
                        wcols = min(512, ncols - c0)
                        s4 = gwa % 4
                        w = list(slot_free.get(s4, [(pe, 0)]))
                        if r == 0 and wd == 0:
                            w.append((ac, qkv_ac))
                        P.t(lambda e, hp=hp, qc=qc, qi=qi, c0=c0, wcols=wcols, s4=s4: e.matmul(
                            ps3[:, s4 * 512: s4 * 512 + wcols],
                            qfl[hp:hp + 64, qc * T + qi * 128: qc * T + (qi + 1) * 128],
                            kfl[hp:hp + 64, qc * T + c0: qc * T + c0 + wcols],
                            start=True, stop=True),
                            waits=w, inc=pe)
                        scp = pe.n
                        ww = [(pe, scp)]
                        if wd == 0 and r >= 2:
                            ww.append(sraw_free.get(rb, (ac, 0)))
                        dstart = qi * 128 - c0
                        pre_evs = None
                        if 0 <= dstart < wcols:
                            if dstart > 0:
                                if gwa % 2 == 0:
                                    pre_evs = (ac, P.s(
                                        lambda e, rb=rb, c0=c0, s4=s4, dstart=dstart: e.copy(
                                            sraw[:, SROFF[rb] + c0: SROFF[rb] + c0 + dstart],
                                            ps3[:, s4 * 512: s4 * 512 + dstart]),
                                        waits=ww + [(ac, ac.n)], inc=ac))
                                else:
                                    pre_evs = (ve, P.v(
                                        lambda e, rb=rb, c0=c0, s4=s4, dstart=dstart: e.tensor_copy(
                                            sraw[:, SROFF[rb] + c0: SROFF[rb] + c0 + dstart],
                                            ps3[:, s4 * 512: s4 * 512 + dstart]),
                                        waits=ww + [(ve, ve.n)], inc=ve))
                                ww = [(pe, scp)]
                            evd = P.v(lambda e, rb=rb, c0=c0, s4=s4, dstart=dstart: e.tensor_add(
                                sraw[:, SROFF[rb] + c0 + dstart: SROFF[rb] + c0 + dstart + 128],
                                ps3[:, s4 * 512 + dstart: s4 * 512 + dstart + 128], cmask[:, :]),
                                waits=ww + [(ve, ve.n)], inc=ve)
                            evs = (ve, evd)
                        elif gwa % 2 == 0:
                            evs = (ac, P.s(lambda e, rb=rb, c0=c0, s4=s4, wcols=wcols: e.copy(
                                sraw[:, SROFF[rb] + c0: SROFF[rb] + c0 + wcols],
                                ps3[:, s4 * 512: s4 * 512 + wcols]),
                                waits=ww + [(ac, ac.n)], inc=ac))
                        else:
                            evs = (ve, P.v(lambda e, rb=rb, c0=c0, s4=s4, wcols=wcols: e.tensor_copy(
                                sraw[:, SROFF[rb] + c0: SROFF[rb] + c0 + wcols],
                                ps3[:, s4 * 512: s4 * 512 + wcols]),
                                waits=ww + [(ve, ve.n)], inc=ve))
                        slot_free[s4] = [evs] + ([pre_evs] if pre_evs else [])
                        wmax_done[r] = P.v(lambda e, rb=rb, c0=c0, wcols=wcols, wd=wd: e.reduce_max(
                            stat[:, CW0 + rb * 4 + wd: CW0 + rb * 4 + wd + 1],
                            sraw[:, SROFF[rb] + c0: SROFF[rb] + c0 + wcols], axis=AX.X),
                            waits=[evs, (ve, ve.n)] + ([pre_evs] if pre_evs else []), inc=ve)
                        gwa += 1
                    return nw

                def passB1(r, nw):
                    hh, qi = rows[r]
                    ncols = (qi + 1) * 128
                    rb = r % 2
                    cmb = P.v(lambda e, rb=rb, nw=nw: e.reduce_max(
                        stat[:, CM0 + rb:CM0 + rb + 1], stat[:, CW0 + rb * 4: CW0 + rb * 4 + nw],
                        axis=AX.X),
                        waits=[(ve, wmax_done[r]), (ve, ve.n)], inc=ve)
                    bs = P.gp(lambda e, rb=rb: e.tensor_scalar(
                        stat[:, CB0 + rb:CB0 + rb + 1], stat[:, CM0 + rb:CM0 + rb + 1],
                        -0.125, None, ALU.mult),
                        waits=[(ve, cmb), (po, po.n)], inc=po, amt=1)
                    exqs = []
                    pf = prow_free.get(rb, (pe, 0))
                    for wd in range(nw):
                        c0 = wd * 512
                        wcols = min(512, ncols - c0)
                        exqs.append(P.s(lambda e, rb=rb, c0=c0, wcols=wcols, wd=wd: e.activation(
                            prow[:, SROFF[rb] + c0: SROFF[rb] + c0 + wcols],
                            sraw[:, SROFF[rb] + c0: SROFF[rb] + c0 + wcols],
                            AF.Exp, scale=0.125, bias=stat[:, CB0 + rb:CB0 + rb + 1],
                            accum_out=stat[:, CS0 + rb * 4 + wd: CS0 + rb * 4 + wd + 1]),
                            waits=[(po, bs), (ac, ac.n), pf], inc=ac))
                    lsum = P.v(lambda e, rb=rb, nw=nw: e.reduce_sum(
                        stat[:, CL0 + rb:CL0 + rb + 1], stat[:, CS0 + rb * 4: CS0 + rb * 4 + nw],
                        axis=AX.X),
                        waits=[(ac, exqs[-1]), (ve, ve.n)], inc=ve)
                    sraw_free[rb] = (ac, exqs[-1])
                    return (exqs, lsum)

                def passB2(r, exq):
                    nonlocal ptr_busy
                    exqs, lsum = exq
                    hh, qi = rows[r]
                    hp = (hh % 2) * 64
                    qc = hh // 2
                    ntk = qi + 1
                    rb = r % 2
                    vli = P.v(lambda e, rb=rb: e.reciprocal(stat[:, CLI0 + rb:CLI0 + rb + 1],
                                                            stat[:, CL0 + rb:CL0 + rb + 1]),
                              waits=[(ve, lsum), (ve, ve.n)], inc=ve)
                    tev = {}
                    for j0 in range(0, ntk, 4):
                        jn = min(4, ntk - j0)
                        slot = (j0 // 4) % 2
                        w = [(ac, exqs[j0 // 4]), (pe, ptr_busy), tp_free.get(slot, (pe, 0))]
                        for jj in range(jn):
                            P.t(lambda e, rb=rb, j0=j0, jj=jj, slot=slot: e.matmul(
                                ps3[:, 2048 + slot * 512 + jj * 128: 2048 + slot * 512 + (jj + 1) * 128],
                                prow[:, SROFF[rb] + (j0 + jj) * 128: SROFF[rb] + (j0 + jj + 1) * 128],
                                ident[:, :], start=True, stop=True),
                                waits=(w if jj == 0 else []), inc=(pe if jj == jn - 1 else None))
                        if (j0 // 4) % 2 == 0:
                            tp_free[slot] = (ve, P.v(lambda e, j0=j0, jn=jn, slot=slot: e.tensor_copy(
                                ptr[:, j0 * 128: (j0 + jn) * 128],
                                ps3[:, 2048 + slot * 512: 2048 + slot * 512 + jn * 128]),
                                waits=[(pe, pe.n), (ve, ve.n)], inc=ve))
                        else:
                            tp_free[slot] = (ac, P.s(lambda e, j0=j0, jn=jn, slot=slot: e.copy(
                                ptr[:, j0 * 128: (j0 + jn) * 128],
                                ps3[:, 2048 + slot * 512: 2048 + slot * 512 + jn * 128]),
                                waits=[(pe, pe.n), (ac, ac.n)], inc=ac))
                    prow_free[rb] = (pe, pe.n)
                    pvslot = r % 2
                    w = [tp_free.get(0, (pe, 0)), tp_free.get(1, (pe, 0)),
                         pv_free.get(pvslot, (ac, 0))]
                    for j in range(ntk):
                        P.t(lambda e, j=j, hh=hh, pvslot=pvslot, ntk=ntk: e.matmul(
                            ps3[:, 3072 + pvslot * 512: 3072 + pvslot * 512 + 64],
                            ptr[:, j * 128:(j + 1) * 128],
                            vtl[:, j * 1024 + hh * 64: j * 1024 + (hh + 1) * 64],
                            start=(j == 0), stop=(j == ntk - 1)),
                            waits=(w if j == 0 else []), inc=(pe if j == ntk - 1 else None))
                    ptr_busy = pvp = pe.n
                    ob = r % 2
                    fin = P.s(lambda e, pvslot=pvslot, ob=ob, rb=rb: e.activation(
                        ofin[:, ob * 64:(ob + 1) * 64],
                        ps3[:, 3072 + pvslot * 512: 3072 + pvslot * 512 + 64],
                        AF.Identity, scale=stat[:, CLI0 + rb:CLI0 + rb + 1]),
                        waits=[(pe, pvp), (ve, vli), (ac, ac.n),
                               ofin_busy.get(ob, (pe, 0))], inc=ac)
                    return fin

                def finalize(r, fin):
                    hh, qi = rows[r]
                    hp = (hh % 2) * 64
                    qc = hh // 2
                    ntk = qi + 1
                    ob = r % 2
                    pvslot = r % 2
                    slot = (ntk + 3) // 4 % 2
                    P.t(lambda e, slot=slot, ob=ob: e.matmul(
                        ps3[0:64, 2048 + slot * 512: 2048 + slot * 512 + 128],
                        ofin[:, ob * 64:(ob + 1) * 64], ident[:, :], start=True, stop=True),
                        waits=[(ac, fin), tp_free.get(slot, (pe, 0))], inc=pe)
                    ofin_busy[ob] = (pe, pe.n)
                    oevq = P.s(lambda e, hp=hp, qc=qc, qi=qi, slot=slot: e.activation(
                        ofm[hp:hp + 64, qc * T + qi * 128: qc * T + (qi + 1) * 128],
                        ps3[0:64, 2048 + slot * 512: 2048 + slot * 512 + 128], AF.Copy),
                        waits=[(pe, pe.n), (ac, ac.n)], inc=ac)
                    tp_free[slot] = (ac, oevq)
                    pv_free[pvslot] = (ac, oevq)

                conv_pool = getattr(es, "conv_pool", [])
                conv_i = [0]

                def drain_conv(k):
                    while k > 0 and conv_i[0] < len(conv_pool):
                        P.gp(conv_pool[conv_i[0]],
                             inc=P.w[conv_i[0] % 2])
                        conv_i[0] += 1
                        k -= 1

                nwl = {0: passA(0)}
                if nrows > 1:
                    nwl[1] = passA(1)
                fins = {}
                for r in range(nrows):
                    exq = passB1(r, nwl.pop(r))
                    if r + 2 < nrows:
                        nwl[r + 2] = passA(r + 2)
                    if r - 1 in fins:
                        finalize(r - 1, fins.pop(r - 1))
                    fins[r] = passB2(r, exq)
                    drain_conv(2)
                finalize(nrows - 1, fins.pop(nrows - 1))
                drain_conv(len(conv_pool))
                es.h12_tile_done = es.fkv_done = (P.w[0].n, P.w[1].n)
                ofm_ac = ac.n
            es2.close()
            P.barrier()

            P.mark("l1a.P4")
            xb = es.enter_context(nc.sbuf_tensor("xb", [128, NT * 1024], F32, side="right"))
            es_p4 = contextlib.ExitStack()
            with es_p4:
                ptile4 = es_p4.enter_context(nc.sbuf_tensor("ptile4", [128, STT * 1024], BF16))
                pf4 = es_p4.enter_context(nc.sbuf_tensor("pf4", [128, KC * ST], BF16))
                xld = es_p4.enter_context(nc.sbuf_tensor("xld", [128, 2 * 1024], F32))
                ps4 = es_p4.enter_context(nc.psum_tensor("ps4", [128, 4096], F32))
                wlx = (P.w[6], P.w[7])
                for kc in range(KC):
                    P.sy(lambda e, kc=kc: e.dma_start(
                        out=wq[:, kc * 1024:(kc + 1) * 1024],
                        in_=kTb["p"][128 * kc:128 * (kc + 1), :]), inc=P.sems["wa"])
                for nc_ in range(KC):
                    P.sy(lambda e, nc_=nc_: e.dma_start(
                        out=wq[:, 8192 + nc_ * 1024: 8192 + (nc_ + 1) * 1024],
                        in_=vVb["p"][128 * nc_:128 * (nc_ + 1), :]), inc=P.sems["wa"])
                pw_ld = P.sems["wa"].n
                L2 = SMX + 16
                tpp4 = {}
                exp_pe4 = {}
                xld_free = {}
                xld_ld = {}
                prev_xi = 0
                def load_xld(gt):
                    par = gt % 2
                    w = [(ve, xld_free.get(par, 0))]
                    if gt < 2:
                        w.append((sto, stash_total))
                    P.sy(lambda e, gt=gt, par=par: e.dma_start(
                        out=xld[:, par * 1024:(par + 1) * 1024],
                        in_=xstash[128 * gt:128 * (gt + 1), :]),
                        waits=w, inc=wlx[par])
                    xld_ld[gt] = (wlx[par], wlx[par].n)

                for sidx in range(NS):
                    for t2 in range(min(2, STT)):
                        load_xld(sidx * STT + t2)
                    for t2 in range(STT):
                        bk = (t2 % 2) * 1024
                        for nh in range(2):
                            for kc in range(KC):
                                w = []
                                if nh == 0 and kc == 0:
                                    w = [(P.sems["wa"], pw_ld), (ac, ofm_ac), (ac, exp_pe4.get(t2 % 2, 0))]
                                P.t(lambda e, kc=kc, nh=nh, bk=bk, sidx=sidx, t2=t2: e.matmul(
                                    ps4[:, bk + nh * 512: bk + (nh + 1) * 512],
                                    ofm[:, kc * T + sidx * ST + t2 * 128: kc * T + sidx * ST + (t2 + 1) * 128],
                                    wq[:, kc * 1024 + nh * 512: kc * 1024 + (nh + 1) * 512],
                                    start=(kc == 0), stop=(kc == KC - 1)),
                                    waits=w, inc=(pe if (nh == 1 and kc == KC - 1) else None))
                        scp = pe.n
                        rm = P.v(lambda e, bk=bk: e.reduce_max(stat[:, SMX:SMX + 1],
                                                               ps4[:, bk:bk + 1024], axis=AX.X),
                                 waits=[(pe, scp), (ve, ve.n)], inc=ve)
                        rb = P.v(lambda e: e.tensor_scalar(stat[:, SMX + 1:SMX + 2], stat[:, SMX:SMX + 1],
                                                           -1.0 / 32.0, None, ALU.mult),
                                 waits=[(ve, rm)], inc=ve)
                        exq = P.s(lambda e, bk=bk, t2=t2: e.activation(
                            ptile4[:, t2 * 1024:(t2 + 1) * 1024], ps4[:, bk:bk + 1024],
                            AF.Exp, scale=1.0 / 32.0, bias=stat[:, SMX + 1:SMX + 2],
                            accum_out=stat[:, SMX + 2:SMX + 3]),
                            waits=[(ve, rb), (ac, ac.n), (pe, tpp4.get(t2, 0))], inc=ac)
                        exp_pe4[t2 % 2] = exq
                        P.v(lambda e, t2=t2: e.reciprocal(stat[:, L2 + t2:L2 + t2 + 1],
                                                          stat[:, SMX + 2:SMX + 3]),
                            waits=[(ac, exq), (ve, ve.n)], inc=ve)

                        def evp4(g, psap, ws, t2=t2):
                            dap = pf4[:, :].rearrange("p (c t3) -> p c t3", t3=ST)[
                                :, 4 * (g % 2):4 * (g % 2) + 4, t2 * 128:(t2 + 1) * 128]
                            return (ve, P.v(lambda e, psap=psap, dap=dap: e.tensor_copy(dap, psap),
                                            waits=ws, inc=ve))

                        tpp4[t2] = transpose_run(
                            [ptile4[:, t2 * 1024 + c * 128: t2 * 1024 + (c + 1) * 128] for c in range(KC)],
                            ps4[:, 2048:3072], 2, evp4, extra_first=[(ac, exq)])
                    pf_ve = ve.n
                    ev4 = {}
                    for t2 in range(STT):
                        gt = sidx * STT + t2
                        bk2 = 2048 + (t2 % 2) * 1024
                        for nc_ in range(KC):
                            for nh in range(2):
                                w = []
                                if nc_ == 0 and nh == 0:
                                    w = [(ve, pf_ve)]
                                    if (t2 % 2) in ev4:
                                        w.append((ve, ev4[t2 % 2]))
                                P.t(lambda e, t2=t2, nc_=nc_, nh=nh, bk2=bk2: e.matmul(
                                    ps4[:, bk2 + nh * 512: bk2 + (nh + 1) * 512],
                                    pf4[:, nc_ * ST + t2 * 128: nc_ * ST + (t2 + 1) * 128],
                                    wq[:, 8192 + nc_ * 1024 + nh * 512: 8192 + nc_ * 1024 + (nh + 1) * 512],
                                    start=(nc_ == 0), stop=(nc_ == KC - 1)),
                                    waits=w, inc=(pe if (nc_ == KC - 1 and nh == 1) else None))
                        avp4 = pe.n
                        ai = P.s(lambda e, gt=gt: e.activation(
                            xb[:, gt * 1024:(gt + 1) * 1024], xld[:, (gt % 2) * 1024:((gt % 2) + 1) * 1024],
                            AF.Identity, scale=stat[:, R1 + gt:R1 + gt + 1],
                            bias=stat[:, MB1 + gt:MB1 + gt + 1]),
                            waits=[xld_ld[gt], (ac, ac.n)], inc=ac)
                        ev4[t2 % 2] = P.v(lambda e, gt=gt, bk2=bk2, t2=t2: e.scalar_tensor_tensor(
                            xb[:, gt * 1024:(gt + 1) * 1024], ps4[:, bk2:bk2 + 1024],
                            stat[:, L2 + t2:L2 + t2 + 1], xb[:, gt * 1024:(gt + 1) * 1024],
                            ALU.mult, ALU.add),
                            waits=[(pe, avp4), (ac, ai), (ve, ve.n)], inc=ve)
                        xld_free[gt % 2] = ev4[t2 % 2]
                        if t2 + 2 < STT:
                            load_xld(gt + 2)
                xb_ve = ve.n
            es3.close()
            P.barrier()


            P.mark("l1a.P5")
            es_p5 = contextlib.ExitStack()
            with es_p5:
                ptile5 = es_p5.enter_context(nc.sbuf_tensor("ptile5", [128, 2 * 2048], BF16))
                ptr5 = es_p5.enter_context(nc.sbuf_tensor("ptr5", [128, STT * 16 * 128], BF16))
                h2fs = es_p5.enter_context(nc.sbuf_tensor("h2fs", [128, STT * 1024], BF16))
                pvh = es_p5.enter_context(nc.sbuf_tensor("pvh", [128, STT * 1024], F32))
                fvs = es_p5.enter_context(nc.sbuf_tensor("fvs", [128, 2 * 1024], BF16))
                xld2 = es_p5.enter_context(nc.sbuf_tensor("xld2", [128, 2 * 1024], F32))
                ps5 = es_p5.enter_context(nc.psum_tensor("ps5", [128, 4096], F32))
                wlf = (P.sems["wa"], P.sems["wb"])
                wlv = (P.sems["wc"], P.sems["wd"])
                wlx2 = (P.w[6], P.w[7])
                FH = SMX + 24
                fk_free = {}
                fk_ld = {}
                fv_free = {}
                fv_ld = {}
                x2_free = {}
                x2_ld = {}
                gfk = 0
                gfv = 0
                prev_sc5 = 0
                prev_tp5 = 0
                prev_ep5 = 0
                h2_tp = {}
                for sidx in range(NS):
                    for t2 in range(STT):
                        gt = sidx * STT + t2
                        half2 = t2 % 2
                        w = [(ve, xb_ve), (ac, ac.n), (ve, prev_ep5)]
                        vr, ar = ln_small(xb[:, gt * 1024:(gt + 1) * 1024], SMX)
                        w2_ = [(ve, vr), (ac, ar), (ac, ac.n)]
                        if t2 >= 2:
                            w2_.append((pe, h2_tp[t2 - 2]))
                        aa = P.s(lambda e, gt=gt, half2=half2: e.activation(
                            nrm[:, half2 * 1024:(half2 + 1) * 1024], xb[:, gt * 1024:(gt + 1) * 1024],
                            AF.Identity, scale=stat[:, SMX + 3:SMX + 4], bias=stat[:, SMX + 4:SMX + 5]),
                            waits=w2_, inc=ac)

                        def evh2(g, psap, ws, t2=t2):
                            dap = h2fs[:, :].rearrange("p (t3 c) -> p t3 c", c=1024)[
                                :, t2, 4 * (g % 2) * 128:(4 * (g % 2) + 4) * 128]
                            return (ve, P.v(lambda e, psap=psap, dap=dap: e.tensor_copy(dap, psap),
                                            waits=ws, inc=ve))

                        h2_tp[t2] = transpose_run(
                            [nrm[:, half2 * 1024 + c * 128: half2 * 1024 + (c + 1) * 128]
                             for c in range(KC)],
                            ps5[:, 0:1024], 2, evh2, extra_first=[(ac, aa), (ve, prev_sc5)])
                    h2_ve = ve.n
                    for half in range(2):
                        parf = gfk % 2
                        w = [(pe, fk_free.get(parf, 0))]
                        if gfk < 2:
                            w += [(P.w[0], es.fkv_done[0]), (P.w[1], es.fkv_done[1])]
                        for kc in range(KC):
                            P.sy(lambda e, kc=kc, half=half: e.dma_start(
                                out=wq[:, kc * 2048:(kc + 1) * 2048],
                                in_=fKb[128 * kc:128 * (kc + 1), half * 2048:(half + 1) * 2048]),
                                waits=(w if kc == 0 else []), inc=wlf[parf])
                        fk_ld[gfk] = (wlf[parf], wlf[parf].n)
                        half_free = {}
                        ep_half = {}
                        tpd_half = {}

                        def p5_scores(t2, half=half, gfk=gfk):
                            hp5 = t2 % 2
                            for wd in range(4):
                                for kc in range(KC):
                                    w = []
                                    if wd == 0 and kc == 0:
                                        w = [fk_ld[gfk], (ve, h2_ve), (ve, prev_ep5),
                                             ep_half.get(hp5, (ac, 0)),
                                             half_free.get(hp5, (ve, 0))]
                                    P.t(lambda e, t2=t2, wd=wd, kc=kc, hp5=hp5: e.matmul(
                                        ps5[:, wd * 512 + hp5 * 2048: wd * 512 + hp5 * 2048 + 512],
                                        h2fs[:, t2 * 1024 + kc * 128: t2 * 1024 + (kc + 1) * 128],
                                        wq[:, kc * 2048 + wd * 512: kc * 2048 + (wd + 1) * 512],
                                        start=(kc == 0), stop=(kc == KC - 1)),
                                        waits=w, inc=(pe if kc == KC - 1 else None))
                                P.v(lambda e, wd=wd, hp5=hp5: e.reduce_max(
                                    stat[:, SMX + 8 + hp5 * 4 + wd:SMX + 9 + hp5 * 4 + wd],
                                    ps5[:, wd * 512 + hp5 * 2048: wd * 512 + hp5 * 2048 + 512],
                                    axis=AX.X),
                                    waits=[(pe, pe.n), (ve, ve.n)], inc=ve)
                            return ve.n

                        def p5_soft(t2, wmx, half=half):
                            nonlocal prev_tp5
                            hp5 = t2 % 2
                            vm = P.v(lambda e, t2=t2, half=half, hp5=hp5: e.reduce_max(
                                stat[:, FH + half * 16 + t2:FH + half * 16 + t2 + 1],
                                stat[:, SMX + 8 + hp5 * 4:SMX + 12 + hp5 * 4], axis=AX.X),
                                waits=[(ve, wmx), (ve, ve.n)], inc=ve)
                            vb = P.v(lambda e, t2=t2, half=half, hp5=hp5: e.tensor_scalar(
                                stat[:, SMX + 16 + hp5:SMX + 17 + hp5],
                                stat[:, FH + half * 16 + t2:FH + half * 16 + t2 + 1],
                                -1.0 / 32.0, None, ALU.mult),
                                waits=[(ve, vm)], inc=ve)
                            for wd in range(4):
                                P.s(lambda e, wd=wd, hp5=hp5: e.activation(
                                    ptile5[:, hp5 * 2048 + wd * 512: hp5 * 2048 + (wd + 1) * 512],
                                    ps5[:, wd * 512 + hp5 * 2048: wd * 512 + hp5 * 2048 + 512],
                                    AF.Exp, scale=1.0 / 32.0, bias=stat[:, SMX + 16 + hp5:SMX + 17 + hp5],
                                    accum_out=stat[:, SMX + 8 + hp5 * 4 + wd:SMX + 9 + hp5 * 4 + wd]),
                                    waits=[(ve, vb), (ac, ac.n), tpd_half.get(hp5, (pe, 0))], inc=ac)
                            ex5 = ac.n
                            ep_half[hp5] = (ac, ex5)
                            P.v(lambda e, t2=t2, half=half, hp5=hp5: e.reduce_sum(
                                stat[:, FH + 32 + half * 16 + t2:FH + 32 + half * 16 + t2 + 1],
                                stat[:, SMX + 8 + hp5 * 4:SMX + 12 + hp5 * 4], axis=AX.X),
                                waits=[(ac, ex5), (ve, ve.n)], inc=ve)
                            tslot = {}
                            for j0 in range(0, 16, 4):
                                slot = (j0 // 4) % 2
                                w = [(ac, ex5)]
                                if slot in tslot:
                                    w.append((ve, tslot[slot]))
                                for jj in range(4):
                                    P.t(lambda e, j0=j0, jj=jj, slot=slot, t2=t2, hp5=hp5: e.matmul(
                                        ps5[:, 1024 + hp5 * 2048 + slot * 512 + jj * 128:
                                            1024 + hp5 * 2048 + slot * 512 + (jj + 1) * 128],
                                        ptile5[:, hp5 * 2048 + (j0 + jj) * 128: hp5 * 2048 + (j0 + jj + 1) * 128],
                                        ident[:, :], start=True, stop=True),
                                        waits=(w if jj == 0 else []),
                                        inc=(pe if jj == 3 else None))
                                tslot[slot] = P.v(lambda e, j0=j0, slot=slot, t2=t2, hp5=hp5: e.tensor_copy(
                                    ptr5[:, t2 * 2048 + j0 * 128: t2 * 2048 + (j0 + 4) * 128],
                                    ps5[:, 1024 + hp5 * 2048 + slot * 512:
                                        1024 + hp5 * 2048 + slot * 512 + 512]),
                                    waits=[(pe, pe.n), (ve, ve.n)], inc=ve)
                            tpd_half[hp5] = (pe, pe.n)
                            prev_tp5 = pe.n
                            half_free[hp5] = (ve, ve.n)

                        wmxs = {0: p5_scores(0)}
                        for t2 in range(STT):
                            if t2 + 1 < STT:
                                wmxs[t2 + 1] = p5_scores(t2 + 1)
                            p5_soft(t2, wmxs.pop(t2))
                        ptr_ve5 = ve.n
                        prev_sc5 = ptr_ve5
                        pv_done = {}
                        for nc2 in range(16):
                            parv = gfv % 2
                            w = [(pe, fv_free.get(parv, 0))]
                            if gfv < 2:
                                w += [(P.w[0], es.fkv_done[0]), (P.w[1], es.fkv_done[1])]
                            P.sy(lambda e, nc2=nc2, half=half, parv=parv: e.dma_start(
                                out=fvs[:, parv * 1024:(parv + 1) * 1024],
                                in_=fVb[half * 2048 + 128 * nc2: half * 2048 + 128 * (nc2 + 1), :]),
                                waits=w, inc=wlv[parv])
                            fv_ld[gfv] = (wlv[parv], wlv[parv].n)
                            for t2 in range(STT):
                                for nh in range(2):
                                    w = []
                                    if nh == 0:
                                        w = [fv_ld[gfv]]
                                        if nc2 == 0:
                                            w += [(ve, ptr_ve5), (ve, prev_ep5)]
                                    P.t(lambda e, nc2=nc2, t2=t2, nh=nh, parv=parv: e.matmul(
                                        ps5[:, t2 * 1024 + nh * 512: t2 * 1024 + (nh + 1) * 512],
                                        ptr5[:, t2 * 2048 + nc2 * 128: t2 * 2048 + (nc2 + 1) * 128],
                                        fvs[:, parv * 1024 + nh * 512: parv * 1024 + (nh + 1) * 512],
                                        start=(nc2 == 0), stop=(nc2 == 15)),
                                        waits=w,
                                        inc=(pe if (t2 == STT - 1 and nh == 1) else None))
                            fv_free[parv] = pe.n
                            gfv += 1
                        pv_pe = pe.n
                        fk_free[parf] = pe.n
                        gfk += 1
                        if half == 0:
                            stv = 0
                            for t2 in range(STT):
                                stv = P.v(lambda e, t2=t2: e.tensor_copy(
                                    pvh[:, t2 * 1024:(t2 + 1) * 1024], ps5[:, t2 * 1024:(t2 + 1) * 1024]),
                                    waits=[(pe, pv_pe), (ve, ve.n)], inc=ve)
                            prev_ep5 = stv
                        else:
                            def load_xld2(gt):
                                par2 = gt % 2
                                w = [(ve, x2_free.get(par2, 0))]
                                if gt < 2:
                                    w.append((sto, stash_total))
                                P.sy(lambda e, gt=gt, par2=par2: e.dma_start(
                                    out=xld2[:, par2 * 1024:(par2 + 1) * 1024],
                                    in_=xstash[128 * gt:128 * (gt + 1), :]),
                                    waits=w, inc=wlx2[par2])
                                x2_ld[gt] = (wlx2[par2], wlx2[par2].n)

                            for t2 in range(min(2, STT)):
                                load_xld2(sidx * STT + t2)
                            ep_last = 0
                            for t2 in range(STT):
                                gt = sidx * STT + t2
                                c0 = FH + t2
                                c1 = FH + 16 + t2
                                c2 = FH + 32 + t2
                                c3 = FH + 48 + t2
                                v1 = P.v(lambda e, c0=c0, c1=c1: e.tensor_max(
                                    stat[:, SMX:SMX + 1], stat[:, c0:c0 + 1], stat[:, c1:c1 + 1]),
                                    waits=[(ve, ve.n)], inc=ve)
                                v2 = P.v(lambda e, c0=c0: e.tensor_sub(
                                    stat[:, SMX + 1:SMX + 2], stat[:, c0:c0 + 1], stat[:, SMX:SMX + 1]),
                                    waits=[(ve, v1)], inc=ve)
                                v3 = P.v(lambda e, c1=c1: e.tensor_sub(
                                    stat[:, SMX + 2:SMX + 3], stat[:, c1:c1 + 1], stat[:, SMX:SMX + 1]),
                                    waits=[(ve, v2)], inc=ve)
                                a1_ = P.s(lambda e: e.activation(
                                    stat[:, SMX + 1:SMX + 3], stat[:, SMX + 1:SMX + 3],
                                    AF.Exp, scale=1.0 / 32.0),
                                    waits=[(ve, v3), (ac, ac.n)], inc=ac)
                                v4 = P.v(lambda e, c2=c2, c3=c3: e.tensor_mul(
                                    stat[:, SMX + 4:SMX + 5], stat[:, c2:c2 + 1], stat[:, SMX + 1:SMX + 2]),
                                    waits=[(ac, a1_), (ve, ve.n)], inc=ve)
                                v5 = P.v(lambda e, c3=c3: e.scalar_tensor_tensor(
                                    stat[:, SMX + 4:SMX + 5], stat[:, c3:c3 + 1], 1.0,
                                    stat[:, SMX + 4:SMX + 5], ALU.bypass, ALU.mult)
                                    if False else e.tensor_mul(
                                    stat[:, SMX + 5:SMX + 6], stat[:, c3:c3 + 1], stat[:, SMX + 2:SMX + 3]),
                                    waits=[(ve, v4)], inc=ve)
                                v6 = P.v(lambda e: e.tensor_add(
                                    stat[:, SMX + 4:SMX + 5], stat[:, SMX + 4:SMX + 5],
                                    stat[:, SMX + 5:SMX + 6]),
                                    waits=[(ve, v5)], inc=ve)
                                v7 = P.v(lambda e: e.reciprocal(stat[:, SMX + 6:SMX + 7],
                                                                stat[:, SMX + 4:SMX + 5]),
                                         waits=[(ve, v6)], inc=ve)
                                v8 = P.v(lambda e: e.tensor_mul(stat[:, SMX + 1:SMX + 2],
                                                                stat[:, SMX + 1:SMX + 2],
                                                                stat[:, SMX + 6:SMX + 7]),
                                         waits=[(ve, v7)], inc=ve)
                                v9 = P.v(lambda e: e.tensor_mul(stat[:, SMX + 2:SMX + 3],
                                                                stat[:, SMX + 2:SMX + 3],
                                                                stat[:, SMX + 6:SMX + 7]),
                                         waits=[(ve, v8)], inc=ve)
                                y1 = P.v(lambda e, t2=t2, gt=gt: e.scalar_tensor_tensor(
                                    ysc[:, :], pvh[:, t2 * 1024:(t2 + 1) * 1024],
                                    stat[:, SMX + 1:SMX + 2], xb[:, gt * 1024:(gt + 1) * 1024],
                                    ALU.mult, ALU.add),
                                    waits=[(ve, v9), (ve, ep_last)], inc=ve)
                                y2 = P.v(lambda e, t2=t2: e.scalar_tensor_tensor(
                                    ysc[:, :], ps5[:, t2 * 1024:(t2 + 1) * 1024],
                                    stat[:, SMX + 2:SMX + 3], ysc[:, :],
                                    ALU.mult, ALU.add),
                                    waits=[(ve, y1), (pe, pv_pe)], inc=ve)
                                vr, ar = ln_small(ysc[:, :], SMX + 8, [(ve, y2)])
                                v10 = P.v(lambda e, gt=gt, par2=gt % 2: e.tensor_scalar(
                                    xld2[:, par2 * 1024:(par2 + 1) * 1024],
                                    xld2[:, par2 * 1024:(par2 + 1) * 1024],
                                    stat[:, SMX + 8 + 4:SMX + 8 + 5], None, ALU.add),
                                    waits=[(ve, vr), x2_ld[gt]], inc=ve)
                                ep_last = P.v(lambda e, gt=gt, par2=gt % 2: e.scalar_tensor_tensor(
                                    xb[:, gt * 1024:(gt + 1) * 1024], ysc[:, :],
                                    stat[:, SMX + 8 + 3:SMX + 8 + 4],
                                    xld2[:, par2 * 1024:(par2 + 1) * 1024],
                                    ALU.mult, ALU.add),
                                    waits=[(ve, v10)], inc=ve)
                                x2_free[gt % 2] = ep_last
                                if t2 + 2 < STT:
                                    load_xld2(gt + 2)
                            prev_ep5 = ep_last
                xb_ve2 = ve.n
            P.barrier()


        if "l1f" in blocks:
            P.mark("l1f")
            es_g = contextlib.ExitStack()
            with es_g:
                xhs2 = es_g.enter_context(nc.sbuf_tensor("xhs2", [128, 2 * KC * ST], BF16))
                w3sb = es_g.enter_context(nc.sbuf_tensor("w3sb", [128, C], F32))
                P.sy(lambda e: e.dma_start(out=w3sb[:, :], in_=w3sb_in[:, :]), inc=P.sems["ld"])
                w3sb_ld = P.sems["ld"].n
                Ab2 = es_g.enter_context(nc.sbuf_tensor("Ab2", [128, FC * ST], BF16))
                Bb2 = es_g.enter_context(nc.sbuf_tensor("Bb2", [128, FC * ST], BF16))
                wfc2 = es_g.enter_context(nc.sbuf_tensor("wfc2", [128, 2 * 1024], BF16))
                ps_t3 = es_g.enter_context(nc.psum_tensor("ps_t3", [128, 1024], F32))
                ps_m2 = es_g.enter_context(nc.psum_tensor("ps_m2", [128, 2048], F32))
                ps_o2 = es_g.enter_context(nc.psum_tensor("ps_o2", [128, 1024], F32))
                gl1 = (P.w[6], P.w[7])
                gl2 = (P.sems["wa"], P.sems["wb"])
                gl3 = (P.w[0], P.w[1])
                m1_free = {}
                m2_free = {}
                m3_free = {}
                m1_ld = {}
                m2_ld = {}
                m3_ld = {}

                def l1f_mm1_slab(gi, j, srcw):
                    par = gi % 2
                    w = [(pe, m1_free.get(par, 0))]
                    if gi < 2 and hasattr(es, "h12_tile_done"):
                        w += [(P.w[0], es.h12_tile_done[0]), (P.w[1], es.h12_tile_done[1])]
                    for kc in range(KC):
                        P.sy(lambda e, j=j, kc=kc, par=par, srcw=srcw: e.dma_start(
                            out=wq[:, par * 4096 + kc * 512: par * 4096 + (kc + 1) * 512],
                            in_=srcw[128 * kc:128 * (kc + 1), 512 * j:512 * (j + 1)]),
                            waits=(w if kc == 0 else []), inc=gl1[par])
                    m1_ld[gi] = (gl1[par], gl1[par].n)
                    return par

                def l1f_mm2_slab(gi, mc, srcw):
                    par = gi % 2
                    w = [(pe, m2_free.get(par, 0))]
                    if gi < 2 and hasattr(es, "h12_tile_done"):
                        w += [(P.w[0], es.h12_tile_done[0]), (P.w[1], es.h12_tile_done[1])]
                    P.sy(lambda e, mc=mc, par=par, srcw=srcw: e.dma_start(
                        out=wq[:, 8192 + par * 4096: 8192 + (par + 1) * 4096].rearrange(
                            "p (k c) -> p k c", c=128),
                        in_=srcw[mc * F: (mc + 1) * F, :].rearrange("(k p) c -> p k c", p=128)),
                        waits=w, inc=gl2[par])
                    m2_ld[gi] = (gl2[par], gl2[par].n)
                    return par

                def l1f_mm3_slab(gi, kc2):
                    par = gi % 2
                    w = [(pe, m3_free.get(par, 0))]
                    if gi < 2 and hasattr(es, "h12_tile_done"):
                        w += [(P.w[0], es.h12_tile_done[0]), (P.w[1], es.h12_tile_done[1])]
                    P.gp(lambda e, kc2=kc2, par=par: e.dma_start(
                        out=wfc2[:, par * 1024:(par + 1) * 1024],
                        in_=w3b[128 * kc2:128 * (kc2 + 1), :]),
                        waits=w, inc=gl3[par])
                    m3_ld[gi] = (gl3[par], gl3[par].n)
                    return par

                g1 = g2 = g3 = 0
                prev_eva = 0

                def l1f_ln(sidx):
                    par = sidx % 2
                    wv = wa = 0
                    for t2 in range(STT):
                        gt = sidx * STT + t2
                        wv, wa = ln_tile_stats(xb[:, gt * 1024:(gt + 1) * 1024], t2)
                    ln_finalize(STT, wv, wa)
                    mbv, rca = ve.n, ac.n
                    tpa = {}
                    for t2 in range(STT):
                        half = t2 % 2
                        gt = sidx * STT + t2
                        w = [(ve, mbv), (ac, rca), (ac, ac.n)]
                        if t2 >= 2:
                            w.append((pe, tpa[t2 - 2]))
                        aa = P.s(lambda e, gt=gt, half=half, t2=t2: e.activation(
                            nrm[:, half * 1024:(half + 1) * 1024], xb[:, gt * 1024:(gt + 1) * 1024],
                            AF.Identity, scale=stat[:, RC + t2:RC + t2 + 1],
                            bias=stat[:, MB + t2:MB + t2 + 1]),
                            waits=w, inc=ac)

                        def evacx2(g, psap, ws, t2=t2, par=par):
                            dap = xhs2[:, par * KC * ST:(par + 1) * KC * ST].rearrange(
                                "p (c t3) -> p c t3", t3=ST)[
                                :, 4 * (g % 2):4 * (g % 2) + 4, t2 * 128:(t2 + 1) * 128]
                            return (ve, P.v(lambda e, psap=psap, dap=dap: e.tensor_copy(dap, psap),
                                            waits=ws, inc=ve))

                        tpa[t2] = transpose_run(
                            [nrm[:, half * 1024 + c * 128: half * 1024 + (c + 1) * 128] for c in range(KC)],
                            ps_t3, 2, evacx2, extra_first=[(ac, aa)])
                    return ve.n

                xh2_ve_n = {0: l1f_ln(0)}
                prev_l1f_mm2b = 0
                for sidx in range(NS):
                    par0 = sidx % 2
                    xhs_ve = xh2_ve_n[sidx]

                    def xs2_ap(kc, par0=par0):
                        return xhs2[:, par0 * KC * ST + kc * ST: par0 * KC * ST + (kc + 1) * ST]
                    ev1 = {}
                    for j in range(8):
                        par = l1f_mm1_slab(g1, j, w1b)
                        for m4 in range(4):
                            mc = j * 4 + m4
                            bank = mc % 4
                            for kc in range(KC):
                                w = []
                                if kc == 0:
                                    w = [m1_ld[g1], (ve, xhs_ve), (ac, prev_l1f_mm2b), (ve, prev_eva)]
                                    if bank in ev1:
                                        w.append((ac, ev1[bank]))
                                P.t(lambda e, par=par, kc=kc, m4=m4, bank=bank, xs2_ap=xs2_ap: e.matmul(
                                    ps_m2[:, bank * 512: bank * 512 + ST],
                                    wq[:, par * 4096 + kc * 512 + m4 * 128: par * 4096 + kc * 512 + (m4 + 1) * 128],
                                    xs2_ap(kc),
                                    start=(kc == 0), stop=(kc == KC - 1)),
                                    waits=w, inc=(pe if kc == KC - 1 else None))
                            if m4 == 3:
                                m1_free[par] = pe.n
                            ev1[bank] = P.s(lambda e, mc=mc, bank=bank: e.activation(
                                Ab2[:, mc * ST:(mc + 1) * ST], ps_m2[:, bank * 512: bank * 512 + ST],
                                AF.Silu, scale=scv[:, mc:mc + 1]),
                                waits=[(pe, pe.n), (ac, ac.n), (ld, init_ld)], inc=ac)
                        g1 += 1
                    mm1a_ac = ac.n
                    ev1b = {}
                    for j in range(8):
                        par = l1f_mm1_slab(g1, j, w2b)
                        for m4 in range(4):
                            mc = j * 4 + m4
                            bank = mc % 4
                            for kc in range(KC):
                                w = []
                                if kc == 0:
                                    w = [m1_ld[g1], (ve, xhs_ve), (ac, mm1a_ac)]
                                    if bank in ev1b:
                                        w.append((ve, ev1b[bank]))
                                P.t(lambda e, par=par, kc=kc, m4=m4, bank=bank, xs2_ap=xs2_ap: e.matmul(
                                    ps_m2[:, bank * 512: bank * 512 + ST],
                                    wq[:, par * 4096 + kc * 512 + m4 * 128: par * 4096 + kc * 512 + (m4 + 1) * 128],
                                    xs2_ap(kc),
                                    start=(kc == 0), stop=(kc == KC - 1)),
                                    waits=w, inc=(pe if kc == KC - 1 else None))
                            if m4 == 3:
                                m1_free[par] = pe.n
                            ev1b[bank] = P.v(lambda e, mc=mc, bank=bank: e.scalar_tensor_tensor(
                                Bb2[:, mc * ST:(mc + 1) * ST], ps_m2[:, bank * 512: bank * 512 + ST],
                                scv[:, 32 + mc:33 + mc], Ab2[:, mc * ST:(mc + 1) * ST],
                                ALU.mult, ALU.mult),
                                waits=[(pe, pe.n), (ac, mm1a_ac), (ve, ve.n)], inc=ve)
                        g1 += 1
                    mm1b_ve = ve.n
                    if sidx + 1 < NS:
                        xh2_ve_n[sidx + 1] = l1f_ln(sidx + 1)
                    ev2 = {}
                    for mc in range(FC):
                        par = l1f_mm2_slab(g2, mc, h1s_t)
                        bank = mc % 4
                        for kc2 in range(FC):
                            w = []
                            if kc2 == 0:
                                w = [m2_ld[g2], (ve, mm1b_ve)]
                                if bank in ev2:
                                    w.append((ac, ev2[bank]))
                            P.t(lambda e, par=par, kc2=kc2, bank=bank: e.matmul(
                                ps_m2[:, bank * 512: bank * 512 + ST],
                                wq[:, 8192 + par * 4096 + kc2 * 128: 8192 + par * 4096 + (kc2 + 1) * 128],
                                Bb2[:, kc2 * ST: (kc2 + 1) * ST],
                                start=(kc2 == 0), stop=(kc2 == FC - 1)),
                                waits=w, inc=(pe if kc2 == FC - 1 else None))
                        m2_free[par] = pe.n
                        ev2[bank] = P.s(lambda e, mc=mc, bank=bank: e.activation(
                            Ab2[:, mc * ST:(mc + 1) * ST], ps_m2[:, bank * 512: bank * 512 + ST],
                            AF.Gelu, scale=scv[:, 64 + mc:65 + mc]),
                            waits=[(pe, pe.n), (ac, ac.n), (ve, mm1b_ve)], inc=ac)
                        g2 += 1
                    mm2a_ac = ac.n
                    ev2b = {}
                    for mc in range(FC):
                        par = l1f_mm2_slab(g2, mc, h2s_t)
                        bank = mc % 4
                        for kc2 in range(FC):
                            w = []
                            if kc2 == 0:
                                w = [m2_ld[g2], (ac, mm2a_ac)]
                                if bank in ev2b:
                                    w.append((ac, ev2b[bank]))
                            P.t(lambda e, par=par, kc2=kc2, bank=bank: e.matmul(
                                ps_m2[:, bank * 512: bank * 512 + ST],
                                wq[:, 8192 + par * 4096 + kc2 * 128: 8192 + par * 4096 + (kc2 + 1) * 128],
                                Ab2[:, kc2 * ST: (kc2 + 1) * ST],
                                start=(kc2 == 0), stop=(kc2 == FC - 1)),
                                waits=w, inc=(pe if kc2 == FC - 1 else None))
                        m2_free[par] = pe.n
                        ev2b[bank] = P.s(lambda e, mc=mc, bank=bank: e.activation(
                            Bb2[:, mc * ST:(mc + 1) * ST], ps_m2[:, bank * 512: bank * 512 + ST],
                            AF.Gelu, scale=scv[:, 96 + mc:97 + mc]),
                            waits=[(pe, pe.n), (ac, ac.n)], inc=ac)
                        g2 += 1
                    mm2b_ac = ac.n
                    prev_l1f_mm2b = mm2b_ac
                    for hf in range(STT // 2):
                        for kc2 in range(FC):
                            par = l1f_mm3_slab(g3, kc2)
                            for ti in range(2):
                                t2 = hf * 2 + ti
                                for n2 in range(2):
                                    w = []
                                    if kc2 == 0 and ti == 0 and n2 == 0:
                                        w = [(ac, mm2b_ac), (ve, prev_eva)]
                                    if ti == 0 and n2 == 0:
                                        w = w + [m3_ld[g3]]
                                    P.t(lambda e, par=par, kc2=kc2, t2=t2, ti=ti, n2=n2: e.matmul(
                                        ps_m2[:, ti * 1024 + n2 * 512: ti * 1024 + (n2 + 1) * 512],
                                        Bb2[:, kc2 * ST + t2 * 128: kc2 * ST + (t2 + 1) * 128],
                                        wfc2[:, par * 1024 + n2 * 512: par * 1024 + (n2 + 1) * 512],
                                        start=(kc2 == 0), stop=(kc2 == FC - 1)),
                                        waits=w,
                                        inc=(pe if (ti == 1 and n2 == 1) else None))
                            m3_free[par] = pe.n
                            g3 += 1
                        mmp = pe.n
                        for ti in range(2):
                            t2 = hf * 2 + ti
                            gt = sidx * STT + t2
                            ya = P.v(lambda e, ti=ti: e.tensor_mul(
                                ysc[:, :], ps_m2[:, ti * 1024:(ti + 1) * 1024], w3sb[:, :]),
                                waits=[(pe, mmp), (P.sems["ld"], w3sb_ld), (ve, prev_eva), (ve, ve.n)], inc=ve)
                            vr, ar = ln_small(ysc[:, :], SMX, [(ve, ya)])
                            v6 = P.v(lambda e, gt=gt: e.tensor_scalar(
                                xb[:, gt * 1024:(gt + 1) * 1024], xb[:, gt * 1024:(gt + 1) * 1024],
                                stat[:, SMX + 4:SMX + 5], None, ALU.add),
                                waits=[(ve, vr)], inc=ve)
                            prev_eva = P.v(lambda e, gt=gt: e.scalar_tensor_tensor(
                                xb[:, gt * 1024:(gt + 1) * 1024], ysc[:, :], stat[:, SMX + 3:SMX + 4],
                                xb[:, gt * 1024:(gt + 1) * 1024], ALU.mult, ALU.add),
                                waits=[(ve, v6)], inc=ve)
            P.barrier()

        def store_residual(buf):
            for tt in range(NT):
                P.sy(lambda e, tt=tt: e.dma_start(out=yout[128 * tt:128 * (tt + 1), :],
                                                  in_=buf[:, tt * 1024:(tt + 1) * 1024]),
                     waits=[(P.ve, P.ve.n), (P.ac, P.ac.n), (P.pe, P.pe.n)], inc=sto)
            P.sy(None, waits=[(sto, sto.n)])

        store_residual(xb if "l1a" in blocks else xr)

        semh = {nm: es.enter_context(nc.semaphore(nm)) for nm in P.sems}
        P.replay(nc, semh)
    build_nc.last_marks = P.marks
    return nc



def prep_weights(i):
    C, F = 1024, 4096
    d = {}
    d["wqkvT"] = np.ascontiguousarray(np.asarray(i["L0_attn_W_qkv"]).T)
    d["wprojT"] = np.ascontiguousarray(np.asarray(i["L0_attn_W_proj"]).T)
    d["bpj_d"] = np.ascontiguousarray(np.broadcast_to(np.asarray(i["L0_attn_b_proj"])[None, :], (128, C)))
    d["k0"] = np.ascontiguousarray(np.asarray(i["L0_ffn_k0"]).view(np.uint8))
    d["k1"] = np.ascontiguousarray(np.asarray(i["L0_ffn_k1"]).view(np.uint8))
    d["k2"] = np.ascontiguousarray(np.asarray(i["L0_ffn_k2"]).view(np.uint8))
    sm = np.zeros((128, 8), np.float32)
    sm[:, 0] = float(i["L0_ffn_s0"])
    sm[:, 1] = float(i["L0_ffn_s1"])
    sm[:, 2] = float(i["L0_ffn_s2"])
    d["smallv_d"] = sm
    for w, (kk, vv) in {"q": ("L1_attn_qK", "L1_attn_qV"), "k": ("L1_attn_kK", "L1_attn_kV"),
                        "v": ("L1_attn_vK", "L1_attn_vV"), "p": ("L1_attn_pK", "L1_attn_pV")}.items():
        d[f"{w}KT"] = np.ascontiguousarray(np.asarray(i[kk]).T)
        d[f"{w}V"] = np.ascontiguousarray(np.asarray(i[vv]))
    d["fKT"] = np.ascontiguousarray(np.asarray(i["L1_attn_fK"]).T)
    d["fV"] = np.ascontiguousarray(np.asarray(i["L1_attn_fV"]))
    d["w1T"] = np.ascontiguousarray(np.asarray(i["L1_ffn_w1_w"]).T)
    d["w2T"] = np.ascontiguousarray(np.asarray(i["L1_ffn_w2_w"]).T)
    d["h1T"] = np.ascontiguousarray(np.asarray(i["L1_ffn_h1_w"]).T)
    d["h2T"] = np.ascontiguousarray(np.asarray(i["L1_ffn_h2_w"]).T)
    d["w3T"] = np.ascontiguousarray(np.asarray(i["L1_ffn_w3_w"]).T)
    sc = np.zeros((128, 128), np.float32)
    sc[:, 0:32] = np.asarray(i["L1_ffn_w1_s"]).reshape(32, 128).T
    sc[:, 32:64] = np.asarray(i["L1_ffn_w2_s"]).reshape(32, 128).T
    sc[:, 64:96] = np.asarray(i["L1_ffn_h1_s"]).reshape(32, 128).T
    sc[:, 96:128] = np.asarray(i["L1_ffn_h2_s"]).reshape(32, 128).T
    d["sc32_d"] = sc
    d["w3sb_d"] = np.ascontiguousarray(np.broadcast_to(np.asarray(i["L1_ffn_w3_s"])[None, :], (128, C)))
    d["idn_d"] = np.eye(128, dtype=np.float32)
    bd = np.zeros((128, 128), np.float32)
    for g in range(4):
        bd[32 * g:32 * (g + 1), 32 * g:32 * (g + 1)] = 1.0
    d["bdiag_d"] = bd
    s4 = np.zeros((4, 128), np.float32)
    for g in range(4):
        s4[g, 32 * g:32 * (g + 1)] = 1.0
    d["sel4_d"] = s4
    cm = np.zeros((128, 128), np.float32)
    cm[np.triu_indices(128, 1)] = NEG
    d["cmask_d"] = cm
    nh = np.zeros((32, 128), np.float32)
    nh[0, :] = -0.5
    d["nh_d"] = nh
    return {k: np.ascontiguousarray(np.asarray(v)) for k, v in d.items()}


def kernel(**inputs):
    inputs = {k: np.asarray(v) for k, v in inputs.items()}
    x = inputs["L0_attn_x"]
    B, T, C = x.shape
    shared = prep_weights(inputs)
    nc = build_nc(T=T)
    in_maps = []
    for b in range(B):
        m = dict(shared)
        m["x"] = np.ascontiguousarray(x[b])
        in_maps.append(m)
    res = run_bass_kernel_spmd(nc, in_maps, core_ids=list(range(8)))
    out = np.stack([np.asarray(r["y"]) for r in res.results], axis=0)
    return out.astype(np.float32)


# version 20
# speedup vs baseline: 1.0210x; 1.0210x over previous
import contextlib
import numpy as np
import concourse.bass as bass
import concourse.mybir as mybir
from concourse.bass_utils import run_bass_kernel_spmd

F32 = mybir.dt.float32
BF16 = mybir.dt.bfloat16
U8 = mybir.dt.uint8
AF = mybir.ActivationFunctionType
ALU = mybir.AluOpType
AX = mybir.AxisListType

ENGINES = ("sync", "gpsimd", "tensor", "vector", "scalar")

NEG = -1.0e30
LN_EPS = 1e-5
L0A_EPS = 1e-15


class S:
    def __init__(self, name):
        self.name = name
        self.n = 0

    def bump(self, a=1):
        self.n += a
        return self.n


class Prog:
    def __init__(self):
        self.marks = []
        self.nmm = 0
        self.ops = {e: [] for e in ENGINES}
        self.lastwait = {e: {} for e in ENGINES}
        self.sems = {}
        for nm in ("ld", "cst", "pe", "ve", "ac", "bar", "sto", "po", "wa", "wb",
                   "wc", "wd", "w0", "w1", "w2", "w3", "w4", "w5", "w6", "w7"):
            self.sems[nm] = S(nm)
        self.ld = self.sems["ld"]
        self.cst = self.sems["cst"]
        self.pe = self.sems["pe"]
        self.ve = self.sems["ve"]
        self.ac = self.sems["ac"]
        self.bar = self.sems["bar"]
        self.sto = self.sems["sto"]
        self.w = [self.sems[f"w{i}"] for i in range(8)]
        self.po = self.sems["po"]

    def emit(self, eng, fn=None, waits=(), inc=None, amt=1):
        lw = self.lastwait[eng]
        w2 = []
        for (s, v) in waits:
            if v is None or v <= 0:
                continue
            if lw.get(s.name, -1) >= v:
                continue
            lw[s.name] = v
            w2.append((s, v))
        self.ops[eng].append((fn, tuple(w2), inc, amt))
        return inc.bump(amt) if inc is not None else None

    def sy(self, fn=None, waits=(), inc=None, amt=16):
        return self.emit("sync", fn, waits, inc, amt)

    def gp(self, fn=None, waits=(), inc=None, amt=16):
        return self.emit("gpsimd", fn, waits, inc, amt)

    def t(self, fn=None, waits=(), inc=None, amt=1):
        if fn is not None:
            self.nmm += 1
        return self.emit("tensor", fn, waits, inc, amt)

    def mark(self, name):
        self.marks.append((name, self.nmm))

    def v(self, fn=None, waits=(), inc=None, amt=1):
        return self.emit("vector", fn, waits, inc, amt)

    def s(self, fn=None, waits=(), inc=None, amt=1):
        return self.emit("scalar", fn, waits, inc, amt)

    def barrier(self):
        target = self.bar.n + 5
        for e in ENGINES:
            self.emit(e, None, inc=self.bar, amt=1)
        for e in ENGINES:
            self.emit(e, None, waits=[(self.bar, target)])
        return target

    def replay(self, nc, semh):
        with nc.Block() as block:
            for ename in ENGINES:
                ops = self.ops[ename]

                def runner(eng, ops=ops):
                    for fn, waits, inc, amt in ops:
                        for (s_, v_) in waits:
                            eng.wait_ge(semh[s_.name], v_)
                        ins = fn(eng) if fn is not None else None
                        if inc is not None:
                            if ins is None:
                                ins = eng.nop(nofuse=True)
                            ins.then_inc(semh[inc.name], amt)

                getattr(block, ename)(runner)


def build_nc(T=2048, blocks=("l0a", "l0f", "l1a", "l1f")):
    NT = T // 128
    ST = min(512, T)
    NS = T // ST
    STT = ST // 128
    C = 1024
    KC = 8
    F = 4096
    FC = 32

    nc = bass.Bass("TRN2", target_bir_lowering=False)

    xin = nc.dram_tensor("x", [T, C], F32, kind="ExternalInput").ap()
    wqkvT = nc.dram_tensor("wqkvT", [C, 3 * C], F32, kind="ExternalInput").ap()
    wprojT = nc.dram_tensor("wprojT", [C, C], F32, kind="ExternalInput").ap()
    bpj_in = nc.dram_tensor("bpj_d", [128, C], F32, kind="ExternalInput").ap()
    k0_in = nc.dram_tensor("k0", [2 * C, F], U8, kind="ExternalInput").ap()
    k1_in = nc.dram_tensor("k1", [F, F], U8, kind="ExternalInput").ap()
    k2_in = nc.dram_tensor("k2", [F, C], U8, kind="ExternalInput").ap()
    smallv = nc.dram_tensor("smallv_d", [128, 8], F32, kind="ExternalInput").ap()
    kTm = {w: nc.dram_tensor(f"{w}KT", [C, C], F32, kind="ExternalInput").ap() for w in ("q", "k", "v", "p")}
    vVm = {w: nc.dram_tensor(f"{w}V", [C, C], F32, kind="ExternalInput").ap() for w in ("q", "k", "v", "p")}
    fKT = nc.dram_tensor("fKT", [C, 4 * C], F32, kind="ExternalInput").ap()
    fV = nc.dram_tensor("fV", [4 * C, C], F32, kind="ExternalInput").ap()
    w1T = nc.dram_tensor("w1T", [C, F], F32, kind="ExternalInput").ap()
    w2T = nc.dram_tensor("w2T", [C, F], F32, kind="ExternalInput").ap()
    h1T = nc.dram_tensor("h1T", [F, F], F32, kind="ExternalInput").ap()
    h2T = nc.dram_tensor("h2T", [F, F], F32, kind="ExternalInput").ap()
    w3T = nc.dram_tensor("w3T", [F, C], F32, kind="ExternalInput").ap()
    sc32 = nc.dram_tensor("sc32_d", [128, 128], F32, kind="ExternalInput").ap()
    w3sb_in = nc.dram_tensor("w3sb_d", [128, C], F32, kind="ExternalInput").ap()
    idn_in = nc.dram_tensor("idn_d", [128, 128], F32, kind="ExternalInput").ap()
    bdiag_in = nc.dram_tensor("bdiag_d", [128, 128], F32, kind="ExternalInput").ap()
    sel4_in = nc.dram_tensor("sel4_d", [4, 128], F32, kind="ExternalInput").ap()
    cmask_in = nc.dram_tensor("cmask_d", [128, 128], F32, kind="ExternalInput").ap()
    nh_in = nc.dram_tensor("nh_d", [32, 128], F32, kind="ExternalInput").ap()

    yout = nc.dram_tensor("y", [T, C], F32, kind="ExternalOutput").ap()

    w0s = nc.dram_tensor("w0s", [C, F], BF16, kind="Internal").ap()
    w1s_d = nc.dram_tensor("w1s_d", [32 * F, 128], U8, kind="Internal").ap()
    fKb = nc.dram_tensor("fKb", [C, 4 * C], BF16, kind="Internal").ap()
    fVb = nc.dram_tensor("fVb", [4 * C, C], BF16, kind="Internal").ap()
    kTb = {w: nc.dram_tensor(f"{w}KTb", [C, C], BF16, kind="Internal").ap() for w in ("q", "k", "v", "p")}
    vVb = {w: nc.dram_tensor(f"{w}Vb", [C, C], BF16, kind="Internal").ap() for w in ("q", "k", "v", "p")}
    h1s_t = nc.dram_tensor("h1s_t", [32 * F, 128], BF16, kind="Internal").ap()
    h2s_t = nc.dram_tensor("h2s_t", [32 * F, 128], BF16, kind="Internal").ap()
    w1b = nc.dram_tensor("w1b", [C, F], BF16, kind="Internal").ap()
    w2b = nc.dram_tensor("w2b", [C, F], BF16, kind="Internal").ap()
    w3b = nc.dram_tensor("w3b", [F, C], BF16, kind="Internal").ap()
    xstash = nc.dram_tensor("xstash", [T, C], F32, kind="Internal").ap()

    P = Prog()
    ld, cst, pe, ve, ac, sto = P.ld, P.cst, P.pe, P.ve, P.ac, P.sto

    es = contextlib.ExitStack()
    with es:
        stat = es.enter_context(nc.sbuf_tensor("stat", [128, 512], F32))
        sq = es.enter_context(nc.sbuf_tensor("sq", [128, 1024], F32))
        nrm = es.enter_context(nc.sbuf_tensor("nrm", [128, 2048], BF16))
        ident = es.enter_context(nc.sbuf_tensor("ident", [128, 128], BF16))
        bdiag = es.enter_context(nc.sbuf_tensor("bdiag", [128, 128], F32))
        sel4 = es.enter_context(nc.sbuf_tensor("sel4", [4, 128], BF16))
        cmask = es.enter_context(nc.sbuf_tensor("cmask", [128, 128], F32))
        scv = es.enter_context(nc.sbuf_tensor("scv", [128, 128], F32))
        smv = es.enter_context(nc.sbuf_tensor("smv", [128, 16], F32))
        onec = es.enter_context(nc.sbuf_tensor("onec", [128, 32], BF16))
        nhalf = es.enter_context(nc.sbuf_tensor("nhalf", [32, 128], BF16))
        wq = es.enter_context(nc.sbuf_tensor("wq", [128, 16384], BF16))
        ysc = es.enter_context(nc.sbuf_tensor("ysc", [128, 1024], F32))

        SU, SQ_, MU, VP, RC, MB = 0, 16, 32, 48, 64, 80
        MU1, R1, MB1 = 96, 112, 128
        SMX = 160

        P.gp(lambda e: e.dma_start(out=ident[:, :], in_=idn_in[:, :]), inc=cst)
        P.gp(lambda e: e.dma_start(out=sel4[:, :], in_=sel4_in[:, :]), inc=cst)
        P.gp(lambda e: e.dma_start(out=nhalf[:, :], in_=nh_in[:, :]), inc=cst)
        consts_cst = cst.n
        P.sy(lambda e: e.dma_start(out=bdiag[:, :], in_=bdiag_in[:, :]), inc=ld)
        P.sy(lambda e: e.dma_start(out=cmask[:, :], in_=cmask_in[:, :]), inc=ld)
        P.sy(lambda e: e.dma_start(out=scv[:, :], in_=sc32[:, :]), inc=ld)
        P.sy(lambda e: e.dma_start(out=smv[:, 0:8], in_=smallv[:, :]), inc=ld)
        P.v(lambda e: e.memset(onec[:, :], 1.0), inc=ve)
        es1 = es.enter_context(contextlib.ExitStack())
        xr = es1.enter_context(nc.sbuf_tensor("xr", [128, NT * 1024], F32, side="right"))
        x3 = xin.rearrange("(t p) c -> p t c", p=128)
        P.sy(lambda e: e.dma_start(out=xr[:, :].rearrange("p (t c) -> p t c", c=1024), in_=x3),
             inc=ld)
        x_ld = init_ld = ld.n
        P.v(lambda e: e.tensor_scalar(smv[:, 4:5], smv[:, 0:1], 4.0, None, ALU.mult),
            waits=[(ld, init_ld)], inc=ve)
        P.v(lambda e: e.tensor_scalar(smv[:, 5:6], smv[:, 1:2], 2.0, None, ALU.mult), inc=ve)
        P.v(lambda e: e.tensor_scalar(smv[:, 7:8], smv[:, 1:2], 4.0, None, ALU.mult), inc=ve)
        P.v(lambda e: e.tensor_scalar(smv[:, 8:9], smv[:, 2:3], 2.0, None, ALU.mult), inc=ve)
        consts_ve = ve.n

        def ln_tile_stats(src_ap, col):
            vv = P.v(lambda e, a=src_ap, c=col: e.reduce_sum(stat[:, SU + c:SU + c + 1], a, axis=AX.X),
                     waits=[(ve, ve.n)], inc=ve)
            aa = P.s(lambda e, a=src_ap, c=col: e.activation(sq[:, :], a, AF.Square,
                                                             accum_out=stat[:, SQ_ + c:SQ_ + c + 1]),
                     waits=[(ac, ac.n)], inc=ac)
            return vv, aa

        def ln_finalize(nt, wv, wa, mu_c=MU, r_c=RC, mb_c=MB):
            P.v(lambda e, n=nt: e.tensor_scalar(stat[:, mu_c:mu_c + n], stat[:, SU:SU + n],
                                                1.0 / C, None, ALU.mult),
                waits=[(ve, wv)], inc=ve)
            P.v(lambda e, n=nt: e.tensor_scalar(stat[:, VP:VP + n], stat[:, SQ_:SQ_ + n],
                                                1.0 / C, LN_EPS, ALU.mult, ALU.add),
                waits=[(ac, wa)], inc=ve)
            P.v(lambda e, n=nt: e.tensor_mul(stat[:, SU:SU + n], stat[:, mu_c:mu_c + n],
                                             stat[:, mu_c:mu_c + n]),
                waits=[(ve, ve.n)], inc=ve)
            vsub = P.v(lambda e, n=nt: e.tensor_sub(stat[:, VP:VP + n], stat[:, VP:VP + n],
                                                    stat[:, SU:SU + n]),
                       waits=[(ve, ve.n)], inc=ve)
            a1 = P.s(lambda e, n=nt: e.activation(stat[:, VP:VP + n], stat[:, VP:VP + n], AF.Ln),
                     waits=[(ve, vsub), (ac, ac.n)], inc=ac)
            a2 = P.s(lambda e, n=nt, r_c=r_c: e.activation(stat[:, r_c:r_c + n], stat[:, VP:VP + n],
                                                           AF.Exp, scale=-0.5),
                     waits=[(ac, a1)], inc=ac)
            vmb = P.v(lambda e, n=nt, mu_c=mu_c, r_c=r_c, mb_c=mb_c: e.scalar_tensor_tensor(
                stat[:, mb_c:mb_c + n], stat[:, mu_c:mu_c + n], -1.0, stat[:, r_c:r_c + n],
                ALU.mult, ALU.mult),
                waits=[(ac, a2), (ve, ve.n)], inc=ve)
            return vmb, a2

        def ln_small(src_ap, base, w_extra=()):
            sv = P.v(lambda e, a=src_ap: e.reduce_sum(stat[:, base:base + 1], a, axis=AX.X),
                     waits=[(ve, ve.n)] + list(w_extra), inc=ve)
            sa = P.s(lambda e, a=src_ap: e.activation(sq[:, :], a, AF.Square,
                                                      accum_out=stat[:, base + 1:base + 2]),
                     waits=[(ac, ac.n)] + list(w_extra), inc=ac)
            v1 = P.v(lambda e: e.tensor_scalar(stat[:, base:base + 1], stat[:, base:base + 1],
                                               1.0 / C, None, ALU.mult),
                     waits=[(ve, sv)], inc=ve)
            v2 = P.v(lambda e: e.tensor_scalar(stat[:, base + 1:base + 2], stat[:, base + 1:base + 2],
                                               1.0 / C, LN_EPS, ALU.mult, ALU.add),
                     waits=[(ac, sa), (ve, v1)], inc=ve)
            v3 = P.v(lambda e: e.scalar_tensor_tensor(stat[:, base + 2:base + 3], stat[:, base:base + 1],
                                                      -1.0, stat[:, base:base + 1], ALU.mult, ALU.mult),
                     waits=[(ve, v2)], inc=ve)
            v4 = P.v(lambda e: e.tensor_add(stat[:, base + 1:base + 2], stat[:, base + 1:base + 2],
                                            stat[:, base + 2:base + 3]),
                     waits=[(ve, v3)], inc=ve)
            a1 = P.s(lambda e: e.activation(stat[:, base + 1:base + 2], stat[:, base + 1:base + 2], AF.Ln),
                     waits=[(ve, v4), (ac, ac.n)], inc=ac)
            a2 = P.s(lambda e: e.activation(stat[:, base + 3:base + 4], stat[:, base + 1:base + 2],
                                            AF.Exp, scale=-0.5),
                     waits=[(ac, a1)], inc=ac)
            v5 = P.v(lambda e: e.scalar_tensor_tensor(stat[:, base + 4:base + 5], stat[:, base:base + 1],
                                                      -1.0, stat[:, base + 3:base + 4], ALU.mult, ALU.mult),
                     waits=[(ac, a2), (ve, ve.n)], inc=ve)
            return v5, a2

        tp_state = {}

        def transpose_run(tiles, pst, nslots, evac_fn, extra_first=()):
            st = tp_state.setdefault(id(pst), {})
            n = len(tiles)
            ngroups = (n + 3) // 4
            g0 = st.get("g", 0)
            for gg in range(ngroups):
                g = g0 + gg
                slot = g % nslots
                lo, hi = gg * 4, min(n, gg * 4 + 4)
                w = list(extra_first) if gg == 0 else []
                if slot in st:
                    w.append(st[slot])
                for i in range(lo, hi):
                    P.t(lambda e, i=i, lo=lo, slot=slot: e.matmul(
                        pst[:, slot * 512 + (i - lo) * 128: slot * 512 + (i - lo + 1) * 128],
                        tiles[i], ident[:, :], start=True, stop=True),
                        waits=(w if i == lo else []),
                        inc=(pe if i == hi - 1 else None))
                st[slot] = evac_fn(g, pst[:, slot * 512: slot * 512 + (hi - lo) * 128], [(pe, pe.n)])
            st["g"] = g0 + ngroups
            return pe.n

        if "l0a" in blocks:
            es_a = contextlib.ExitStack()
            with es_a:
                xh = es_a.enter_context(nc.sbuf_tensor("xh", [128, KC * T], BF16))
                bpj = es_a.enter_context(nc.sbuf_tensor("bpj", [128, C], F32))
                P.sy(lambda e: e.dma_start(out=bpj[:, :], in_=bpj_in[:, :]), inc=P.sems["wa"])
                bpj_ld = P.sems["wa"].n
                ktm = es_a.enter_context(nc.sbuf_tensor("ktm", [128, STT * 1024], BF16))
                vtm = es_a.enter_context(nc.sbuf_tensor("vtm", [128, STT * 1024], BF16))
                qf = es_a.enter_context(nc.sbuf_tensor("qf", [128, KC * ST], BF16))
                qt = qf
                af = es_a.enter_context(nc.sbuf_tensor("af", [128, KC * ST], BF16))
                kvq = es_a.enter_context(nc.sbuf_tensor("kvq", [128, 2 * ST], BF16))
                s1t = es_a.enter_context(nc.sbuf_tensor("s1t", [128, KC * 128], BF16))
                dstt = es_a.enter_context(nc.sbuf_tensor("dstt", [128, KC * 4], BF16))
                dtmp = es_a.enter_context(nc.sbuf_tensor("dtmp", [4, 512], F32))
                dnvb = es_a.enter_context(nc.sbuf_tensor("dnvb", [4, KC * ST], BF16))
                ps_mm = es_a.enter_context(nc.psum_tensor("ps_mm", [128, 1024], F32))
                ps_t = es_a.enter_context(nc.psum_tensor("ps_t", [128, 1024], F32))
                ps_s1 = es_a.enter_context(nc.psum_tensor("ps_s1", [128, 1536], F32))
                ps_x = es_a.enter_context(nc.psum_tensor("ps_x", [128, 512], F32))

                def s1slice(c, a, b):
                    base = (c // 3) * 512 + (c % 3) * 160
                    return ps_s1[:, base + a: base + b]

                def xh_ap(kc, t0, w):
                    return xh[:, kc * T + t0: kc * T + t0 + w]

                w0 = P.w[0]
                for j in range(4):
                    P.gp(lambda e, j=j: e.dma_start(
                        out=wq[:, j * 4096:(j + 1) * 4096].rearrange("p (k c) -> p k c", c=512),
                        in_=wqkvT[:, 1024 + 512 * j: 1024 + 512 * (j + 1)].rearrange(
                            "(k p) c -> p k c", p=128)),
                        inc=w0)
                kv_total = w0.n
                kv_slab_ld = [kv_total] * 4
                if "l0f" in blocks:
                    for r in range(FC):
                        for gq in range(4):
                            P.sy(lambda e, r=r, gq=gq: e.dma_start(
                                out=w1s_d.rearrange("(m k) c -> k m c", k=F)[
                                    128 * r:128 * (r + 1), 8 * gq:8 * gq + 8, :],
                                in_=k1_in[128 * r:128 * (r + 1), :].rearrange(
                                    "p (m c) -> p m c", c=128)[:, 8 * gq:8 * gq + 8, :]),
                                inc=sto)
                k1t_total = sto.n
                k0_ld = (P.w[2], P.w[3])
                k0_st = (P.w[6], P.w[7])
                k0_cv = {}
                k0_sts = {}
                k0_i = [0]

                K0SW = min(2048, KC * ST // 2)
                K0N = (F // K0SW) * 8
                def emit_k0_chunk():
                    i = k0_i[0]
                    if i >= K0N or "l0f" not in blocks:
                        return
                    k0_i[0] += 1
                    nsub = F // K0SW
                    r, wdx = i // nsub, i % nsub
                    p = i % 2
                    aap = qf[:, p * K0SW:(p + 1) * K0SW]
                    bap = af[:, p * K0SW:(p + 1) * K0SW]
                    w = [(k0_st[p], k0_sts.get(p, 0))]
                    P.gp(lambda e, r=r, wdx=wdx, aap=aap: e.dma_start(
                        out=aap, in_=k0_in[128 * r:128 * (r + 1), K0SW * wdx:K0SW * (wdx + 1)]),
                        waits=w, inc=k0_ld[p])
                    P.gp(lambda e, r=r, wdx=wdx, bap=bap: e.dma_start(
                        out=bap, in_=k0_in[1024 + 128 * r:1024 + 128 * (r + 1), K0SW * wdx:K0SW * (wdx + 1)]),
                        inc=k0_ld[p])
                    k0_cv[p] = P.v(lambda e, aap=aap, bap=bap: e.tensor_sub(aap, aap, bap),
                                   waits=[(k0_ld[p], k0_ld[p].n), (ve, ve.n)], inc=ve)
                    P.sy(lambda e, r=r, wdx=wdx, aap=aap: e.dma_start(
                        out=w0s[128 * r:128 * (r + 1), K0SW * wdx:K0SW * (wdx + 1)], in_=aap),
                        waits=[(ve, k0_cv[p])], inc=k0_st[p])
                    k0_sts[p] = k0_st[p].n

                P.mark("l0a.A1")
                P.v(None, waits=[(ld, x_ld)])
                P.s(None, waits=[(ld, x_ld)])
                wv = wa = 0
                for tt in range(NT):
                    wv, wa = ln_tile_stats(xr[:, tt * 1024:(tt + 1) * 1024], tt)
                ln_finalize(NT, wv, wa)
                mbv, rca = ve.n, ac.n
                tp_after = {}
                for tt in range(NT):
                    half = tt % 2
                    w = [(ve, mbv), (ac, rca), (ac, ac.n)]
                    if tt >= 2:
                        w.append((pe, tp_after[tt - 2]))
                    aa = P.s(lambda e, tt=tt, half=half: e.activation(
                        nrm[:, half * 1024:(half + 1) * 1024], xr[:, tt * 1024:(tt + 1) * 1024],
                        AF.Identity, scale=stat[:, RC + tt:RC + tt + 1], bias=stat[:, MB + tt:MB + tt + 1]),
                        waits=w, inc=ac)

                    def evac(g, psap, ws, tt=tt):
                        dap = xh[:, :].rearrange("p (c t2) -> p c t2", t2=T)[:, 4 * (g % 2):4 * (g % 2) + 4,
                                                                            tt * 128:(tt + 1) * 128]
                        return (ve, P.v(lambda e, psap=psap, dap=dap: e.tensor_copy(dap, psap),
                                        waits=ws, inc=ve))

                    tp_after[tt] = transpose_run(
                        [nrm[:, half * 1024 + c * 128: half * 1024 + (c + 1) * 128] for c in range(KC)],
                        ps_t, 2, evac, extra_first=[(ac, aa), (cst, consts_cst)])
                xh_ve = ve.n

                P.mark("l0a.A2")
                kv_ev = {}
                kv_tp = {}
                s1_started = set()
                for sidx in range(NS):
                    for hi, (half, slabs) in enumerate((("k", (0, 1)), ("v", (2, 3)))):
                        for j2, j in enumerate(slabs):
                            if T <= 512 and j2 > 0:
                                pass
                            for oc4 in range(4):
                                bank = oc4 % 2
                                for kc in range(KC):
                                    w = []
                                    if kc == 0:
                                        w = [(P.w[0], kv_slab_ld[j]), (ve, xh_ve)]
                                        if bank in kv_ev:
                                            w.append((ac, kv_ev[bank]))
                                    P.t(lambda e, j=j, kc=kc, oc4=oc4, bank=bank, sidx=sidx: e.matmul(
                                        ps_mm[:, bank * 512: bank * 512 + ST],
                                        wq[:, j * 4096 + kc * 512 + oc4 * 128: j * 4096 + kc * 512 + (oc4 + 1) * 128],
                                        xh_ap(kc, sidx * ST, ST),
                                        start=(kc == 0), stop=(kc == KC - 1)),
                                        waits=w, inc=(pe if kc == KC - 1 else None))
                                mmc = pe.n
                                buf = oc4 % 2
                                fn = AF.Relu if half == "k" else AF.Copy
                                w = [(pe, mmc), (ac, ac.n)]
                                if buf in kv_tp:
                                    w.append((pe, kv_tp[buf]))
                                kv_ev[bank] = P.s(lambda e, buf=buf, bank=bank, fn=fn: e.activation(
                                    kvq[:, buf * ST: buf * ST + ST],
                                    ps_mm[:, bank * 512: bank * 512 + ST], fn),
                                    waits=w, inc=ac)
                                dstb = ktm if half == "k" else vtm
                                oco = j2 * 4 + oc4

                                def evac2(g, psap, ws, dstb=dstb, oco=oco):
                                    dap = dstb[:, :].rearrange("p (t2 c) -> p t2 c", c=1024)[
                                        :, :, oco * 128:(oco + 1) * 128]
                                    return (ve, P.v(lambda e, psap=psap, dap=dap: e.tensor_copy(dap, psap),
                                                    waits=ws, inc=ve))

                                kv_tp[buf] = transpose_run(
                                    [kvq[:, buf * ST + t2 * 128: buf * ST + (t2 + 1) * 128]
                                     for t2 in range(STT)],
                                    ps_t, 2, evac2, extra_first=[(ac, kv_ev[bank])])
                    for _ in range(K0N // NS + 1):
                        emit_k0_chunk()
                    ktmv = ve.n
                    last_s = (sidx == NS - 1)
                    for c in range(KC):
                        for t2 in range(STT):
                            st_first = (sidx == 0 and t2 == 0 and c % 3 == 0)
                            fin = last_s and t2 == STT - 1
                            w = [(ve, ktmv), (ve, consts_ve)] if t2 == 0 else []
                            P.t(lambda e, c=c, t2=t2, st_first=st_first, fin=fin: e.matmul(
                                s1slice(c, 0, 128),
                                ktm[:, t2 * 1024 + c * 128: t2 * 1024 + (c + 1) * 128],
                                vtm[:, t2 * 1024 + c * 128: t2 * 1024 + (c + 1) * 128],
                                start=st_first, stop=fin, skip_group_check=True),
                                waits=w)
                            P.t(lambda e, c=c, t2=t2, fin=fin: e.matmul(
                                s1slice(c, 128, 129),
                                ktm[:, t2 * 1024 + c * 128: t2 * 1024 + (c + 1) * 128],
                                onec[:, 0:1],
                                start=False, stop=fin, skip_group_check=True),
                                inc=(pe if (fin and c == KC - 1) else None))
                s1_pe = pe.n

                for c in range(KC):
                    P.v(lambda e, c=c: e.tensor_mul(s1t[:, c * 128:(c + 1) * 128],
                                                    s1slice(c, 0, 128), bdiag[:, :]),
                        waits=[(pe, s1_pe), (ld, init_ld), (ve, ve.n)], inc=ve)
                P.v(lambda e: e.memset(dstt[:, :], 0.0), waits=[(ve, ve.n)], inc=ve)
                mset = ve.n
                for c in range(KC):
                    for g in range(4):
                        P.v(lambda e, c=c, g=g: e.tensor_copy(
                            dstt[32 * g:32 * (g + 1), c * 4 + g: c * 4 + g + 1],
                            s1slice(c, 128, 129)[32 * g:32 * (g + 1), :]),
                            waits=[(ve, mset)], inc=ve)
                dst_ve = ve.n

                P.mark("l0a.A3")
                q_slab_ld = {}
                prj_ld = None
                prev_dv = 0
                prev_proj_free = 0
                prev_qt_all = 0
                for sidx in range(NS):
                    if sidx == 0:
                        w1 = P.w[1]
                        for j in range(2):
                            P.gp(lambda e, j=j: e.dma_start(
                                out=wq[:, j * 4096:(j + 1) * 4096].rearrange("p (k c) -> p k c", c=512),
                                in_=wqkvT[:, 512 * j: 512 * (j + 1)].rearrange("(k p) c -> p k c", p=128)),
                                waits=([(pe, s1_pe)] if j == 0 else []), inc=w1)
                        P.gp(lambda e: e.dma_start(
                            out=wq[:, 8192:16384].rearrange("p (k c) -> p k c", c=1024),
                            in_=wprojT[:, :].rearrange("(k p) c -> p k c", p=128)), inc=w1)
                        q_slab_ld[0] = q_slab_ld[1] = prj_ld = w1.n
                    q_ev = {}
                    for j in range(2):
                        for oc4 in range(4):
                            bank = oc4 % 2
                            cchunk = j * 4 + oc4
                            for kc in range(KC):
                                w = []
                                if kc == 0:
                                    w = [(P.w[1], q_slab_ld[j]), (ve, xh_ve), (ve, prev_proj_free),
                                         (ve, prev_qt_all)]
                                    if bank in q_ev:
                                        w.append((ac, q_ev[bank]))
                                    elif bank in kv_ev:
                                        w.append((ac, kv_ev[bank]))
                                P.t(lambda e, j=j, kc=kc, oc4=oc4, bank=bank, sidx=sidx: e.matmul(
                                    ps_mm[:, bank * 512: bank * 512 + ST],
                                    wq[:, j * 4096 + kc * 512 + oc4 * 128: j * 4096 + kc * 512 + (oc4 + 1) * 128],
                                    xh_ap(kc, sidx * ST, ST),
                                    start=(kc == 0), stop=(kc == KC - 1)),
                                    waits=w, inc=(pe if kc == KC - 1 else None))
                            mmc = pe.n
                            w = [(pe, mmc), (ac, ac.n),
                                 (k0_st[0], k0_sts.get(0, 0)), (k0_st[1], k0_sts.get(1, 0))]
                            q_ev[bank] = P.s(lambda e, cchunk=cchunk, bank=bank: e.activation(
                                qf[:, cchunk * ST: cchunk * ST + ST],
                                ps_mm[:, bank * 512: bank * 512 + ST], AF.Relu),
                                waits=w, inc=ac)
                    qf_ac = ac.n
                    den_ev = {}
                    for c in range(KC):
                        bp = (c % 2) * 32
                        w = [(ve, ve.n if c == 0 else 0)]
                        if c == 0:
                            w = [(ve, dst_ve), (ac, qf_ac), (ve, prev_dv)]
                        elif (c % 2) in den_ev:
                            w = [(ve, den_ev[c % 2])]
                        P.t(lambda e, c=c, bp=bp: e.matmul(ps_x[bp:bp + 4, 0:ST],
                                                           dstt[:, c * 4:(c + 1) * 4],
                                                           qf[:, c * ST: c * ST + ST],
                                                           start=True, stop=True),
                            waits=w, inc=pe)
                        dmp = pe.n
                        e1 = P.v(lambda e, c=c, bp=bp: e.tensor_scalar(
                            dtmp[0:4, 0:ST], ps_x[bp:bp + 4, 0:ST],
                            L0A_EPS, None, ALU.add),
                            waits=[(pe, dmp), (ve, ve.n)], inc=ve)
                        e2 = P.v(lambda e: e.reciprocal(dtmp[0:4, 0:ST], dtmp[0:4, 0:ST]),
                                 waits=[(ve, e1)], inc=ve)
                        den_ev[c % 2] = P.v(lambda e, c=c: e.tensor_copy(
                            dnvb[0:4, c * ST:(c + 1) * ST], dtmp[0:4, 0:ST]),
                            waits=[(ve, e2)], inc=ve)
                    prev_dv = dv2 = ve.n
                    qt_done = []
                    bc_ev = {}
                    for c in range(KC):
                        bank = c % 2
                        w = [(ve, dv2)]
                        if bank in bc_ev:
                            w.append((ve, bc_ev[bank]))
                        P.t(lambda e, c=c, bank=bank: e.matmul(
                            ps_t[:, bank * 512: bank * 512 + ST], sel4[0:4, :],
                            dnvb[0:4, c * ST:(c + 1) * ST], start=True, stop=True),
                            waits=w, inc=pe)
                        bcp = pe.n
                        bc_ev[bank] = P.v(lambda e, c=c, bank=bank: e.tensor_mul(
                            qt[:, c * ST: c * ST + ST], qf[:, c * ST: c * ST + ST],
                            ps_t[:, bank * 512: bank * 512 + ST]),
                            waits=[(pe, bcp), (ac, qf_ac), (ve, ve.n)], inc=ve)
                        qt_done.append(bc_ev[bank])
                    qt_all = ve.n
                    prev_qt_all = qt_all
                    af_ev = {}
                    for c in range(KC):
                        bank = c % 2
                        w = [(ve, qt_all)]
                        if bank in af_ev:
                            w.append((ac, af_ev[bank]))
                        P.t(lambda e, c=c, bank=bank: e.matmul(
                            ps_mm[:, bank * 512: bank * 512 + ST], s1t[:, c * 128:(c + 1) * 128],
                            qt[:, c * ST: c * ST + ST], start=True, stop=True),
                            waits=w, inc=pe)
                        nmp = pe.n
                        af_ev[bank] = P.s(lambda e, c=c, bank=bank: e.activation(
                            af[:, c * ST: c * ST + ST], ps_mm[:, bank * 512: bank * 512 + ST], AF.Copy),
                            waits=[(pe, nmp), (ac, ac.n),
                                   (k0_st[0], k0_sts.get(0, 0)), (k0_st[1], k0_sts.get(1, 0))], inc=ac)
                    af_all = ac.n
                    for t2 in range(STT):
                        for kc in range(KC):
                            for n2 in range(2):
                                w = []
                                if kc == 0 and n2 == 0:
                                    w = [(ac, af_all), (P.w[1], prj_ld), (ve, prev_proj_free)]
                                P.t(lambda e, kc=kc, n2=n2, t2=t2: e.matmul(
                                    ps_mm[:, n2 * 512:(n2 + 1) * 512],
                                    af[:, kc * ST + t2 * 128: kc * ST + (t2 + 1) * 128],
                                    wq[:, 8192 + kc * 1024 + n2 * 512: 8192 + kc * 1024 + (n2 + 1) * 512],
                                    start=(kc == 0), stop=(kc == KC - 1)),
                                    waits=w, inc=(pe if (kc == KC - 1 and n2 == 1) else None))
                        prp = pe.n
                        gt = sidx * STT + t2
                        ya = P.v(lambda e: e.tensor_add(ysc[:, :], ps_mm[:, :], bpj[:, :]),
                                 waits=[(pe, prp), (P.sems["wa"], bpj_ld), (ve, ve.n)], inc=ve)
                        prev_proj_free = ya
                        vr, ar = ln_small(ysc[:, :], SMX, [(ve, ya)])
                        v6 = P.v(lambda e, gt=gt: e.tensor_scalar(
                            xr[:, gt * 1024:(gt + 1) * 1024], xr[:, gt * 1024:(gt + 1) * 1024],
                            stat[:, SMX + 4:SMX + 5], None, ALU.add),
                            waits=[(ve, vr)], inc=ve)
                        P.v(lambda e, gt=gt: e.scalar_tensor_tensor(
                            xr[:, gt * 1024:(gt + 1) * 1024], ysc[:, :], stat[:, SMX + 3:SMX + 4],
                            xr[:, gt * 1024:(gt + 1) * 1024], ALU.mult, ALU.add),
                            waits=[(ve, v6)], inc=ve)
                P.barrier()


        if "l0f" in blocks:
            P.mark("l0f")
            es_f = contextlib.ExitStack()
            with es_f:
                xhs = es_f.enter_context(nc.sbuf_tensor("xhs", [128, 2 * KC * ST], BF16))
                Ab = es_f.enter_context(nc.sbuf_tensor("Ab", [128, FC * ST], BF16))
                Bb = es_f.enter_context(nc.sbuf_tensor("Bb", [128, FC * ST], BF16))
                wfc = es_f.enter_context(nc.sbuf_tensor("wfc", [128, 2 * 1024], BF16))
                rshl = es_f.enter_context(nc.sbuf_tensor("rshl", [32, 2 * 512], BF16))
                ps_t2 = es_f.enter_context(nc.psum_tensor("ps_t2", [128, 1024], F32))
                ps_m = es_f.enter_context(nc.psum_tensor("ps_m", [128, 2048], F32))
                ps_o = es_f.enter_context(nc.psum_tensor("ps_o", [128, 1024], F32))
                wl1 = (P.w[6], P.w[7])
                wl2 = (P.w[4], P.w[5])
                wl3 = (P.w[0], P.w[1])
                mm1_free = {}
                mm2_free = {}
                mm3_free = {}
                mm1_ld = {}
                mm2_ld = {}
                mm3_ld = {}

                def load_mm1_slab(gi, j):
                    par = gi % 2
                    w = [(pe, mm1_free.get(par, 0))]
                    if gi < 2:
                        w += [(k0_st[0], k0_sts.get(0, 0)), (k0_st[1], k0_sts.get(1, 0))]
                    for kc in range(KC):
                        P.sy(lambda e, j=j, kc=kc, par=par: e.dma_start(
                            out=wq[:, par * 4096 + kc * 512: par * 4096 + (kc + 1) * 512],
                            in_=w0s[128 * kc:128 * (kc + 1), 512 * j:512 * (j + 1)]),
                            waits=(w if kc == 0 else []), inc=wl1[par])
                    mm1_ld[gi] = (wl1[par], wl1[par].n)
                    return par

                def load_mm2_slab(gi, mc):
                    par = gi % 2
                    w = [(pe, mm2_free.get(par, 0))]
                    if gi < 2:
                        w += [(sto, k1t_total)]
                    P.gp(lambda e, mc=mc, par=par: e.dma_start(
                        out=wq[:, 8192 + par * 4096: 8192 + (par + 1) * 4096].rearrange(
                            "p (k c) -> p k c", c=128),
                        in_=w1s_d[mc * F: (mc + 1) * F, :].rearrange("(k p) c -> p k c", p=128)),
                        waits=w, inc=wl2[par])
                    mm2_ld[gi] = (wl2[par], wl2[par].n)
                    return par

                def load_mm3_slab(gi, kc2):
                    par = gi % 2
                    w = [(pe, mm3_free.get(par, 0))]
                    P.gp(lambda e, kc2=kc2, par=par: e.dma_start(
                        out=wfc[:, par * 1024:(par + 1) * 1024],
                        in_=k2_in[128 * kc2:128 * (kc2 + 1), :]),
                        waits=w, inc=wl3[par])
                    mm3_ld[gi] = (wl3[par], wl3[par].n)
                    return par

                g1 = g2 = g3 = 0
                prev_eva = 0

                def l0f_ln(sidx):
                    par = sidx % 2
                    wv = wa = 0
                    for t2 in range(STT):
                        gt = sidx * STT + t2
                        wv, wa = ln_tile_stats(xr[:, gt * 1024:(gt + 1) * 1024], t2)
                    ln_finalize(STT, wv, wa)
                    mbv, rca = ve.n, ac.n
                    tpa = {}
                    for t2 in range(STT):
                        half = t2 % 2
                        gt = sidx * STT + t2
                        w = [(ve, mbv), (ac, rca), (ac, ac.n)]
                        if t2 >= 2:
                            w.append((pe, tpa[t2 - 2]))
                        aa = P.s(lambda e, gt=gt, half=half, t2=t2: e.activation(
                            nrm[:, half * 1024:(half + 1) * 1024], xr[:, gt * 1024:(gt + 1) * 1024],
                            AF.Identity, scale=stat[:, RC + t2:RC + t2 + 1],
                            bias=stat[:, MB + t2:MB + t2 + 1]),
                            waits=w, inc=ac)

                        def evacx(g, psap, ws, t2=t2, par=par):
                            dap = xhs[:, par * KC * ST:(par + 1) * KC * ST].rearrange(
                                "p (c t3) -> p c t3", t3=ST)[
                                :, 4 * (g % 2):4 * (g % 2) + 4, t2 * 128:(t2 + 1) * 128]
                            return (ve, P.v(lambda e, psap=psap, dap=dap: e.tensor_copy(dap, psap),
                                            waits=ws, inc=ve))

                        tpa[t2] = transpose_run(
                            [nrm[:, half * 1024 + c * 128: half * 1024 + (c + 1) * 128] for c in range(KC)],
                            ps_t2, 2, evacx, extra_first=[(ac, aa)])
                    return ve.n

                p2cv = []
                if "l1a" in blocks:
                    for wnm in ("q", "k", "v", "p"):
                        for srcw, dstw in ((kTm[wnm], kTb[wnm]), (vVm[wnm], vVb[wnm])):
                            for r in range(KC):
                                p2cv.append(lambda e, r=r, srcw=srcw, dstw=dstw: e.dma_start(
                                    out=dstw[128 * r:128 * (r + 1), :],
                                    in_=srcw[128 * r:128 * (r + 1), :]))
                p2cv_i = [0]

                def drain_p2cv(k):
                    while k > 0 and p2cv_i[0] < len(p2cv):
                        P.gp(p2cv[p2cv_i[0]], inc=P.w[2 + p2cv_i[0] % 2])
                        p2cv_i[0] += 1
                        k -= 1

                xhs_ve_n = {0: l0f_ln(0)}
                prev_rc_ve = 0
                prev_mm2_ac = 0
                prev_mm3_ac = 0
                for sidx in range(NS):
                    par0 = sidx % 2
                    xhs_ve = xhs_ve_n[sidx]

                    def xs_ap(kc, par0=par0):
                        return xhs[:, par0 * KC * ST + kc * ST: par0 * KC * ST + (kc + 1) * ST]
                    ev1 = {}
                    for j in range(8):
                        if sidx > 0:
                            drain_p2cv(3)
                        par = load_mm1_slab(g1, j)
                        for m4 in range(4):
                            mc = j * 4 + m4
                            bank = mc % 4
                            for kc in range(KC):
                                w = []
                                if kc == 0:
                                    w = [mm1_ld[g1], (ve, xhs_ve), (ve, prev_rc_ve),
                                         (ac, prev_mm2_ac), (ac, prev_mm3_ac)]
                                    if bank in ev1:
                                        w.append((ac, ev1[bank]))
                                P.t(lambda e, par=par, kc=kc, m4=m4, bank=bank, xs_ap=xs_ap: e.matmul(
                                    ps_m[:, bank * 512: bank * 512 + ST],
                                    wq[:, par * 4096 + kc * 512 + m4 * 128: par * 4096 + kc * 512 + (m4 + 1) * 128],
                                    xs_ap(kc),
                                    start=(kc == 0), stop=(kc == KC - 1)),
                                    waits=w, inc=(pe if kc == KC - 1 else None))
                            if m4 == 3:
                                mm1_free[par] = pe.n
                            ev1[bank] = P.s(lambda e, mc=mc, bank=bank: e.activation(
                                Ab[:, mc * ST:(mc + 1) * ST], ps_m[:, bank * 512: bank * 512 + ST],
                                AF.Relu, scale=smv[:, 4:5]),
                                waits=[(pe, pe.n), (ac, ac.n)], inc=ac)
                        g1 += 1
                    mm1_ac = ac.n
                    if sidx + 1 < NS:
                        xhs_ve_n[sidx + 1] = l0f_ln(sidx + 1)
                    for kc2 in range(FC):
                        P.t(lambda e, kc2=kc2: e.matmul(
                            ps_o[0:32, 0:ST], onec[:, 0:32], Ab[:, kc2 * ST:(kc2 + 1) * ST],
                            start=(kc2 == 0), stop=(kc2 == FC - 1)),
                            waits=([(ac, mm1_ac)] if kc2 == 0 else []),
                            inc=(pe if kc2 == FC - 1 else None))
                    rs_pe = pe.n
                    rv1 = P.v(lambda e: e.tensor_copy(rshl[0:32, 0:ST], ps_o[0:32, 0:ST]),
                              waits=[(pe, rs_pe), (ve, ve.n)], inc=ve)
                    rv2 = P.v(lambda e: e.tensor_sub(rshl[0:32, 512:512 + ST], ps_o[0:32, 0:ST],
                                                     rshl[0:32, 0:ST]),
                              waits=[(ve, rv1)], inc=ve)
                    ev2 = {}
                    for mc in range(FC):
                        par = load_mm2_slab(g2, mc)
                        bank = mc % 4
                        for kc2 in range(FC):
                            w = []
                            if kc2 == 0:
                                w = [mm2_ld[g2], (ac, mm1_ac)]
                                if bank in ev2:
                                    w.append((ac, ev2[bank]))
                            P.t(lambda e, par=par, kc2=kc2, bank=bank: e.matmul(
                                ps_m[:, bank * 512: bank * 512 + ST],
                                wq[:, 8192 + par * 4096 + kc2 * 128: 8192 + par * 4096 + (kc2 + 1) * 128],
                                Ab[:, kc2 * ST: (kc2 + 1) * ST],
                                start=(kc2 == 0), stop=False),
                                waits=w)
                        for hl in range(2):
                            P.t(lambda e, hl=hl, bank=bank: e.matmul(
                                ps_m[:, bank * 512: bank * 512 + ST],
                                nhalf[0:32, :], rshl[0:32, hl * 512: hl * 512 + ST],
                                start=False, stop=(hl == 1)),
                                waits=([(ve, rv2)] if hl == 0 else []),
                                inc=(pe if hl == 1 else None))
                        mm2_free[par] = pe.n
                        ev2[bank] = P.s(lambda e, mc=mc, bank=bank: e.activation(
                            Bb[:, mc * ST:(mc + 1) * ST], ps_m[:, bank * 512: bank * 512 + ST],
                            AF.Relu, scale=smv[:, 7:8]),
                            waits=[(pe, pe.n), (ac, ac.n)], inc=ac)
                        g2 += 1
                    mm2_ac = ac.n
                    prev_mm2_ac = mm2_ac
                    for hf in range(STT // 2):
                        for kc2 in range(FC):
                            par = load_mm3_slab(g3, kc2)
                            for ti in range(2):
                                t2 = hf * 2 + ti
                                for n2 in range(2):
                                    w = []
                                    if kc2 == 0 and ti == 0 and n2 == 0:
                                        w = [(ac, mm2_ac), (ve, prev_eva), (ac, prev_mm3_ac)]
                                    if ti == 0 and n2 == 0:
                                        w = w + [mm3_ld[g3]]
                                    P.t(lambda e, par=par, kc2=kc2, t2=t2, ti=ti, n2=n2: e.matmul(
                                        ps_m[:, ti * 1024 + n2 * 512: ti * 1024 + (n2 + 1) * 512],
                                        Bb[:, kc2 * ST + t2 * 128: kc2 * ST + (t2 + 1) * 128],
                                        wfc[:, par * 1024 + n2 * 512: par * 1024 + (n2 + 1) * 512],
                                        start=(kc2 == 0), stop=(kc2 == FC - 1)),
                                        waits=w)
                                P.t(lambda e, kc2=kc2, t2=t2, ti=ti: e.matmul(
                                    ps_o[:, ti * 512: ti * 512 + 1],
                                    Bb[:, kc2 * ST + t2 * 128: kc2 * ST + (t2 + 1) * 128],
                                    onec[:, 0:1],
                                    start=(kc2 == 0), stop=(kc2 == FC - 1)),
                                    inc=(pe if ti == 1 else None))
                            mm3_free[par] = pe.n
                            g3 += 1
                        mmp = pe.n
                        for ti in range(2):
                            t2 = hf * 2 + ti
                            gt = sidx * STT + t2
                            rc1 = P.v(lambda e, ti=ti: e.tensor_scalar(
                                stat[:, SMX + 6:SMX + 7], ps_o[:, ti * 512: ti * 512 + 1],
                                smv[:, 2:3], -1.0, ALU.mult, ALU.mult),
                                waits=[(pe, mmp), (ve, ve.n)], inc=ve)
                            prev_rc_ve = rc1
                            ya = P.s(lambda e, ti=ti: e.activation(
                                ysc[:, :], ps_m[:, ti * 1024:(ti + 1) * 1024],
                                AF.Identity, scale=smv[:, 8:9], bias=stat[:, SMX + 6:SMX + 7]),
                                waits=[(pe, mmp), (ac, ac.n), (ve, prev_eva), (ve, rc1)], inc=ac)
                            prev_mm3_ac = ya
                            vr, ar = ln_small(ysc[:, :], SMX, [(ac, ya)])
                            v6 = P.v(lambda e, gt=gt: e.tensor_scalar(
                                xr[:, gt * 1024:(gt + 1) * 1024], xr[:, gt * 1024:(gt + 1) * 1024],
                                stat[:, SMX + 4:SMX + 5], None, ALU.add),
                                waits=[(ve, vr)], inc=ve)
                            prev_eva = P.v(lambda e, gt=gt: e.scalar_tensor_tensor(
                                xr[:, gt * 1024:(gt + 1) * 1024], ysc[:, :], stat[:, SMX + 3:SMX + 4],
                                xr[:, gt * 1024:(gt + 1) * 1024], ALU.mult, ALU.add),
                                waits=[(ve, v6)], inc=ve)
                drain_p2cv(len(p2cv))
                es.p2cv_done = (P.w[2].n, P.w[3].n)
            P.barrier()


        if "l1a" in blocks:
            P.mark("l1a.N1")
            H1, D1 = 16, 64
            es_h = contextlib.ExitStack()
            hfm = es_h.enter_context(nc.sbuf_tensor("hfm", [128, KC * T], BF16))
            es_n1 = contextlib.ExitStack()
            with es_n1:
                ps_n1 = es_n1.enter_context(nc.psum_tensor("ps_n1", [128, 1024], F32))
                wv = wa = 0
                for tt in range(NT):
                    wv, wa = ln_tile_stats(xr[:, tt * 1024:(tt + 1) * 1024], tt)
                ln_finalize(NT, wv, wa, mu_c=MU1, r_c=R1, mb_c=MB1)
                mbv, rca = ve.n, ac.n
                tpa = {}
                prev_ai = 0
                for tt in range(NT):
                    half = tt % 2
                    w = [(ve, mbv), (ac, rca), (ac, ac.n), (ve, prev_ai)]
                    ai = P.s(lambda e, tt=tt: e.activation(
                        ysc[:, :], xr[:, tt * 1024:(tt + 1) * 1024], AF.Identity,
                        scale=stat[:, R1 + tt:R1 + tt + 1], bias=stat[:, MB1 + tt:MB1 + tt + 1]),
                        waits=w, inc=ac)
                    vr, ar = ln_small(ysc[:, :], SMX, [(ac, ai)])
                    w2_ = [(ve, vr), (ac, ar), (ac, ac.n)]
                    if tt >= 2:
                        w2_.append((pe, tpa[tt - 2]))
                    aa = P.s(lambda e, half=half: e.activation(
                        nrm[:, half * 1024:(half + 1) * 1024], ysc[:, :],
                        AF.Identity, scale=stat[:, SMX + 3:SMX + 4], bias=stat[:, SMX + 4:SMX + 5]),
                        waits=w2_, inc=ac)

                    def evach(g, psap, ws, tt=tt):
                        dap = hfm[:, :].rearrange("p (c t2) -> p c t2", t2=T)[
                            :, 4 * (g % 2):4 * (g % 2) + 4, tt * 128:(tt + 1) * 128]
                        return (ve, P.v(lambda e, psap=psap, dap=dap: e.tensor_copy(dap, psap),
                                        waits=ws, inc=ve))

                    tpa[tt] = transpose_run(
                        [nrm[:, half * 1024 + c * 128: half * 1024 + (c + 1) * 128] for c in range(KC)],
                        ps_n1, 2, evach, extra_first=[(ac, aa)])
                    prev_ai = ve.n
                    P.sy(lambda e, tt=tt: e.dma_start(out=xstash[128 * tt:128 * (tt + 1), :],
                                                      in_=xr[:, tt * 1024:(tt + 1) * 1024]),
                         inc=sto)
                hfm_ve = ve.n
                stash_total = sto.n
            es1.close()
            P.barrier()
            P.s(None, waits=[(sto, stash_total)])
            P.v(None, waits=[(sto, stash_total)])
            P.t(None, waits=[(sto, stash_total)])

            P.mark("l1a.P2")
            es2 = contextlib.ExitStack()
            qfl = es2.enter_context(nc.sbuf_tensor("qfl", [128, KC * T], BF16, side="right"))
            kfl = es2.enter_context(nc.sbuf_tensor("kfl", [128, KC * T], BF16, side="right"))
            vtl = es2.enter_context(nc.sbuf_tensor("vtl", [128, NT * 1024], BF16, side="right"))
            es_p2 = contextlib.ExitStack()
            with es_p2:
                ptile = es_p2.enter_context(nc.sbuf_tensor("ptile", [128, STT * 1024], BF16))
                pf = es_p2.enter_context(nc.sbuf_tensor("pf", [128, KC * ST], BF16))
                ps2 = es_p2.enter_context(nc.psum_tensor("ps2", [128, 4096], F32))
                L2 = SMX + 16
                wlg = (P.sems["wa"], P.sems["wb"])
                wlg2 = (P.sems["wc"], P.sems["wd"])
                kt_ld = {}
                v_ld = {}
                exp_done = {}
                reg_free = {}
                prev_scores_pe = 0
                prev_av_pe = 0
                for widx, wnm in enumerate(("q", "k", "v")):
                    par = widx % 2
                    w = [(pe, prev_scores_pe)]
                    if widx < 2:
                        w += [(P.w[2], es.p2cv_done[0]), (P.w[3], es.p2cv_done[1])]
                    for kc in range(KC):
                        P.sy(lambda e, wnm=wnm, kc=kc: e.dma_start(
                            out=wq[:, kc * 1024:(kc + 1) * 1024],
                            in_=kTb[wnm][128 * kc:128 * (kc + 1), :]),
                            waits=(w if kc == 0 else []), inc=wlg[par])
                    kt_ld[widx] = (wlg[par], wlg[par].n)
                    w = [(pe, prev_av_pe)]
                    if widx < 2:
                        w += [(P.w[2], es.p2cv_done[0]), (P.w[3], es.p2cv_done[1])]
                    for nc_ in range(KC):
                        P.sy(lambda e, wnm=wnm, nc_=nc_: e.dma_start(
                            out=wq[:, 8192 + nc_ * 1024: 8192 + (nc_ + 1) * 1024],
                            in_=vVb[wnm][128 * nc_:128 * (nc_ + 1), :]),
                            waits=(w if nc_ == 0 else []), inc=wlg2[par])
                    v_ld[widx] = (wlg2[par], wlg2[par].n)
                    if widx == 2:
                        cp = es.conv_pool = []
                        for r in range(KC):
                            cp.append(lambda e, r=r: e.dma_start(
                                out=fKb[128 * r:128 * (r + 1), :],
                                in_=fKT[128 * r:128 * (r + 1), :]))
                        for r in range(FC):
                            cp.append(lambda e, r=r: e.dma_start(
                                out=fVb[128 * r:128 * (r + 1), :],
                                in_=fV[128 * r:128 * (r + 1), :]))
                        if "l1f" in blocks:
                            for srcw, dstw in ((w1T, w1b), (w2T, w2b)):
                                for r in range(KC):
                                    cp.append(lambda e, r=r, srcw=srcw, dstw=dstw: e.dma_start(
                                        out=dstw[128 * r:128 * (r + 1), :],
                                        in_=srcw[128 * r:128 * (r + 1), :]))
                            for r in range(FC):
                                cp.append(lambda e, r=r: e.dma_start(
                                    out=w3b[128 * r:128 * (r + 1), :],
                                    in_=w3T[128 * r:128 * (r + 1), :]))
                            for srcw, dstw in ((h1T, h1s_t), (h2T, h2s_t)):
                                for r in range(FC):
                                    for gq in range(4):
                                        cp.append(lambda e, r=r, gq=gq, srcw=srcw, dstw=dstw: e.dma_start(
                                            out=dstw.rearrange("(m k) c -> k m c", k=F)[
                                                128 * r:128 * (r + 1), 8 * gq:8 * gq + 8, :],
                                            in_=srcw[128 * r:128 * (r + 1), :].rearrange(
                                                "p (m c) -> p m c", c=128)[:, 8 * gq:8 * gq + 8, :]))
                    for sidx in range(NS):
                        sc_done = {}
                        for t2 in range(STT):
                            for nh in range(2):
                                for kc in range(KC):
                                    w = []
                                    if nh == 0 and kc == 0:
                                        w = [kt_ld[widx], (ve, hfm_ve),
                                             exp_done.get(t2, (ac, 0)),
                                             reg_free.get(t2, (ac, 0))]
                                    P.t(lambda e, kc=kc, nh=nh, sidx=sidx, t2=t2: e.matmul(
                                        ps2[:, t2 * 1024 + nh * 512: t2 * 1024 + (nh + 1) * 512],
                                        hfm[:, kc * T + sidx * ST + t2 * 128: kc * T + sidx * ST + (t2 + 1) * 128],
                                        wq[:, kc * 1024 + nh * 512: kc * 1024 + (nh + 1) * 512],
                                        start=(kc == 0), stop=(kc == KC - 1)),
                                        waits=w, inc=(pe if (nh == 1 and kc == KC - 1) else None))
                            sc_done[t2] = pe.n
                        for t2 in range(STT):
                            rm = P.v(lambda e, t2=t2: e.reduce_max(
                                stat[:, SMX:SMX + 1], ps2[:, t2 * 1024:(t2 + 1) * 1024], axis=AX.X),
                                waits=[(pe, sc_done[t2]), (ve, ve.n)], inc=ve)
                            rb = P.v(lambda e: e.tensor_scalar(
                                stat[:, SMX + 1:SMX + 2], stat[:, SMX:SMX + 1],
                                -1.0 / 32.0, None, ALU.mult),
                                waits=[(ve, rm)], inc=ve)
                            exq = P.s(lambda e, t2=t2: e.activation(
                                ptile[:, t2 * 1024:(t2 + 1) * 1024], ps2[:, t2 * 1024:(t2 + 1) * 1024],
                                AF.Exp, scale=1.0 / 32.0, bias=stat[:, SMX + 1:SMX + 2],
                                accum_out=stat[:, SMX + 2:SMX + 3]),
                                waits=[(ve, rb), (ac, ac.n)], inc=ac)
                            exp_done[t2] = (ac, exq)
                            rc = P.v(lambda e, t2=t2: e.reciprocal(stat[:, L2 + t2:L2 + t2 + 1],
                                                                   stat[:, SMX + 2:SMX + 3]),
                                     waits=[(ac, exq), (ve, ve.n)], inc=ve)
                            nmv = rc
                            if wnm in ("q", "k"):
                                nmv = P.v(lambda e, t2=t2: e.tensor_scalar_mul(
                                    ptile[:, t2 * 1024:(t2 + 1) * 1024],
                                    ptile[:, t2 * 1024:(t2 + 1) * 1024],
                                    stat[:, L2 + t2:L2 + t2 + 1]),
                                    waits=[(ve, rc), (ac, exq)], inc=ve)
                            for g2_ in range(2):
                                w = [(ve, nmv), (ac, exq)]
                                for jj in range(4):
                                    P.t(lambda e, t2=t2, g2_=g2_, jj=jj: e.matmul(
                                        ps2[:, t2 * 1024 + g2_ * 512 + jj * 128:
                                            t2 * 1024 + g2_ * 512 + (jj + 1) * 128],
                                        ptile[:, t2 * 1024 + (g2_ * 4 + jj) * 128:
                                              t2 * 1024 + (g2_ * 4 + jj + 1) * 128],
                                        ident[:, :], start=True, stop=True),
                                        waits=(w if jj == 0 else []),
                                        inc=(pe if jj == 3 else None))
                                if g2_ == 0:
                                    P.v(lambda e, t2=t2: e.tensor_copy(
                                        pf[:, :].rearrange("p (c t3) -> p c t3", t3=ST)[
                                            :, 0:4, t2 * 128:(t2 + 1) * 128],
                                        ps2[:, t2 * 1024: t2 * 1024 + 512]),
                                        waits=[(pe, pe.n), (ve, ve.n)], inc=ve)
                                else:
                                    P.s(lambda e, t2=t2: e.copy(
                                        pf[:, :].rearrange("p (c t3) -> p c t3", t3=ST)[
                                            :, 4:8, t2 * 128:(t2 + 1) * 128],
                                        ps2[:, t2 * 1024 + 512: t2 * 1024 + 1024]),
                                        waits=[(pe, pe.n), (ac, ac.n)], inc=ac)
                        pf_ve, pf_ac = ve.n, ac.n
                        if sidx == NS - 1:
                            prev_scores_pe = pe.n
                        if wnm in ("q", "k"):
                            dstb = qfl if wnm == "q" else kfl
                            ev = {}
                            for cc in range(KC):
                                reg = (cc % 4) * 512
                                for nc_ in range(KC):
                                    w = []
                                    if nc_ == 0:
                                        w = [(ve, pf_ve), (ac, pf_ac), v_ld[widx]]
                                        if (cc % 4) in ev:
                                            w.append((ac, ev[cc % 4]))
                                    P.t(lambda e, cc=cc, nc_=nc_, reg=reg: e.matmul(
                                        ps2[:, reg: reg + ST],
                                        wq[:, 8192 + nc_ * 1024 + cc * 128: 8192 + nc_ * 1024 + (cc + 1) * 128],
                                        pf[:, nc_ * ST:(nc_ + 1) * ST],
                                        start=(nc_ == 0), stop=(nc_ == KC - 1)),
                                        waits=w, inc=(pe if nc_ == KC - 1 else None))
                                avp2 = pe.n
                                ev[cc % 4] = P.s(lambda e, cc=cc, reg=reg, dstb=dstb, sidx=sidx: e.activation(
                                    dstb[:, cc * T + sidx * ST: cc * T + sidx * ST + ST],
                                    ps2[:, reg: reg + ST], AF.Copy),
                                    waits=[(pe, avp2), (ac, ac.n)], inc=ac)
                            for t2 in range(STT):
                                reg_free[t2] = (ac, ac.n)
                            if sidx == NS - 1:
                                prev_av_pe = pe.n
                        else:
                            ev = {}
                            for t2 in range(STT):
                                reg = t2 * 1024
                                for nc_ in range(KC):
                                    for nh in range(2):
                                        w = []
                                        if nc_ == 0 and nh == 0:
                                            w = [(ve, pf_ve), (ac, pf_ac), v_ld[widx]]
                                        P.t(lambda e, t2=t2, nc_=nc_, nh=nh, reg=reg: e.matmul(
                                            ps2[:, reg + nh * 512: reg + (nh + 1) * 512],
                                            pf[:, nc_ * ST + t2 * 128: nc_ * ST + (t2 + 1) * 128],
                                            wq[:, 8192 + nc_ * 1024 + nh * 512: 8192 + nc_ * 1024 + (nh + 1) * 512],
                                            start=(nc_ == 0), stop=(nc_ == KC - 1)),
                                            waits=w, inc=(pe if (nc_ == KC - 1 and nh == 1) else None))
                                avp2 = pe.n
                                gt = sidx * STT + t2
                                ev[t2] = P.s(lambda e, t2=t2, reg=reg, gt=gt: e.activation(
                                    vtl[:, gt * 1024:(gt + 1) * 1024], ps2[:, reg:reg + 1024],
                                    AF.Identity, scale=stat[:, L2 + t2:L2 + t2 + 1]),
                                    waits=[(pe, avp2), (ac, ac.n)], inc=ac)
                                reg_free[t2] = (ac, ac.n)
                            if sidx == NS - 1:
                                prev_av_pe = pe.n
                qkv_ac = ac.n
            P.barrier()

            P.mark("l1a.P3")
            es_h.close()
            es3 = contextlib.ExitStack()
            ofm = es3.enter_context(nc.sbuf_tensor("ofm", [128, KC * T], BF16))
            es_p3 = contextlib.ExitStack()
            with es_p3:
                sraw = es_p3.enter_context(nc.sbuf_tensor("sraw", [128, 6144], F32))
                ptr = wq[:, 0:2048]
                prow = wq[:, 2048:6144]
                ofin = es_p3.enter_context(nc.sbuf_tensor("ofin", [128, 2 * 64], BF16))
                ps3 = es_p3.enter_context(nc.psum_tensor("ps3", [128, 4096], F32))
                SROFF = (0, 2048, 4096)
                PROFF = (0, 2048)
                po = P.po
                CM0, CB0, CL0, CLI0, CW0 = SMX, SMX + 3, SMX + 6, SMX + 9, SMX + 12
                rows = [(hh, qi) for hh in range(H1) for qi in range(NT)]
                nrows = len(rows)
                slot_free = {}
                sraw_free = {}
                prow_free = {}
                tp_free = {}
                pv_free = {}
                ptr_busy = 0
                ofin_busy = {}
                wmax_done = {}
                gwa = 0

                def passA(r):
                    nonlocal gwa
                    hh, qi = rows[r]
                    hp = (hh % 2) * 64
                    qc = hh // 2
                    ncols = (qi + 1) * 128
                    rb = r % 3
                    nw = (ncols + 511) // 512
                    for wd in range(nw):
                        c0 = wd * 512
                        wcols = min(512, ncols - c0)
                        s4 = gwa % 4
                        w = list(slot_free.get(s4, [(pe, 0)]))
                        if r == 0 and wd == 0:
                            w.append((ac, qkv_ac))
                        P.t(lambda e, hp=hp, qc=qc, qi=qi, c0=c0, wcols=wcols, s4=s4: e.matmul(
                            ps3[:, s4 * 512: s4 * 512 + wcols],
                            qfl[hp:hp + 64, qc * T + qi * 128: qc * T + (qi + 1) * 128],
                            kfl[hp:hp + 64, qc * T + c0: qc * T + c0 + wcols],
                            start=True, stop=True),
                            waits=w, inc=pe)
                        scp = pe.n
                        ww = [(pe, scp)]
                        if wd == 0 and r >= 3:
                            ww.append(sraw_free.get(rb, (ac, 0)))
                        wwk = list(ww)
                        dstart = qi * 128 - c0
                        pre_evs = None
                        if 0 <= dstart < wcols:
                            if dstart > 0:
                                if gwa % 2 == 0:
                                    pre_evs = (ac, P.s(
                                        lambda e, rb=rb, c0=c0, s4=s4, dstart=dstart: e.copy(
                                            sraw[:, SROFF[rb] + c0: SROFF[rb] + c0 + dstart],
                                            ps3[:, s4 * 512: s4 * 512 + dstart]),
                                        waits=ww + [(ac, ac.n)], inc=ac))
                                else:
                                    pre_evs = (ve, P.v(
                                        lambda e, rb=rb, c0=c0, s4=s4, dstart=dstart: e.tensor_copy(
                                            sraw[:, SROFF[rb] + c0: SROFF[rb] + c0 + dstart],
                                            ps3[:, s4 * 512: s4 * 512 + dstart]),
                                        waits=ww + [(ve, ve.n)], inc=ve))
                                ww = wwk
                            evd = P.v(lambda e, rb=rb, c0=c0, s4=s4, dstart=dstart: e.tensor_add(
                                sraw[:, SROFF[rb] + c0 + dstart: SROFF[rb] + c0 + dstart + 128],
                                ps3[:, s4 * 512 + dstart: s4 * 512 + dstart + 128], cmask[:, :]),
                                waits=ww + [(ve, ve.n)], inc=ve)
                            evs = (ve, evd)
                        elif gwa % 2 == 0:
                            evs = (ac, P.s(lambda e, rb=rb, c0=c0, s4=s4, wcols=wcols: e.copy(
                                sraw[:, SROFF[rb] + c0: SROFF[rb] + c0 + wcols],
                                ps3[:, s4 * 512: s4 * 512 + wcols]),
                                waits=ww + [(ac, ac.n)], inc=ac))
                        else:
                            evs = (ve, P.v(lambda e, rb=rb, c0=c0, s4=s4, wcols=wcols: e.tensor_copy(
                                sraw[:, SROFF[rb] + c0: SROFF[rb] + c0 + wcols],
                                ps3[:, s4 * 512: s4 * 512 + wcols]),
                                waits=ww + [(ve, ve.n)], inc=ve))
                        slot_free[s4] = [evs] + ([pre_evs] if pre_evs else [])
                        wmax_done[r] = P.v(lambda e, rb=rb, c0=c0, wcols=wcols, wd=wd: e.reduce_max(
                            stat[:, CW0 + rb * 4 + wd: CW0 + rb * 4 + wd + 1],
                            sraw[:, SROFF[rb] + c0: SROFF[rb] + c0 + wcols], axis=AX.X),
                            waits=[evs, (ve, ve.n)] + ([pre_evs] if pre_evs else []), inc=ve)
                        gwa += 1
                    return nw

                def passB1(r, nw):
                    hh, qi = rows[r]
                    ncols = (qi + 1) * 128
                    rb = r % 3
                    rbp = r % 2
                    cmb = P.v(lambda e, rb=rb, nw=nw: e.reduce_max(
                        stat[:, CM0 + rb:CM0 + rb + 1], stat[:, CW0 + rb * 4: CW0 + rb * 4 + nw],
                        axis=AX.X),
                        waits=[(ve, wmax_done[r]), (ve, ve.n)], inc=ve)
                    bs = P.gp(lambda e, rb=rb: e.tensor_scalar(
                        stat[:, CB0 + rb:CB0 + rb + 1], stat[:, CM0 + rb:CM0 + rb + 1],
                        -0.125, None, ALU.mult),
                        waits=[(ve, cmb), (po, po.n)], inc=po, amt=1)
                    w = [(po, bs), (ac, ac.n), prow_free.get(rbp, (pe, 0))]
                    exq = P.s(lambda e, rb=rb, rbp=rbp, ncols=ncols: e.activation(
                        prow[:, PROFF[rbp]: PROFF[rbp] + ncols], sraw[:, SROFF[rb]: SROFF[rb] + ncols],
                        AF.Exp, scale=0.125, bias=stat[:, CB0 + rb:CB0 + rb + 1],
                        accum_out=stat[:, CL0 + rb:CL0 + rb + 1]),
                        waits=w, inc=ac)
                    sraw_free[rb] = (ac, exq)
                    return exq

                def passB2(r, exq):
                    nonlocal ptr_busy
                    hh, qi = rows[r]
                    hp = (hh % 2) * 64
                    qc = hh // 2
                    ntk = qi + 1
                    rb = r % 3
                    rbp = r % 2
                    vli = P.v(lambda e, rb=rb: e.reciprocal(stat[:, CLI0 + rb:CLI0 + rb + 1],
                                                            stat[:, CL0 + rb:CL0 + rb + 1]),
                              waits=[(ac, exq), (ve, ve.n)], inc=ve)
                    tev = {}
                    for j0 in range(0, ntk, 4):
                        jn = min(4, ntk - j0)
                        slot = (j0 // 4) % 2
                        w = [(ac, exq), (pe, ptr_busy), tp_free.get(slot, (pe, 0))]
                        for jj in range(jn):
                            P.t(lambda e, rbp=rbp, j0=j0, jj=jj, slot=slot: e.matmul(
                                ps3[:, 2048 + slot * 512 + jj * 128: 2048 + slot * 512 + (jj + 1) * 128],
                                prow[:, PROFF[rbp] + (j0 + jj) * 128: PROFF[rbp] + (j0 + jj + 1) * 128],
                                ident[:, :], start=True, stop=True),
                                waits=(w if jj == 0 else []), inc=(pe if jj == jn - 1 else None))
                        if (j0 // 4) % 2 == 0:
                            tp_free[slot] = (ve, P.v(lambda e, j0=j0, jn=jn, slot=slot: e.tensor_copy(
                                ptr[:, j0 * 128: (j0 + jn) * 128],
                                ps3[:, 2048 + slot * 512: 2048 + slot * 512 + jn * 128]),
                                waits=[(pe, pe.n), (ve, ve.n)], inc=ve))
                        else:
                            tp_free[slot] = (ac, P.s(lambda e, j0=j0, jn=jn, slot=slot: e.copy(
                                ptr[:, j0 * 128: (j0 + jn) * 128],
                                ps3[:, 2048 + slot * 512: 2048 + slot * 512 + jn * 128]),
                                waits=[(pe, pe.n), (ac, ac.n)], inc=ac))
                    prow_free[rbp] = (pe, pe.n)
                    pvslot = r % 2
                    w = [tp_free.get(0, (pe, 0)), tp_free.get(1, (pe, 0)),
                         pv_free.get(pvslot, (ac, 0))]
                    for j in range(ntk):
                        P.t(lambda e, j=j, hh=hh, pvslot=pvslot, ntk=ntk: e.matmul(
                            ps3[:, 3072 + pvslot * 512: 3072 + pvslot * 512 + 64],
                            ptr[:, j * 128:(j + 1) * 128],
                            vtl[:, j * 1024 + hh * 64: j * 1024 + (hh + 1) * 64],
                            start=(j == 0), stop=(j == ntk - 1)),
                            waits=(w if j == 0 else []), inc=(pe if j == ntk - 1 else None))
                    ptr_busy = pvp = pe.n
                    ob = r % 2
                    fin = P.s(lambda e, pvslot=pvslot, ob=ob, rb=rb: e.activation(
                        ofin[:, ob * 64:(ob + 1) * 64],
                        ps3[:, 3072 + pvslot * 512: 3072 + pvslot * 512 + 64],
                        AF.Identity, scale=stat[:, CLI0 + rb:CLI0 + rb + 1]),
                        waits=[(pe, pvp), (ve, vli), (ac, ac.n),
                               ofin_busy.get(ob, (pe, 0))], inc=ac)
                    return fin

                def finalize(r, fin):
                    hh, qi = rows[r]
                    hp = (hh % 2) * 64
                    qc = hh // 2
                    ntk = qi + 1
                    ob = r % 2
                    pvslot = r % 2
                    slot = (ntk + 3) // 4 % 2
                    P.t(lambda e, slot=slot, ob=ob: e.matmul(
                        ps3[0:64, 2048 + slot * 512: 2048 + slot * 512 + 128],
                        ofin[:, ob * 64:(ob + 1) * 64], ident[:, :], start=True, stop=True),
                        waits=[(ac, fin), tp_free.get(slot, (pe, 0))], inc=pe)
                    ofin_busy[ob] = (pe, pe.n)
                    oevq = P.s(lambda e, hp=hp, qc=qc, qi=qi, slot=slot: e.activation(
                        ofm[hp:hp + 64, qc * T + qi * 128: qc * T + (qi + 1) * 128],
                        ps3[0:64, 2048 + slot * 512: 2048 + slot * 512 + 128], AF.Copy),
                        waits=[(pe, pe.n), (ac, ac.n)], inc=ac)
                    tp_free[slot] = (ac, oevq)
                    pv_free[pvslot] = (ac, oevq)

                conv_pool = getattr(es, "conv_pool", [])
                conv_i = [0]

                def drain_conv(k):
                    while k > 0 and conv_i[0] < len(conv_pool):
                        P.gp(conv_pool[conv_i[0]],
                             inc=P.w[conv_i[0] % 2])
                        conv_i[0] += 1
                        k -= 1

                LKA = 2
                nwl = {}
                for rr in range(min(LKA, nrows)):
                    nwl[rr] = passA(rr)
                fins = {}
                for r in range(nrows):
                    exq = passB1(r, nwl.pop(r))
                    if r + LKA < nrows:
                        nwl[r + LKA] = passA(r + LKA)
                    if r - 1 in fins:
                        finalize(r - 1, fins.pop(r - 1))
                    fins[r] = passB2(r, exq)
                    drain_conv(2)
                finalize(nrows - 1, fins.pop(nrows - 1))
                drain_conv(len(conv_pool))
                es.h12_tile_done = es.fkv_done = (P.w[0].n, P.w[1].n)
                ofm_ac = ac.n
            es2.close()
            P.barrier()

            P.mark("l1a.P4")
            xb = es.enter_context(nc.sbuf_tensor("xb", [128, NT * 1024], F32, side="right"))
            es_p4 = contextlib.ExitStack()
            with es_p4:
                ptile4 = es_p4.enter_context(nc.sbuf_tensor("ptile4", [128, STT * 1024], BF16))
                pf4 = es_p4.enter_context(nc.sbuf_tensor("pf4", [128, KC * ST], BF16))
                xld = es_p4.enter_context(nc.sbuf_tensor("xld", [128, 2 * 1024], F32))
                ps4 = es_p4.enter_context(nc.psum_tensor("ps4", [128, 4096], F32))
                wlx = (P.w[6], P.w[7])
                for kc in range(KC):
                    P.sy(lambda e, kc=kc: e.dma_start(
                        out=wq[:, kc * 1024:(kc + 1) * 1024],
                        in_=kTb["p"][128 * kc:128 * (kc + 1), :]), inc=P.sems["wa"])
                for nc_ in range(KC):
                    P.sy(lambda e, nc_=nc_: e.dma_start(
                        out=wq[:, 8192 + nc_ * 1024: 8192 + (nc_ + 1) * 1024],
                        in_=vVb["p"][128 * nc_:128 * (nc_ + 1), :]), inc=P.sems["wa"])
                pw_ld = P.sems["wa"].n
                L2 = SMX + 16
                tpp4 = {}
                exp_pe4 = {}
                xld_free = {}
                xld_ld = {}
                prev_xi = 0
                def load_xld(gt):
                    par = gt % 2
                    w = [(ve, xld_free.get(par, 0))]
                    if gt < 2:
                        w.append((sto, stash_total))
                    P.sy(lambda e, gt=gt, par=par: e.dma_start(
                        out=xld[:, par * 1024:(par + 1) * 1024],
                        in_=xstash[128 * gt:128 * (gt + 1), :]),
                        waits=w, inc=wlx[par])
                    xld_ld[gt] = (wlx[par], wlx[par].n)

                for sidx in range(NS):
                    for t2 in range(min(2, STT)):
                        load_xld(sidx * STT + t2)
                    for t2 in range(STT):
                        bk = (t2 % 2) * 1024
                        for nh in range(2):
                            for kc in range(KC):
                                w = []
                                if nh == 0 and kc == 0:
                                    w = [(P.sems["wa"], pw_ld), (ac, ofm_ac), (ac, exp_pe4.get(t2 % 2, 0))]
                                P.t(lambda e, kc=kc, nh=nh, bk=bk, sidx=sidx, t2=t2: e.matmul(
                                    ps4[:, bk + nh * 512: bk + (nh + 1) * 512],
                                    ofm[:, kc * T + sidx * ST + t2 * 128: kc * T + sidx * ST + (t2 + 1) * 128],
                                    wq[:, kc * 1024 + nh * 512: kc * 1024 + (nh + 1) * 512],
                                    start=(kc == 0), stop=(kc == KC - 1)),
                                    waits=w, inc=(pe if (nh == 1 and kc == KC - 1) else None))
                        scp = pe.n
                        rm = P.v(lambda e, bk=bk: e.reduce_max(stat[:, SMX:SMX + 1],
                                                               ps4[:, bk:bk + 1024], axis=AX.X),
                                 waits=[(pe, scp), (ve, ve.n)], inc=ve)
                        rb = P.v(lambda e: e.tensor_scalar(stat[:, SMX + 1:SMX + 2], stat[:, SMX:SMX + 1],
                                                           -1.0 / 32.0, None, ALU.mult),
                                 waits=[(ve, rm)], inc=ve)
                        exq = P.s(lambda e, bk=bk, t2=t2: e.activation(
                            ptile4[:, t2 * 1024:(t2 + 1) * 1024], ps4[:, bk:bk + 1024],
                            AF.Exp, scale=1.0 / 32.0, bias=stat[:, SMX + 1:SMX + 2],
                            accum_out=stat[:, SMX + 2:SMX + 3]),
                            waits=[(ve, rb), (ac, ac.n), (pe, tpp4.get(t2, 0))], inc=ac)
                        exp_pe4[t2 % 2] = exq
                        P.v(lambda e, t2=t2: e.reciprocal(stat[:, L2 + t2:L2 + t2 + 1],
                                                          stat[:, SMX + 2:SMX + 3]),
                            waits=[(ac, exq), (ve, ve.n)], inc=ve)

                        def evp4(g, psap, ws, t2=t2):
                            dap = pf4[:, :].rearrange("p (c t3) -> p c t3", t3=ST)[
                                :, 4 * (g % 2):4 * (g % 2) + 4, t2 * 128:(t2 + 1) * 128]
                            return (ve, P.v(lambda e, psap=psap, dap=dap: e.tensor_copy(dap, psap),
                                            waits=ws, inc=ve))

                        tpp4[t2] = transpose_run(
                            [ptile4[:, t2 * 1024 + c * 128: t2 * 1024 + (c + 1) * 128] for c in range(KC)],
                            ps4[:, 2048:3072], 2, evp4, extra_first=[(ac, exq)])
                    pf_ve = ve.n
                    ev4 = {}
                    for t2 in range(STT):
                        gt = sidx * STT + t2
                        bk2 = 2048 + (t2 % 2) * 1024
                        for nc_ in range(KC):
                            for nh in range(2):
                                w = []
                                if nc_ == 0 and nh == 0:
                                    w = [(ve, pf_ve)]
                                    if (t2 % 2) in ev4:
                                        w.append((ve, ev4[t2 % 2]))
                                P.t(lambda e, t2=t2, nc_=nc_, nh=nh, bk2=bk2: e.matmul(
                                    ps4[:, bk2 + nh * 512: bk2 + (nh + 1) * 512],
                                    pf4[:, nc_ * ST + t2 * 128: nc_ * ST + (t2 + 1) * 128],
                                    wq[:, 8192 + nc_ * 1024 + nh * 512: 8192 + nc_ * 1024 + (nh + 1) * 512],
                                    start=(nc_ == 0), stop=(nc_ == KC - 1)),
                                    waits=w, inc=(pe if (nc_ == KC - 1 and nh == 1) else None))
                        avp4 = pe.n
                        ai = P.s(lambda e, gt=gt: e.activation(
                            xb[:, gt * 1024:(gt + 1) * 1024], xld[:, (gt % 2) * 1024:((gt % 2) + 1) * 1024],
                            AF.Identity, scale=stat[:, R1 + gt:R1 + gt + 1],
                            bias=stat[:, MB1 + gt:MB1 + gt + 1]),
                            waits=[xld_ld[gt], (ac, ac.n)], inc=ac)
                        ev4[t2 % 2] = P.v(lambda e, gt=gt, bk2=bk2, t2=t2: e.scalar_tensor_tensor(
                            xb[:, gt * 1024:(gt + 1) * 1024], ps4[:, bk2:bk2 + 1024],
                            stat[:, L2 + t2:L2 + t2 + 1], xb[:, gt * 1024:(gt + 1) * 1024],
                            ALU.mult, ALU.add),
                            waits=[(pe, avp4), (ac, ai), (ve, ve.n)], inc=ve)
                        xld_free[gt % 2] = ev4[t2 % 2]
                        if t2 + 2 < STT:
                            load_xld(gt + 2)
                xb_ve = ve.n
            es3.close()
            P.barrier()


            P.mark("l1a.P5")
            es_p5 = contextlib.ExitStack()
            with es_p5:
                ptile5 = es_p5.enter_context(nc.sbuf_tensor("ptile5", [128, 2 * 2048], BF16))
                ptr5 = es_p5.enter_context(nc.sbuf_tensor("ptr5", [128, STT * 16 * 128], BF16))
                h2fs = es_p5.enter_context(nc.sbuf_tensor("h2fs", [128, STT * 1024], BF16))
                pvh = es_p5.enter_context(nc.sbuf_tensor("pvh", [128, STT * 1024], F32))
                fvs = es_p5.enter_context(nc.sbuf_tensor("fvs", [128, 2 * 1024], BF16))
                xld2 = es_p5.enter_context(nc.sbuf_tensor("xld2", [128, 2 * 1024], F32))
                ps5 = es_p5.enter_context(nc.psum_tensor("ps5", [128, 4096], F32))
                wlf = (P.sems["wa"], P.sems["wb"])
                wlv = (P.sems["wc"], P.sems["wd"])
                wlx2 = (P.w[6], P.w[7])
                FH = SMX + 24
                fk_free = {}
                fk_ld = {}
                fv_free = {}
                fv_ld = {}
                x2_free = {}
                x2_ld = {}
                gfk = 0
                gfv = 0
                prev_sc5 = 0
                prev_tp5 = 0
                prev_ep5 = 0
                h2_tp = {}
                for sidx in range(NS):
                    for t2 in range(STT):
                        gt = sidx * STT + t2
                        half2 = t2 % 2
                        w = [(ve, xb_ve), (ac, ac.n), (ve, prev_ep5)]
                        vr, ar = ln_small(xb[:, gt * 1024:(gt + 1) * 1024], SMX)
                        w2_ = [(ve, vr), (ac, ar), (ac, ac.n)]
                        if t2 >= 2:
                            w2_.append((pe, h2_tp[t2 - 2]))
                        aa = P.s(lambda e, gt=gt, half2=half2: e.activation(
                            nrm[:, half2 * 1024:(half2 + 1) * 1024], xb[:, gt * 1024:(gt + 1) * 1024],
                            AF.Identity, scale=stat[:, SMX + 3:SMX + 4], bias=stat[:, SMX + 4:SMX + 5]),
                            waits=w2_, inc=ac)

                        def evh2(g, psap, ws, t2=t2):
                            dap = h2fs[:, :].rearrange("p (t3 c) -> p t3 c", c=1024)[
                                :, t2, 4 * (g % 2) * 128:(4 * (g % 2) + 4) * 128]
                            return (ve, P.v(lambda e, psap=psap, dap=dap: e.tensor_copy(dap, psap),
                                            waits=ws, inc=ve))

                        h2_tp[t2] = transpose_run(
                            [nrm[:, half2 * 1024 + c * 128: half2 * 1024 + (c + 1) * 128]
                             for c in range(KC)],
                            ps5[:, 0:1024], 2, evh2, extra_first=[(ac, aa), (ve, prev_sc5)])
                    h2_ve = ve.n
                    for half in range(2):
                        parf = gfk % 2
                        w = [(pe, fk_free.get(parf, 0))]
                        if gfk < 2:
                            w += [(P.w[0], es.fkv_done[0]), (P.w[1], es.fkv_done[1])]
                        for kc in range(KC):
                            P.sy(lambda e, kc=kc, half=half: e.dma_start(
                                out=wq[:, kc * 2048:(kc + 1) * 2048],
                                in_=fKb[128 * kc:128 * (kc + 1), half * 2048:(half + 1) * 2048]),
                                waits=(w if kc == 0 else []), inc=wlf[parf])
                        fk_ld[gfk] = (wlf[parf], wlf[parf].n)
                        half_free = {}
                        ep_half = {}
                        tpd_half = {}

                        def p5_scores(t2, half=half, gfk=gfk):
                            hp5 = t2 % 2
                            for wd in range(4):
                                for kc in range(KC):
                                    w = []
                                    if wd == 0 and kc == 0:
                                        w = [fk_ld[gfk], (ve, h2_ve), (ve, prev_ep5),
                                             ep_half.get(hp5, (ac, 0)),
                                             half_free.get(hp5, (ve, 0))]
                                    P.t(lambda e, t2=t2, wd=wd, kc=kc, hp5=hp5: e.matmul(
                                        ps5[:, wd * 512 + hp5 * 2048: wd * 512 + hp5 * 2048 + 512],
                                        h2fs[:, t2 * 1024 + kc * 128: t2 * 1024 + (kc + 1) * 128],
                                        wq[:, kc * 2048 + wd * 512: kc * 2048 + (wd + 1) * 512],
                                        start=(kc == 0), stop=(kc == KC - 1)),
                                        waits=w, inc=(pe if kc == KC - 1 else None))
                                P.v(lambda e, wd=wd, hp5=hp5: e.reduce_max(
                                    stat[:, SMX + 8 + hp5 * 4 + wd:SMX + 9 + hp5 * 4 + wd],
                                    ps5[:, wd * 512 + hp5 * 2048: wd * 512 + hp5 * 2048 + 512],
                                    axis=AX.X),
                                    waits=[(pe, pe.n), (ve, ve.n)], inc=ve)
                            return ve.n

                        def p5_soft(t2, wmx, half=half):
                            nonlocal prev_tp5
                            hp5 = t2 % 2
                            vm = P.v(lambda e, t2=t2, half=half, hp5=hp5: e.reduce_max(
                                stat[:, FH + half * 16 + t2:FH + half * 16 + t2 + 1],
                                stat[:, SMX + 8 + hp5 * 4:SMX + 12 + hp5 * 4], axis=AX.X),
                                waits=[(ve, wmx), (ve, ve.n)], inc=ve)
                            vb = P.v(lambda e, t2=t2, half=half, hp5=hp5: e.tensor_scalar(
                                stat[:, SMX + 16 + hp5:SMX + 17 + hp5],
                                stat[:, FH + half * 16 + t2:FH + half * 16 + t2 + 1],
                                -1.0 / 32.0, None, ALU.mult),
                                waits=[(ve, vm)], inc=ve)
                            for wd in range(4):
                                P.s(lambda e, wd=wd, hp5=hp5: e.activation(
                                    ptile5[:, hp5 * 2048 + wd * 512: hp5 * 2048 + (wd + 1) * 512],
                                    ps5[:, wd * 512 + hp5 * 2048: wd * 512 + hp5 * 2048 + 512],
                                    AF.Exp, scale=1.0 / 32.0, bias=stat[:, SMX + 16 + hp5:SMX + 17 + hp5],
                                    accum_out=stat[:, SMX + 8 + hp5 * 4 + wd:SMX + 9 + hp5 * 4 + wd]),
                                    waits=[(ve, vb), (ac, ac.n), tpd_half.get(hp5, (pe, 0))], inc=ac)
                            ex5 = ac.n
                            ep_half[hp5] = (ac, ex5)
                            P.v(lambda e, t2=t2, half=half, hp5=hp5: e.reduce_sum(
                                stat[:, FH + 32 + half * 16 + t2:FH + 32 + half * 16 + t2 + 1],
                                stat[:, SMX + 8 + hp5 * 4:SMX + 12 + hp5 * 4], axis=AX.X),
                                waits=[(ac, ex5), (ve, ve.n)], inc=ve)
                            tslot = {}
                            for j0 in range(0, 16, 4):
                                slot = (j0 // 4) % 2
                                w = [(ac, ex5)]
                                if slot in tslot:
                                    w.append((ve, tslot[slot]))
                                for jj in range(4):
                                    P.t(lambda e, j0=j0, jj=jj, slot=slot, t2=t2, hp5=hp5: e.matmul(
                                        ps5[:, 1024 + hp5 * 2048 + slot * 512 + jj * 128:
                                            1024 + hp5 * 2048 + slot * 512 + (jj + 1) * 128],
                                        ptile5[:, hp5 * 2048 + (j0 + jj) * 128: hp5 * 2048 + (j0 + jj + 1) * 128],
                                        ident[:, :], start=True, stop=True),
                                        waits=(w if jj == 0 else []),
                                        inc=(pe if jj == 3 else None))
                                tslot[slot] = P.v(lambda e, j0=j0, slot=slot, t2=t2, hp5=hp5: e.tensor_copy(
                                    ptr5[:, t2 * 2048 + j0 * 128: t2 * 2048 + (j0 + 4) * 128],
                                    ps5[:, 1024 + hp5 * 2048 + slot * 512:
                                        1024 + hp5 * 2048 + slot * 512 + 512]),
                                    waits=[(pe, pe.n), (ve, ve.n)], inc=ve)
                            tpd_half[hp5] = (pe, pe.n)
                            prev_tp5 = pe.n
                            half_free[hp5] = (ve, ve.n)

                        wmxs = {0: p5_scores(0)}
                        for t2 in range(STT):
                            if t2 + 1 < STT:
                                wmxs[t2 + 1] = p5_scores(t2 + 1)
                            p5_soft(t2, wmxs.pop(t2))
                        ptr_ve5 = ve.n
                        prev_sc5 = ptr_ve5
                        pv_done = {}
                        for nc2 in range(16):
                            parv = gfv % 2
                            w = [(pe, fv_free.get(parv, 0))]
                            if gfv < 2:
                                w += [(P.w[0], es.fkv_done[0]), (P.w[1], es.fkv_done[1])]
                            P.sy(lambda e, nc2=nc2, half=half, parv=parv: e.dma_start(
                                out=fvs[:, parv * 1024:(parv + 1) * 1024],
                                in_=fVb[half * 2048 + 128 * nc2: half * 2048 + 128 * (nc2 + 1), :]),
                                waits=w, inc=wlv[parv])
                            fv_ld[gfv] = (wlv[parv], wlv[parv].n)
                            for t2 in range(STT):
                                for nh in range(2):
                                    w = []
                                    if nh == 0:
                                        w = [fv_ld[gfv]]
                                        if nc2 == 0:
                                            w += [(ve, ptr_ve5), (ve, prev_ep5)]
                                    P.t(lambda e, nc2=nc2, t2=t2, nh=nh, parv=parv: e.matmul(
                                        ps5[:, t2 * 1024 + nh * 512: t2 * 1024 + (nh + 1) * 512],
                                        ptr5[:, t2 * 2048 + nc2 * 128: t2 * 2048 + (nc2 + 1) * 128],
                                        fvs[:, parv * 1024 + nh * 512: parv * 1024 + (nh + 1) * 512],
                                        start=(nc2 == 0), stop=(nc2 == 15)),
                                        waits=w,
                                        inc=(pe if (t2 == STT - 1 and nh == 1) else None))
                            fv_free[parv] = pe.n
                            gfv += 1
                        pv_pe = pe.n
                        fk_free[parf] = pe.n
                        gfk += 1
                        if half == 0:
                            stv = 0
                            for t2 in range(STT):
                                stv = P.v(lambda e, t2=t2: e.tensor_copy(
                                    pvh[:, t2 * 1024:(t2 + 1) * 1024], ps5[:, t2 * 1024:(t2 + 1) * 1024]),
                                    waits=[(pe, pv_pe), (ve, ve.n)], inc=ve)
                            prev_ep5 = stv
                        else:
                            def load_xld2(gt):
                                par2 = gt % 2
                                w = [(ve, x2_free.get(par2, 0))]
                                if gt < 2:
                                    w.append((sto, stash_total))
                                P.sy(lambda e, gt=gt, par2=par2: e.dma_start(
                                    out=xld2[:, par2 * 1024:(par2 + 1) * 1024],
                                    in_=xstash[128 * gt:128 * (gt + 1), :]),
                                    waits=w, inc=wlx2[par2])
                                x2_ld[gt] = (wlx2[par2], wlx2[par2].n)

                            for t2 in range(min(2, STT)):
                                load_xld2(sidx * STT + t2)
                            ep_last = 0
                            for t2 in range(STT):
                                gt = sidx * STT + t2
                                c0 = FH + t2
                                c1 = FH + 16 + t2
                                c2 = FH + 32 + t2
                                c3 = FH + 48 + t2
                                v1 = P.v(lambda e, c0=c0, c1=c1: e.tensor_max(
                                    stat[:, SMX:SMX + 1], stat[:, c0:c0 + 1], stat[:, c1:c1 + 1]),
                                    waits=[(ve, ve.n)], inc=ve)
                                v2 = P.v(lambda e, c0=c0: e.tensor_sub(
                                    stat[:, SMX + 1:SMX + 2], stat[:, c0:c0 + 1], stat[:, SMX:SMX + 1]),
                                    waits=[(ve, v1)], inc=ve)
                                v3 = P.v(lambda e, c1=c1: e.tensor_sub(
                                    stat[:, SMX + 2:SMX + 3], stat[:, c1:c1 + 1], stat[:, SMX:SMX + 1]),
                                    waits=[(ve, v2)], inc=ve)
                                a1_ = P.s(lambda e: e.activation(
                                    stat[:, SMX + 1:SMX + 3], stat[:, SMX + 1:SMX + 3],
                                    AF.Exp, scale=1.0 / 32.0),
                                    waits=[(ve, v3), (ac, ac.n)], inc=ac)
                                v4 = P.v(lambda e, c2=c2, c3=c3: e.tensor_mul(
                                    stat[:, SMX + 4:SMX + 5], stat[:, c2:c2 + 1], stat[:, SMX + 1:SMX + 2]),
                                    waits=[(ac, a1_), (ve, ve.n)], inc=ve)
                                v5 = P.v(lambda e, c3=c3: e.scalar_tensor_tensor(
                                    stat[:, SMX + 4:SMX + 5], stat[:, c3:c3 + 1], 1.0,
                                    stat[:, SMX + 4:SMX + 5], ALU.bypass, ALU.mult)
                                    if False else e.tensor_mul(
                                    stat[:, SMX + 5:SMX + 6], stat[:, c3:c3 + 1], stat[:, SMX + 2:SMX + 3]),
                                    waits=[(ve, v4)], inc=ve)
                                v6 = P.v(lambda e: e.tensor_add(
                                    stat[:, SMX + 4:SMX + 5], stat[:, SMX + 4:SMX + 5],
                                    stat[:, SMX + 5:SMX + 6]),
                                    waits=[(ve, v5)], inc=ve)
                                v7 = P.v(lambda e: e.reciprocal(stat[:, SMX + 6:SMX + 7],
                                                                stat[:, SMX + 4:SMX + 5]),
                                         waits=[(ve, v6)], inc=ve)
                                v8 = P.v(lambda e: e.tensor_mul(stat[:, SMX + 1:SMX + 2],
                                                                stat[:, SMX + 1:SMX + 2],
                                                                stat[:, SMX + 6:SMX + 7]),
                                         waits=[(ve, v7)], inc=ve)
                                v9 = P.v(lambda e: e.tensor_mul(stat[:, SMX + 2:SMX + 3],
                                                                stat[:, SMX + 2:SMX + 3],
                                                                stat[:, SMX + 6:SMX + 7]),
                                         waits=[(ve, v8)], inc=ve)
                                y1 = P.v(lambda e, t2=t2, gt=gt: e.scalar_tensor_tensor(
                                    ysc[:, :], pvh[:, t2 * 1024:(t2 + 1) * 1024],
                                    stat[:, SMX + 1:SMX + 2], xb[:, gt * 1024:(gt + 1) * 1024],
                                    ALU.mult, ALU.add),
                                    waits=[(ve, v9), (ve, ep_last)], inc=ve)
                                y2 = P.v(lambda e, t2=t2: e.scalar_tensor_tensor(
                                    ysc[:, :], ps5[:, t2 * 1024:(t2 + 1) * 1024],
                                    stat[:, SMX + 2:SMX + 3], ysc[:, :],
                                    ALU.mult, ALU.add),
                                    waits=[(ve, y1), (pe, pv_pe)], inc=ve)
                                vr, ar = ln_small(ysc[:, :], SMX + 8, [(ve, y2)])
                                v10 = P.v(lambda e, gt=gt, par2=gt % 2: e.tensor_scalar(
                                    xld2[:, par2 * 1024:(par2 + 1) * 1024],
                                    xld2[:, par2 * 1024:(par2 + 1) * 1024],
                                    stat[:, SMX + 8 + 4:SMX + 8 + 5], None, ALU.add),
                                    waits=[(ve, vr), x2_ld[gt]], inc=ve)
                                ep_last = P.v(lambda e, gt=gt, par2=gt % 2: e.scalar_tensor_tensor(
                                    xb[:, gt * 1024:(gt + 1) * 1024], ysc[:, :],
                                    stat[:, SMX + 8 + 3:SMX + 8 + 4],
                                    xld2[:, par2 * 1024:(par2 + 1) * 1024],
                                    ALU.mult, ALU.add),
                                    waits=[(ve, v10)], inc=ve)
                                x2_free[gt % 2] = ep_last
                                if t2 + 2 < STT:
                                    load_xld2(gt + 2)
                            prev_ep5 = ep_last
                xb_ve2 = ve.n
            P.barrier()


        if "l1f" in blocks:
            P.mark("l1f")
            es_g = contextlib.ExitStack()
            with es_g:
                xhs2 = es_g.enter_context(nc.sbuf_tensor("xhs2", [128, 2 * KC * ST], BF16))
                w3sb = es_g.enter_context(nc.sbuf_tensor("w3sb", [128, C], F32))
                P.sy(lambda e: e.dma_start(out=w3sb[:, :], in_=w3sb_in[:, :]), inc=P.sems["ld"])
                w3sb_ld = P.sems["ld"].n
                Ab2 = es_g.enter_context(nc.sbuf_tensor("Ab2", [128, FC * ST], BF16))
                Bb2 = es_g.enter_context(nc.sbuf_tensor("Bb2", [128, FC * ST], BF16))
                wfc2 = es_g.enter_context(nc.sbuf_tensor("wfc2", [128, 2 * 1024], BF16))
                ps_t3 = es_g.enter_context(nc.psum_tensor("ps_t3", [128, 1024], F32))
                ps_m2 = es_g.enter_context(nc.psum_tensor("ps_m2", [128, 2048], F32))
                ps_o2 = es_g.enter_context(nc.psum_tensor("ps_o2", [128, 1024], F32))
                gl1 = (P.w[6], P.w[7])
                gl2 = (P.sems["wa"], P.sems["wb"])
                gl3 = (P.w[0], P.w[1])
                m1_free = {}
                m2_free = {}
                m3_free = {}
                m1_ld = {}
                m2_ld = {}
                m3_ld = {}

                def l1f_mm1_slab(gi, j, srcw):
                    par = gi % 2
                    w = [(pe, m1_free.get(par, 0))]
                    if gi < 2 and hasattr(es, "h12_tile_done"):
                        w += [(P.w[0], es.h12_tile_done[0]), (P.w[1], es.h12_tile_done[1])]
                    for kc in range(KC):
                        P.sy(lambda e, j=j, kc=kc, par=par, srcw=srcw: e.dma_start(
                            out=wq[:, par * 4096 + kc * 512: par * 4096 + (kc + 1) * 512],
                            in_=srcw[128 * kc:128 * (kc + 1), 512 * j:512 * (j + 1)]),
                            waits=(w if kc == 0 else []), inc=gl1[par])
                    m1_ld[gi] = (gl1[par], gl1[par].n)
                    return par

                def l1f_mm2_slab(gi, mc, srcw):
                    par = gi % 2
                    w = [(pe, m2_free.get(par, 0))]
                    if gi < 2 and hasattr(es, "h12_tile_done"):
                        w += [(P.w[0], es.h12_tile_done[0]), (P.w[1], es.h12_tile_done[1])]
                    P.sy(lambda e, mc=mc, par=par, srcw=srcw: e.dma_start(
                        out=wq[:, 8192 + par * 4096: 8192 + (par + 1) * 4096].rearrange(
                            "p (k c) -> p k c", c=128),
                        in_=srcw[mc * F: (mc + 1) * F, :].rearrange("(k p) c -> p k c", p=128)),
                        waits=w, inc=gl2[par])
                    m2_ld[gi] = (gl2[par], gl2[par].n)
                    return par

                def l1f_mm3_slab(gi, kc2):
                    par = gi % 2
                    w = [(pe, m3_free.get(par, 0))]
                    if gi < 2 and hasattr(es, "h12_tile_done"):
                        w += [(P.w[0], es.h12_tile_done[0]), (P.w[1], es.h12_tile_done[1])]
                    P.gp(lambda e, kc2=kc2, par=par: e.dma_start(
                        out=wfc2[:, par * 1024:(par + 1) * 1024],
                        in_=w3b[128 * kc2:128 * (kc2 + 1), :]),
                        waits=w, inc=gl3[par])
                    m3_ld[gi] = (gl3[par], gl3[par].n)
                    return par

                g1 = g2 = g3 = 0
                prev_eva = 0

                def l1f_ln(sidx):
                    par = sidx % 2
                    wv = wa = 0
                    for t2 in range(STT):
                        gt = sidx * STT + t2
                        wv, wa = ln_tile_stats(xb[:, gt * 1024:(gt + 1) * 1024], t2)
                    ln_finalize(STT, wv, wa)
                    mbv, rca = ve.n, ac.n
                    tpa = {}
                    for t2 in range(STT):
                        half = t2 % 2
                        gt = sidx * STT + t2
                        w = [(ve, mbv), (ac, rca), (ac, ac.n)]
                        if t2 >= 2:
                            w.append((pe, tpa[t2 - 2]))
                        aa = P.s(lambda e, gt=gt, half=half, t2=t2: e.activation(
                            nrm[:, half * 1024:(half + 1) * 1024], xb[:, gt * 1024:(gt + 1) * 1024],
                            AF.Identity, scale=stat[:, RC + t2:RC + t2 + 1],
                            bias=stat[:, MB + t2:MB + t2 + 1]),
                            waits=w, inc=ac)

                        def evacx2(g, psap, ws, t2=t2, par=par):
                            dap = xhs2[:, par * KC * ST:(par + 1) * KC * ST].rearrange(
                                "p (c t3) -> p c t3", t3=ST)[
                                :, 4 * (g % 2):4 * (g % 2) + 4, t2 * 128:(t2 + 1) * 128]
                            return (ve, P.v(lambda e, psap=psap, dap=dap: e.tensor_copy(dap, psap),
                                            waits=ws, inc=ve))

                        tpa[t2] = transpose_run(
                            [nrm[:, half * 1024 + c * 128: half * 1024 + (c + 1) * 128] for c in range(KC)],
                            ps_t3, 2, evacx2, extra_first=[(ac, aa)])
                    return ve.n

                xh2_ve_n = {0: l1f_ln(0)}
                prev_l1f_mm2b = 0
                for sidx in range(NS):
                    par0 = sidx % 2
                    xhs_ve = xh2_ve_n[sidx]

                    def xs2_ap(kc, par0=par0):
                        return xhs2[:, par0 * KC * ST + kc * ST: par0 * KC * ST + (kc + 1) * ST]
                    ev1 = {}
                    for j in range(8):
                        par = l1f_mm1_slab(g1, j, w1b)
                        for m4 in range(4):
                            mc = j * 4 + m4
                            bank = mc % 4
                            for kc in range(KC):
                                w = []
                                if kc == 0:
                                    w = [m1_ld[g1], (ve, xhs_ve), (ac, prev_l1f_mm2b), (ve, prev_eva)]
                                    if bank in ev1:
                                        w.append((ac, ev1[bank]))
                                P.t(lambda e, par=par, kc=kc, m4=m4, bank=bank, xs2_ap=xs2_ap: e.matmul(
                                    ps_m2[:, bank * 512: bank * 512 + ST],
                                    wq[:, par * 4096 + kc * 512 + m4 * 128: par * 4096 + kc * 512 + (m4 + 1) * 128],
                                    xs2_ap(kc),
                                    start=(kc == 0), stop=(kc == KC - 1)),
                                    waits=w, inc=(pe if kc == KC - 1 else None))
                            if m4 == 3:
                                m1_free[par] = pe.n
                            ev1[bank] = P.s(lambda e, mc=mc, bank=bank: e.activation(
                                Ab2[:, mc * ST:(mc + 1) * ST], ps_m2[:, bank * 512: bank * 512 + ST],
                                AF.Silu, scale=scv[:, mc:mc + 1]),
                                waits=[(pe, pe.n), (ac, ac.n), (ld, init_ld)], inc=ac)
                        g1 += 1
                    mm1a_ac = ac.n
                    ev1b = {}
                    for j in range(8):
                        par = l1f_mm1_slab(g1, j, w2b)
                        for m4 in range(4):
                            mc = j * 4 + m4
                            bank = mc % 4
                            for kc in range(KC):
                                w = []
                                if kc == 0:
                                    w = [m1_ld[g1], (ve, xhs_ve), (ac, mm1a_ac)]
                                    if bank in ev1b:
                                        w.append((ve, ev1b[bank]))
                                P.t(lambda e, par=par, kc=kc, m4=m4, bank=bank, xs2_ap=xs2_ap: e.matmul(
                                    ps_m2[:, bank * 512: bank * 512 + ST],
                                    wq[:, par * 4096 + kc * 512 + m4 * 128: par * 4096 + kc * 512 + (m4 + 1) * 128],
                                    xs2_ap(kc),
                                    start=(kc == 0), stop=(kc == KC - 1)),
                                    waits=w, inc=(pe if kc == KC - 1 else None))
                            if m4 == 3:
                                m1_free[par] = pe.n
                            ev1b[bank] = P.v(lambda e, mc=mc, bank=bank: e.scalar_tensor_tensor(
                                Bb2[:, mc * ST:(mc + 1) * ST], ps_m2[:, bank * 512: bank * 512 + ST],
                                scv[:, 32 + mc:33 + mc], Ab2[:, mc * ST:(mc + 1) * ST],
                                ALU.mult, ALU.mult),
                                waits=[(pe, pe.n), (ac, mm1a_ac), (ve, ve.n)], inc=ve)
                        g1 += 1
                    mm1b_ve = ve.n
                    if sidx + 1 < NS:
                        xh2_ve_n[sidx + 1] = l1f_ln(sidx + 1)
                    ev2 = {}
                    for mc in range(FC):
                        par = l1f_mm2_slab(g2, mc, h1s_t)
                        bank = mc % 4
                        for kc2 in range(FC):
                            w = []
                            if kc2 == 0:
                                w = [m2_ld[g2], (ve, mm1b_ve)]
                                if bank in ev2:
                                    w.append((ac, ev2[bank]))
                            P.t(lambda e, par=par, kc2=kc2, bank=bank: e.matmul(
                                ps_m2[:, bank * 512: bank * 512 + ST],
                                wq[:, 8192 + par * 4096 + kc2 * 128: 8192 + par * 4096 + (kc2 + 1) * 128],
                                Bb2[:, kc2 * ST: (kc2 + 1) * ST],
                                start=(kc2 == 0), stop=(kc2 == FC - 1)),
                                waits=w, inc=(pe if kc2 == FC - 1 else None))
                        m2_free[par] = pe.n
                        ev2[bank] = P.s(lambda e, mc=mc, bank=bank: e.activation(
                            Ab2[:, mc * ST:(mc + 1) * ST], ps_m2[:, bank * 512: bank * 512 + ST],
                            AF.Gelu, scale=scv[:, 64 + mc:65 + mc]),
                            waits=[(pe, pe.n), (ac, ac.n), (ve, mm1b_ve)], inc=ac)
                        g2 += 1
                    mm2a_ac = ac.n
                    ev2b = {}
                    for mc in range(FC):
                        par = l1f_mm2_slab(g2, mc, h2s_t)
                        bank = mc % 4
                        for kc2 in range(FC):
                            w = []
                            if kc2 == 0:
                                w = [m2_ld[g2], (ac, mm2a_ac)]
                                if bank in ev2b:
                                    w.append((ac, ev2b[bank]))
                            P.t(lambda e, par=par, kc2=kc2, bank=bank: e.matmul(
                                ps_m2[:, bank * 512: bank * 512 + ST],
                                wq[:, 8192 + par * 4096 + kc2 * 128: 8192 + par * 4096 + (kc2 + 1) * 128],
                                Ab2[:, kc2 * ST: (kc2 + 1) * ST],
                                start=(kc2 == 0), stop=(kc2 == FC - 1)),
                                waits=w, inc=(pe if kc2 == FC - 1 else None))
                        m2_free[par] = pe.n
                        ev2b[bank] = P.s(lambda e, mc=mc, bank=bank: e.activation(
                            Bb2[:, mc * ST:(mc + 1) * ST], ps_m2[:, bank * 512: bank * 512 + ST],
                            AF.Gelu, scale=scv[:, 96 + mc:97 + mc]),
                            waits=[(pe, pe.n), (ac, ac.n)], inc=ac)
                        g2 += 1
                    mm2b_ac = ac.n
                    prev_l1f_mm2b = mm2b_ac
                    for hf in range(STT // 2):
                        for kc2 in range(FC):
                            par = l1f_mm3_slab(g3, kc2)
                            for ti in range(2):
                                t2 = hf * 2 + ti
                                for n2 in range(2):
                                    w = []
                                    if kc2 == 0 and ti == 0 and n2 == 0:
                                        w = [(ac, mm2b_ac), (ve, prev_eva)]
                                    if ti == 0 and n2 == 0:
                                        w = w + [m3_ld[g3]]
                                    P.t(lambda e, par=par, kc2=kc2, t2=t2, ti=ti, n2=n2: e.matmul(
                                        ps_m2[:, ti * 1024 + n2 * 512: ti * 1024 + (n2 + 1) * 512],
                                        Bb2[:, kc2 * ST + t2 * 128: kc2 * ST + (t2 + 1) * 128],
                                        wfc2[:, par * 1024 + n2 * 512: par * 1024 + (n2 + 1) * 512],
                                        start=(kc2 == 0), stop=(kc2 == FC - 1)),
                                        waits=w,
                                        inc=(pe if (ti == 1 and n2 == 1) else None))
                            m3_free[par] = pe.n
                            g3 += 1
                        mmp = pe.n
                        for ti in range(2):
                            t2 = hf * 2 + ti
                            gt = sidx * STT + t2
                            ya = P.v(lambda e, ti=ti: e.tensor_mul(
                                ysc[:, :], ps_m2[:, ti * 1024:(ti + 1) * 1024], w3sb[:, :]),
                                waits=[(pe, mmp), (P.sems["ld"], w3sb_ld), (ve, prev_eva), (ve, ve.n)], inc=ve)
                            vr, ar = ln_small(ysc[:, :], SMX, [(ve, ya)])
                            v6 = P.v(lambda e, gt=gt: e.tensor_scalar(
                                xb[:, gt * 1024:(gt + 1) * 1024], xb[:, gt * 1024:(gt + 1) * 1024],
                                stat[:, SMX + 4:SMX + 5], None, ALU.add),
                                waits=[(ve, vr)], inc=ve)
                            prev_eva = P.v(lambda e, gt=gt: e.scalar_tensor_tensor(
                                xb[:, gt * 1024:(gt + 1) * 1024], ysc[:, :], stat[:, SMX + 3:SMX + 4],
                                xb[:, gt * 1024:(gt + 1) * 1024], ALU.mult, ALU.add),
                                waits=[(ve, v6)], inc=ve)
            P.barrier()

        def store_residual(buf):
            for tt in range(NT):
                P.sy(lambda e, tt=tt: e.dma_start(out=yout[128 * tt:128 * (tt + 1), :],
                                                  in_=buf[:, tt * 1024:(tt + 1) * 1024]),
                     waits=[(P.ve, P.ve.n), (P.ac, P.ac.n), (P.pe, P.pe.n)], inc=sto)
            P.sy(None, waits=[(sto, sto.n)])

        store_residual(xb if "l1a" in blocks else xr)

        semh = {nm: es.enter_context(nc.semaphore(nm)) for nm in P.sems}
        P.replay(nc, semh)
    build_nc.last_marks = P.marks
    return nc



def prep_weights(i):
    C, F = 1024, 4096
    d = {}
    d["wqkvT"] = np.ascontiguousarray(np.asarray(i["L0_attn_W_qkv"]).T)
    d["wprojT"] = np.ascontiguousarray(np.asarray(i["L0_attn_W_proj"]).T)
    d["bpj_d"] = np.ascontiguousarray(np.broadcast_to(np.asarray(i["L0_attn_b_proj"])[None, :], (128, C)))
    d["k0"] = np.ascontiguousarray(np.asarray(i["L0_ffn_k0"]).view(np.uint8))
    d["k1"] = np.ascontiguousarray(np.asarray(i["L0_ffn_k1"]).view(np.uint8))
    d["k2"] = np.ascontiguousarray(np.asarray(i["L0_ffn_k2"]).view(np.uint8))
    sm = np.zeros((128, 8), np.float32)
    sm[:, 0] = float(i["L0_ffn_s0"])
    sm[:, 1] = float(i["L0_ffn_s1"])
    sm[:, 2] = float(i["L0_ffn_s2"])
    d["smallv_d"] = sm
    for w, (kk, vv) in {"q": ("L1_attn_qK", "L1_attn_qV"), "k": ("L1_attn_kK", "L1_attn_kV"),
                        "v": ("L1_attn_vK", "L1_attn_vV"), "p": ("L1_attn_pK", "L1_attn_pV")}.items():
        d[f"{w}KT"] = np.ascontiguousarray(np.asarray(i[kk]).T)
        d[f"{w}V"] = np.ascontiguousarray(np.asarray(i[vv]))
    d["fKT"] = np.ascontiguousarray(np.asarray(i["L1_attn_fK"]).T)
    d["fV"] = np.ascontiguousarray(np.asarray(i["L1_attn_fV"]))
    d["w1T"] = np.ascontiguousarray(np.asarray(i["L1_ffn_w1_w"]).T)
    d["w2T"] = np.ascontiguousarray(np.asarray(i["L1_ffn_w2_w"]).T)
    d["h1T"] = np.ascontiguousarray(np.asarray(i["L1_ffn_h1_w"]).T)
    d["h2T"] = np.ascontiguousarray(np.asarray(i["L1_ffn_h2_w"]).T)
    d["w3T"] = np.ascontiguousarray(np.asarray(i["L1_ffn_w3_w"]).T)
    sc = np.zeros((128, 128), np.float32)
    sc[:, 0:32] = np.asarray(i["L1_ffn_w1_s"]).reshape(32, 128).T
    sc[:, 32:64] = np.asarray(i["L1_ffn_w2_s"]).reshape(32, 128).T
    sc[:, 64:96] = np.asarray(i["L1_ffn_h1_s"]).reshape(32, 128).T
    sc[:, 96:128] = np.asarray(i["L1_ffn_h2_s"]).reshape(32, 128).T
    d["sc32_d"] = sc
    d["w3sb_d"] = np.ascontiguousarray(np.broadcast_to(np.asarray(i["L1_ffn_w3_s"])[None, :], (128, C)))
    d["idn_d"] = np.eye(128, dtype=np.float32)
    bd = np.zeros((128, 128), np.float32)
    for g in range(4):
        bd[32 * g:32 * (g + 1), 32 * g:32 * (g + 1)] = 1.0
    d["bdiag_d"] = bd
    s4 = np.zeros((4, 128), np.float32)
    for g in range(4):
        s4[g, 32 * g:32 * (g + 1)] = 1.0
    d["sel4_d"] = s4
    cm = np.zeros((128, 128), np.float32)
    cm[np.triu_indices(128, 1)] = NEG
    d["cmask_d"] = cm
    nh = np.zeros((32, 128), np.float32)
    nh[0, :] = -0.5
    d["nh_d"] = nh
    return {k: np.ascontiguousarray(np.asarray(v)) for k, v in d.items()}


def kernel(**inputs):
    inputs = {k: np.asarray(v) for k, v in inputs.items()}
    x = inputs["L0_attn_x"]
    B, T, C = x.shape
    shared = prep_weights(inputs)
    nc = build_nc(T=T)
    in_maps = []
    for b in range(B):
        m = dict(shared)
        m["x"] = np.ascontiguousarray(x[b])
        in_maps.append(m)
    res = run_bass_kernel_spmd(nc, in_maps, core_ids=list(range(8)))
    out = np.stack([np.asarray(r["y"]) for r in res.results], axis=0)
    return out.astype(np.float32)


# version 22
# speedup vs baseline: 1.1049x; 1.0822x over previous
import contextlib
import numpy as np
import concourse.bass as bass
import concourse.mybir as mybir
from concourse.bass_utils import run_bass_kernel_spmd

F32 = mybir.dt.float32
BF16 = mybir.dt.bfloat16
U8 = mybir.dt.uint8
AF = mybir.ActivationFunctionType
ALU = mybir.AluOpType
AX = mybir.AxisListType

ENGINES = ("sync", "gpsimd", "tensor", "vector", "scalar")

NEG = -1.0e30
LN_EPS = 1e-5
L0A_EPS = 1e-15


class S:
    def __init__(self, name):
        self.name = name
        self.n = 0

    def bump(self, a=1):
        self.n += a
        return self.n


class Prog:
    def __init__(self):
        self.marks = []
        self.nmm = 0
        self.ops = {e: [] for e in ENGINES}
        self.lastwait = {e: {} for e in ENGINES}
        self.sems = {}
        for nm in ("ld", "cst", "pe", "ve", "ac", "bar", "sto", "po", "wa", "wb",
                   "wc", "wd", "w0", "w1", "w2", "w3", "w4", "w5", "w6", "w7"):
            self.sems[nm] = S(nm)
        self.ld = self.sems["ld"]
        self.cst = self.sems["cst"]
        self.pe = self.sems["pe"]
        self.ve = self.sems["ve"]
        self.ac = self.sems["ac"]
        self.bar = self.sems["bar"]
        self.sto = self.sems["sto"]
        self.w = [self.sems[f"w{i}"] for i in range(8)]
        self.po = self.sems["po"]

    def emit(self, eng, fn=None, waits=(), inc=None, amt=1):
        lw = self.lastwait[eng]
        w2 = []
        for (s, v) in waits:
            if v is None or v <= 0:
                continue
            if lw.get(s.name, -1) >= v:
                continue
            lw[s.name] = v
            w2.append((s, v))
        self.ops[eng].append((fn, tuple(w2), inc, amt))
        return inc.bump(amt) if inc is not None else None

    def sy(self, fn=None, waits=(), inc=None, amt=16):
        return self.emit("sync", fn, waits, inc, amt)

    def gp(self, fn=None, waits=(), inc=None, amt=16):
        return self.emit("gpsimd", fn, waits, inc, amt)

    def t(self, fn=None, waits=(), inc=None, amt=1):
        if fn is not None:
            self.nmm += 1
        return self.emit("tensor", fn, waits, inc, amt)

    def mark(self, name):
        self.marks.append((name, self.nmm))

    def v(self, fn=None, waits=(), inc=None, amt=1):
        return self.emit("vector", fn, waits, inc, amt)

    def s(self, fn=None, waits=(), inc=None, amt=1):
        return self.emit("scalar", fn, waits, inc, amt)

    def barrier(self):
        target = self.bar.n + 5
        for e in ENGINES:
            self.emit(e, None, inc=self.bar, amt=1)
        for e in ENGINES:
            self.emit(e, None, waits=[(self.bar, target)])
        return target

    def replay(self, nc, semh):
        with nc.Block() as block:
            for ename in ENGINES:
                ops = self.ops[ename]

                def runner(eng, ops=ops):
                    for fn, waits, inc, amt in ops:
                        for (s_, v_) in waits:
                            eng.wait_ge(semh[s_.name], v_)
                        ins = fn(eng) if fn is not None else None
                        if inc is not None:
                            if ins is None:
                                ins = eng.nop(nofuse=True)
                            ins.then_inc(semh[inc.name], amt)

                getattr(block, ename)(runner)


def build_nc(T=2048, blocks=("l0a", "l0f", "l1a", "l1f")):
    NT = T // 128
    ST = min(512, T)
    NS = T // ST
    STT = ST // 128
    C = 1024
    KC = 8
    F = 4096
    FC = 32

    nc = bass.Bass("TRN2", target_bir_lowering=False)

    xin = nc.dram_tensor("x", [T, C], F32, kind="ExternalInput").ap()
    wqkvT = nc.dram_tensor("wqkvT", [C, 3 * C], F32, kind="ExternalInput").ap()
    wprojT = nc.dram_tensor("wprojT", [C, C], F32, kind="ExternalInput").ap()
    bpj_in = nc.dram_tensor("bpj_d", [128, C], F32, kind="ExternalInput").ap()
    k0_in = nc.dram_tensor("k0", [2 * C, F], U8, kind="ExternalInput").ap()
    k1_in = nc.dram_tensor("k1", [F, F], U8, kind="ExternalInput").ap()
    k2_in = nc.dram_tensor("k2", [F, C], U8, kind="ExternalInput").ap()
    smallv = nc.dram_tensor("smallv_d", [128, 8], F32, kind="ExternalInput").ap()
    kTm = {w: nc.dram_tensor(f"{w}KT", [C, C], F32, kind="ExternalInput").ap() for w in ("q", "k", "v", "p")}
    vVm = {w: nc.dram_tensor(f"{w}V", [C, C], F32, kind="ExternalInput").ap() for w in ("q", "k", "v", "p")}
    fKT = nc.dram_tensor("fKT", [C, 4 * C], F32, kind="ExternalInput").ap()
    fV = nc.dram_tensor("fV", [4 * C, C], F32, kind="ExternalInput").ap()
    w1T = nc.dram_tensor("w1T", [C, F], F32, kind="ExternalInput").ap()
    w2T = nc.dram_tensor("w2T", [C, F], F32, kind="ExternalInput").ap()
    h1T = nc.dram_tensor("h1T", [F, F], F32, kind="ExternalInput").ap()
    h2T = nc.dram_tensor("h2T", [F, F], F32, kind="ExternalInput").ap()
    w3T = nc.dram_tensor("w3T", [F, C], F32, kind="ExternalInput").ap()
    sc32 = nc.dram_tensor("sc32_d", [128, 128], F32, kind="ExternalInput").ap()
    w3sb_in = nc.dram_tensor("w3sb_d", [128, C], F32, kind="ExternalInput").ap()
    idn_in = nc.dram_tensor("idn_d", [128, 128], F32, kind="ExternalInput").ap()
    bdiag_in = nc.dram_tensor("bdiag_d", [128, 128], F32, kind="ExternalInput").ap()
    sel4_in = nc.dram_tensor("sel4_d", [4, 128], F32, kind="ExternalInput").ap()
    cmask_in = nc.dram_tensor("cmask_d", [128, 128], F32, kind="ExternalInput").ap()
    nh_in = nc.dram_tensor("nh_d", [32, 128], F32, kind="ExternalInput").ap()

    yout = nc.dram_tensor("y", [T, C], F32, kind="ExternalOutput").ap()

    w0s = nc.dram_tensor("w0s", [C, F], BF16, kind="Internal").ap()
    w1s_d = nc.dram_tensor("w1s_d", [32 * F, 128], U8, kind="Internal").ap()
    fKb = nc.dram_tensor("fKb", [C, 4 * C], BF16, kind="Internal").ap()
    fVb = nc.dram_tensor("fVb", [4 * C, C], BF16, kind="Internal").ap()
    kTb = {w: nc.dram_tensor(f"{w}KTb", [C, C], BF16, kind="Internal").ap() for w in ("q", "k", "v", "p")}
    vVb = {w: nc.dram_tensor(f"{w}Vb", [C, C], BF16, kind="Internal").ap() for w in ("q", "k", "v", "p")}
    h1s_t = nc.dram_tensor("h1s_t", [32 * F, 128], BF16, kind="Internal").ap()
    h2s_t = nc.dram_tensor("h2s_t", [32 * F, 128], BF16, kind="Internal").ap()
    w1b = nc.dram_tensor("w1b", [C, F], BF16, kind="Internal").ap()
    w2b = nc.dram_tensor("w2b", [C, F], BF16, kind="Internal").ap()
    w3b = nc.dram_tensor("w3b", [F, C], BF16, kind="Internal").ap()
    xstash = nc.dram_tensor("xstash", [T, C], F32, kind="Internal").ap()

    P = Prog()
    ld, cst, pe, ve, ac, sto = P.ld, P.cst, P.pe, P.ve, P.ac, P.sto

    es = contextlib.ExitStack()
    with es:
        stat = es.enter_context(nc.sbuf_tensor("stat", [128, 512], F32))
        sq = es.enter_context(nc.sbuf_tensor("sq", [128, 1024], F32))
        nrm = es.enter_context(nc.sbuf_tensor("nrm", [128, 2048], BF16))
        ident = es.enter_context(nc.sbuf_tensor("ident", [128, 128], BF16))
        bdiag = es.enter_context(nc.sbuf_tensor("bdiag", [128, 128], F32))
        sel4 = es.enter_context(nc.sbuf_tensor("sel4", [4, 128], BF16))
        cmask = es.enter_context(nc.sbuf_tensor("cmask", [128, 128], F32))
        scv = es.enter_context(nc.sbuf_tensor("scv", [128, 128], F32))
        smv = es.enter_context(nc.sbuf_tensor("smv", [128, 16], F32))
        onec = es.enter_context(nc.sbuf_tensor("onec", [128, 32], BF16))
        nhalf = es.enter_context(nc.sbuf_tensor("nhalf", [32, 128], BF16))
        wq = es.enter_context(nc.sbuf_tensor("wq", [128, 16384], BF16))
        ysc = es.enter_context(nc.sbuf_tensor("ysc", [128, 1024], F32))

        SU, SQ_, MU, VP, RC, MB = 0, 16, 32, 48, 64, 80
        MU1, R1, MB1 = 96, 112, 128
        SMX = 160

        P.gp(lambda e: e.dma_start(out=ident[:, :], in_=idn_in[:, :]), inc=cst)
        P.gp(lambda e: e.dma_start(out=sel4[:, :], in_=sel4_in[:, :]), inc=cst)
        P.gp(lambda e: e.dma_start(out=nhalf[:, :], in_=nh_in[:, :]), inc=cst)
        consts_cst = cst.n
        P.sy(lambda e: e.dma_start(out=bdiag[:, :], in_=bdiag_in[:, :]), inc=ld)
        P.sy(lambda e: e.dma_start(out=cmask[:, :], in_=cmask_in[:, :]), inc=ld)
        P.sy(lambda e: e.dma_start(out=scv[:, :], in_=sc32[:, :]), inc=ld)
        P.sy(lambda e: e.dma_start(out=smv[:, 0:8], in_=smallv[:, :]), inc=ld)
        P.v(lambda e: e.memset(onec[:, :], 1.0), inc=ve)
        es1 = es.enter_context(contextlib.ExitStack())
        xr = es1.enter_context(nc.sbuf_tensor("xr", [128, NT * 1024], F32, side="right"))
        x3 = xin.rearrange("(t p) c -> p t c", p=128)
        P.sy(lambda e: e.dma_start(out=xr[:, :].rearrange("p (t c) -> p t c", c=1024), in_=x3),
             inc=ld)
        x_ld = init_ld = ld.n
        P.v(lambda e: e.tensor_scalar(smv[:, 4:5], smv[:, 0:1], 4.0, None, ALU.mult),
            waits=[(ld, init_ld)], inc=ve)
        P.v(lambda e: e.tensor_scalar(smv[:, 5:6], smv[:, 1:2], 2.0, None, ALU.mult), inc=ve)
        P.v(lambda e: e.tensor_scalar(smv[:, 7:8], smv[:, 1:2], 4.0, None, ALU.mult), inc=ve)
        P.v(lambda e: e.tensor_scalar(smv[:, 8:9], smv[:, 2:3], 2.0, None, ALU.mult), inc=ve)
        consts_ve = ve.n

        def ln_tile_stats(src_ap, col):
            vv = P.v(lambda e, a=src_ap, c=col: e.reduce_sum(stat[:, SU + c:SU + c + 1], a, axis=AX.X),
                     waits=[(ve, ve.n)], inc=ve)
            aa = P.s(lambda e, a=src_ap, c=col: e.activation(sq[:, :], a, AF.Square,
                                                             accum_out=stat[:, SQ_ + c:SQ_ + c + 1]),
                     waits=[(ac, ac.n)], inc=ac)
            return vv, aa

        def ln_finalize(nt, wv, wa, mu_c=MU, r_c=RC, mb_c=MB):
            P.v(lambda e, n=nt: e.tensor_scalar(stat[:, mu_c:mu_c + n], stat[:, SU:SU + n],
                                                1.0 / C, None, ALU.mult),
                waits=[(ve, wv)], inc=ve)
            P.v(lambda e, n=nt: e.tensor_scalar(stat[:, VP:VP + n], stat[:, SQ_:SQ_ + n],
                                                1.0 / C, LN_EPS, ALU.mult, ALU.add),
                waits=[(ac, wa)], inc=ve)
            P.v(lambda e, n=nt: e.tensor_mul(stat[:, SU:SU + n], stat[:, mu_c:mu_c + n],
                                             stat[:, mu_c:mu_c + n]),
                waits=[(ve, ve.n)], inc=ve)
            vsub = P.v(lambda e, n=nt: e.tensor_sub(stat[:, VP:VP + n], stat[:, VP:VP + n],
                                                    stat[:, SU:SU + n]),
                       waits=[(ve, ve.n)], inc=ve)
            a1 = P.s(lambda e, n=nt: e.activation(stat[:, VP:VP + n], stat[:, VP:VP + n], AF.Ln),
                     waits=[(ve, vsub), (ac, ac.n)], inc=ac)
            a2 = P.s(lambda e, n=nt, r_c=r_c: e.activation(stat[:, r_c:r_c + n], stat[:, VP:VP + n],
                                                           AF.Exp, scale=-0.5),
                     waits=[(ac, a1)], inc=ac)
            vmb = P.v(lambda e, n=nt, mu_c=mu_c, r_c=r_c, mb_c=mb_c: e.scalar_tensor_tensor(
                stat[:, mb_c:mb_c + n], stat[:, mu_c:mu_c + n], -1.0, stat[:, r_c:r_c + n],
                ALU.mult, ALU.mult),
                waits=[(ac, a2), (ve, ve.n)], inc=ve)
            return vmb, a2

        def ln_small(src_ap, base, w_extra=()):
            sv = P.v(lambda e, a=src_ap: e.reduce_sum(stat[:, base:base + 1], a, axis=AX.X),
                     waits=[(ve, ve.n)] + list(w_extra), inc=ve)
            sa = P.s(lambda e, a=src_ap: e.activation(sq[:, :], a, AF.Square,
                                                      accum_out=stat[:, base + 1:base + 2]),
                     waits=[(ac, ac.n)] + list(w_extra), inc=ac)
            v1 = P.v(lambda e: e.tensor_scalar(stat[:, base:base + 1], stat[:, base:base + 1],
                                               1.0 / C, None, ALU.mult),
                     waits=[(ve, sv)], inc=ve)
            v2 = P.v(lambda e: e.tensor_scalar(stat[:, base + 1:base + 2], stat[:, base + 1:base + 2],
                                               1.0 / C, LN_EPS, ALU.mult, ALU.add),
                     waits=[(ac, sa), (ve, v1)], inc=ve)
            v3 = P.v(lambda e: e.scalar_tensor_tensor(stat[:, base + 2:base + 3], stat[:, base:base + 1],
                                                      -1.0, stat[:, base:base + 1], ALU.mult, ALU.mult),
                     waits=[(ve, v2)], inc=ve)
            v4 = P.v(lambda e: e.tensor_add(stat[:, base + 1:base + 2], stat[:, base + 1:base + 2],
                                            stat[:, base + 2:base + 3]),
                     waits=[(ve, v3)], inc=ve)
            a1 = P.s(lambda e: e.activation(stat[:, base + 1:base + 2], stat[:, base + 1:base + 2], AF.Ln),
                     waits=[(ve, v4), (ac, ac.n)], inc=ac)
            a2 = P.s(lambda e: e.activation(stat[:, base + 3:base + 4], stat[:, base + 1:base + 2],
                                            AF.Exp, scale=-0.5),
                     waits=[(ac, a1)], inc=ac)
            v5 = P.v(lambda e: e.scalar_tensor_tensor(stat[:, base + 4:base + 5], stat[:, base:base + 1],
                                                      -1.0, stat[:, base + 3:base + 4], ALU.mult, ALU.mult),
                     waits=[(ac, a2), (ve, ve.n)], inc=ve)
            return v5, a2

        tp_state = {}

        def transpose_run(tiles, pst, nslots, evac_fn, extra_first=()):
            st = tp_state.setdefault(id(pst), {})
            n = len(tiles)
            ngroups = (n + 3) // 4
            g0 = st.get("g", 0)
            for gg in range(ngroups):
                g = g0 + gg
                slot = g % nslots
                lo, hi = gg * 4, min(n, gg * 4 + 4)
                w = list(extra_first) if gg == 0 else []
                if slot in st:
                    w.append(st[slot])
                for i in range(lo, hi):
                    P.t(lambda e, i=i, lo=lo, slot=slot: e.matmul(
                        pst[:, slot * 512 + (i - lo) * 128: slot * 512 + (i - lo + 1) * 128],
                        tiles[i], ident[:, :], start=True, stop=True),
                        waits=(w if i == lo else []),
                        inc=(pe if i == hi - 1 else None))
                st[slot] = evac_fn(g, pst[:, slot * 512: slot * 512 + (hi - lo) * 128], [(pe, pe.n)])
            st["g"] = g0 + ngroups
            return pe.n

        if "l0a" in blocks:
            es_a = contextlib.ExitStack()
            with es_a:
                xh = es_a.enter_context(nc.sbuf_tensor("xh", [128, KC * T], BF16))
                bpj = es_a.enter_context(nc.sbuf_tensor("bpj", [128, C], F32))
                P.sy(lambda e: e.dma_start(out=bpj[:, :], in_=bpj_in[:, :]), inc=P.sems["wa"])
                bpj_ld = P.sems["wa"].n
                ktm = es_a.enter_context(nc.sbuf_tensor("ktm", [128, STT * 1024], BF16))
                vtm = es_a.enter_context(nc.sbuf_tensor("vtm", [128, STT * 1024], BF16))
                qf = es_a.enter_context(nc.sbuf_tensor("qf", [128, KC * ST], BF16))
                qt = qf
                af = es_a.enter_context(nc.sbuf_tensor("af", [128, KC * ST], BF16))
                kvq = es_a.enter_context(nc.sbuf_tensor("kvq", [128, 2 * ST], BF16))
                s1t = es_a.enter_context(nc.sbuf_tensor("s1t", [128, KC * 128], BF16))
                dstt = es_a.enter_context(nc.sbuf_tensor("dstt", [128, KC * 4], BF16))
                dtmp = es_a.enter_context(nc.sbuf_tensor("dtmp", [4, 512], F32))
                dnvb = es_a.enter_context(nc.sbuf_tensor("dnvb", [4, KC * ST], BF16))
                ps_mm = es_a.enter_context(nc.psum_tensor("ps_mm", [128, 1024], F32))
                ps_t = es_a.enter_context(nc.psum_tensor("ps_t", [128, 1024], F32))
                ps_s1 = es_a.enter_context(nc.psum_tensor("ps_s1", [128, 1536], F32))
                ps_x = es_a.enter_context(nc.psum_tensor("ps_x", [128, 512], F32))

                def s1slice(c, a, b):
                    base = (c // 3) * 512 + (c % 3) * 160
                    return ps_s1[:, base + a: base + b]

                def xh_ap(kc, t0, w):
                    return xh[:, kc * T + t0: kc * T + t0 + w]

                w0 = P.w[0]
                for j in range(4):
                    P.gp(lambda e, j=j: e.dma_start(
                        out=wq[:, j * 4096:(j + 1) * 4096].rearrange("p (k c) -> p k c", c=512),
                        in_=wqkvT[:, 1024 + 512 * j: 1024 + 512 * (j + 1)].rearrange(
                            "(k p) c -> p k c", p=128)),
                        inc=w0)
                kv_total = w0.n
                kv_slab_ld = [kv_total] * 4
                if "l0f" in blocks:
                    for r in range(FC):
                        for gq in range(4):
                            P.sy(lambda e, r=r, gq=gq: e.dma_start(
                                out=w1s_d.rearrange("(m k) c -> k m c", k=F)[
                                    128 * r:128 * (r + 1), 8 * gq:8 * gq + 8, :],
                                in_=k1_in[128 * r:128 * (r + 1), :].rearrange(
                                    "p (m c) -> p m c", c=128)[:, 8 * gq:8 * gq + 8, :]),
                                inc=sto)
                k1t_total = sto.n
                k0_ld = (P.w[2], P.w[3])
                k0_st = (P.w[6], P.w[7])
                k0_cv = {}
                k0_sts = {}
                k0_i = [0]

                K0SW = min(2048, KC * ST // 2)
                K0N = (F // K0SW) * 8
                def emit_k0_chunk():
                    i = k0_i[0]
                    if i >= K0N or "l0f" not in blocks:
                        return
                    k0_i[0] += 1
                    nsub = F // K0SW
                    r, wdx = i // nsub, i % nsub
                    p = i % 2
                    aap = qf[:, p * K0SW:(p + 1) * K0SW]
                    bap = af[:, p * K0SW:(p + 1) * K0SW]
                    w = [(k0_st[p], k0_sts.get(p, 0))]
                    P.gp(lambda e, r=r, wdx=wdx, aap=aap: e.dma_start(
                        out=aap, in_=k0_in[128 * r:128 * (r + 1), K0SW * wdx:K0SW * (wdx + 1)]),
                        waits=w, inc=k0_ld[p])
                    P.gp(lambda e, r=r, wdx=wdx, bap=bap: e.dma_start(
                        out=bap, in_=k0_in[1024 + 128 * r:1024 + 128 * (r + 1), K0SW * wdx:K0SW * (wdx + 1)]),
                        inc=k0_ld[p])
                    k0_cv[p] = P.v(lambda e, aap=aap, bap=bap: e.tensor_sub(aap, aap, bap),
                                   waits=[(k0_ld[p], k0_ld[p].n), (ve, ve.n)], inc=ve)
                    P.sy(lambda e, r=r, wdx=wdx, aap=aap: e.dma_start(
                        out=w0s[128 * r:128 * (r + 1), K0SW * wdx:K0SW * (wdx + 1)], in_=aap),
                        waits=[(ve, k0_cv[p])], inc=k0_st[p])
                    k0_sts[p] = k0_st[p].n

                P.mark("l0a.A1")
                P.v(None, waits=[(ld, x_ld)])
                P.s(None, waits=[(ld, x_ld)])
                wv = wa = 0
                for tt in range(NT):
                    wv, wa = ln_tile_stats(xr[:, tt * 1024:(tt + 1) * 1024], tt)
                ln_finalize(NT, wv, wa)
                mbv, rca = ve.n, ac.n
                tp_after = {}
                for tt in range(NT):
                    half = tt % 2
                    w = [(ve, mbv), (ac, rca), (ac, ac.n)]
                    if tt >= 2:
                        w.append((pe, tp_after[tt - 2]))
                    aa = P.s(lambda e, tt=tt, half=half: e.activation(
                        nrm[:, half * 1024:(half + 1) * 1024], xr[:, tt * 1024:(tt + 1) * 1024],
                        AF.Identity, scale=stat[:, RC + tt:RC + tt + 1], bias=stat[:, MB + tt:MB + tt + 1]),
                        waits=w, inc=ac)

                    def evac(g, psap, ws, tt=tt):
                        dap = xh[:, :].rearrange("p (c t2) -> p c t2", t2=T)[:, 4 * (g % 2):4 * (g % 2) + 4,
                                                                            tt * 128:(tt + 1) * 128]
                        return (ve, P.v(lambda e, psap=psap, dap=dap: e.tensor_copy(dap, psap),
                                        waits=ws, inc=ve))

                    tp_after[tt] = transpose_run(
                        [nrm[:, half * 1024 + c * 128: half * 1024 + (c + 1) * 128] for c in range(KC)],
                        ps_t, 2, evac, extra_first=[(ac, aa), (cst, consts_cst)])
                xh_ve = ve.n

                P.mark("l0a.A2")
                kv_ev = {}
                kv_tp = {}
                s1_started = set()
                for sidx in range(NS):
                    for hi, (half, slabs) in enumerate((("k", (0, 1)), ("v", (2, 3)))):
                        for j2, j in enumerate(slabs):
                            if T <= 512 and j2 > 0:
                                pass
                            for oc4 in range(4):
                                bank = oc4 % 2
                                for kc in range(KC):
                                    w = []
                                    if kc == 0:
                                        w = [(P.w[0], kv_slab_ld[j]), (ve, xh_ve)]
                                        if bank in kv_ev:
                                            w.append((ac, kv_ev[bank]))
                                    P.t(lambda e, j=j, kc=kc, oc4=oc4, bank=bank, sidx=sidx: e.matmul(
                                        ps_mm[:, bank * 512: bank * 512 + ST],
                                        wq[:, j * 4096 + kc * 512 + oc4 * 128: j * 4096 + kc * 512 + (oc4 + 1) * 128],
                                        xh_ap(kc, sidx * ST, ST),
                                        start=(kc == 0), stop=(kc == KC - 1)),
                                        waits=w, inc=(pe if kc == KC - 1 else None))
                                mmc = pe.n
                                buf = oc4 % 2
                                fn = AF.Relu if half == "k" else AF.Copy
                                w = [(pe, mmc), (ac, ac.n)]
                                if buf in kv_tp:
                                    w.append((pe, kv_tp[buf]))
                                kv_ev[bank] = P.s(lambda e, buf=buf, bank=bank, fn=fn: e.activation(
                                    kvq[:, buf * ST: buf * ST + ST],
                                    ps_mm[:, bank * 512: bank * 512 + ST], fn),
                                    waits=w, inc=ac)
                                dstb = ktm if half == "k" else vtm
                                oco = j2 * 4 + oc4

                                def evac2(g, psap, ws, dstb=dstb, oco=oco):
                                    dap = dstb[:, :].rearrange("p (t2 c) -> p t2 c", c=1024)[
                                        :, :, oco * 128:(oco + 1) * 128]
                                    return (ve, P.v(lambda e, psap=psap, dap=dap: e.tensor_copy(dap, psap),
                                                    waits=ws, inc=ve))

                                kv_tp[buf] = transpose_run(
                                    [kvq[:, buf * ST + t2 * 128: buf * ST + (t2 + 1) * 128]
                                     for t2 in range(STT)],
                                    ps_t, 2, evac2, extra_first=[(ac, kv_ev[bank])])
                    for _ in range(K0N // NS + 1):
                        emit_k0_chunk()
                    ktmv = ve.n
                    last_s = (sidx == NS - 1)
                    for c in range(KC):
                        for t2 in range(STT):
                            st_first = (sidx == 0 and t2 == 0 and c % 3 == 0)
                            fin = last_s and t2 == STT - 1
                            w = [(ve, ktmv), (ve, consts_ve)] if t2 == 0 else []
                            P.t(lambda e, c=c, t2=t2, st_first=st_first, fin=fin: e.matmul(
                                s1slice(c, 0, 128),
                                ktm[:, t2 * 1024 + c * 128: t2 * 1024 + (c + 1) * 128],
                                vtm[:, t2 * 1024 + c * 128: t2 * 1024 + (c + 1) * 128],
                                start=st_first, stop=fin, skip_group_check=True),
                                waits=w)
                            P.t(lambda e, c=c, t2=t2, fin=fin: e.matmul(
                                s1slice(c, 128, 129),
                                ktm[:, t2 * 1024 + c * 128: t2 * 1024 + (c + 1) * 128],
                                onec[:, 0:1],
                                start=False, stop=fin, skip_group_check=True),
                                inc=(pe if (fin and c == KC - 1) else None))
                s1_pe = pe.n

                for c in range(KC):
                    P.v(lambda e, c=c: e.tensor_mul(s1t[:, c * 128:(c + 1) * 128],
                                                    s1slice(c, 0, 128), bdiag[:, :]),
                        waits=[(pe, s1_pe), (ld, init_ld), (ve, ve.n)], inc=ve)
                P.v(lambda e: e.memset(dstt[:, :], 0.0), waits=[(ve, ve.n)], inc=ve)
                mset = ve.n
                for c in range(KC):
                    for g in range(4):
                        P.v(lambda e, c=c, g=g: e.tensor_copy(
                            dstt[32 * g:32 * (g + 1), c * 4 + g: c * 4 + g + 1],
                            s1slice(c, 128, 129)[32 * g:32 * (g + 1), :]),
                            waits=[(ve, mset)], inc=ve)
                dst_ve = ve.n

                P.mark("l0a.A3")
                q_slab_ld = {}
                prj_ld = None
                prev_dv = 0
                prev_proj_free = 0
                prev_qt_all = 0
                for sidx in range(NS):
                    if sidx == 0:
                        w1 = P.w[1]
                        for j in range(2):
                            P.gp(lambda e, j=j: e.dma_start(
                                out=wq[:, j * 4096:(j + 1) * 4096].rearrange("p (k c) -> p k c", c=512),
                                in_=wqkvT[:, 512 * j: 512 * (j + 1)].rearrange("(k p) c -> p k c", p=128)),
                                waits=([(pe, s1_pe)] if j == 0 else []), inc=w1)
                        P.gp(lambda e: e.dma_start(
                            out=wq[:, 8192:16384].rearrange("p (k c) -> p k c", c=1024),
                            in_=wprojT[:, :].rearrange("(k p) c -> p k c", p=128)), inc=w1)
                        q_slab_ld[0] = q_slab_ld[1] = prj_ld = w1.n
                    q_ev = {}
                    for j in range(2):
                        for oc4 in range(4):
                            bank = oc4 % 2
                            cchunk = j * 4 + oc4
                            for kc in range(KC):
                                w = []
                                if kc == 0:
                                    w = [(P.w[1], q_slab_ld[j]), (ve, xh_ve), (ve, prev_proj_free),
                                         (ve, prev_qt_all)]
                                    if bank in q_ev:
                                        w.append((ac, q_ev[bank]))
                                    elif bank in kv_ev:
                                        w.append((ac, kv_ev[bank]))
                                P.t(lambda e, j=j, kc=kc, oc4=oc4, bank=bank, sidx=sidx: e.matmul(
                                    ps_mm[:, bank * 512: bank * 512 + ST],
                                    wq[:, j * 4096 + kc * 512 + oc4 * 128: j * 4096 + kc * 512 + (oc4 + 1) * 128],
                                    xh_ap(kc, sidx * ST, ST),
                                    start=(kc == 0), stop=(kc == KC - 1)),
                                    waits=w, inc=(pe if kc == KC - 1 else None))
                            mmc = pe.n
                            w = [(pe, mmc), (ac, ac.n),
                                 (k0_st[0], k0_sts.get(0, 0)), (k0_st[1], k0_sts.get(1, 0))]
                            q_ev[bank] = P.s(lambda e, cchunk=cchunk, bank=bank: e.activation(
                                qf[:, cchunk * ST: cchunk * ST + ST],
                                ps_mm[:, bank * 512: bank * 512 + ST], AF.Relu),
                                waits=w, inc=ac)
                    qf_ac = ac.n
                    den_ev = {}
                    for c in range(KC):
                        bp = (c % 2) * 32
                        w = [(ve, ve.n if c == 0 else 0)]
                        if c == 0:
                            w = [(ve, dst_ve), (ac, qf_ac), (ve, prev_dv)]
                        elif (c % 2) in den_ev:
                            w = [(ve, den_ev[c % 2])]
                        P.t(lambda e, c=c, bp=bp: e.matmul(ps_x[bp:bp + 4, 0:ST],
                                                           dstt[:, c * 4:(c + 1) * 4],
                                                           qf[:, c * ST: c * ST + ST],
                                                           start=True, stop=True),
                            waits=w, inc=pe)
                        dmp = pe.n
                        e1 = P.v(lambda e, c=c, bp=bp: e.tensor_scalar(
                            dtmp[0:4, 0:ST], ps_x[bp:bp + 4, 0:ST],
                            L0A_EPS, None, ALU.add),
                            waits=[(pe, dmp), (ve, ve.n)], inc=ve)
                        e2 = P.v(lambda e: e.reciprocal(dtmp[0:4, 0:ST], dtmp[0:4, 0:ST]),
                                 waits=[(ve, e1)], inc=ve)
                        den_ev[c % 2] = P.v(lambda e, c=c: e.tensor_copy(
                            dnvb[0:4, c * ST:(c + 1) * ST], dtmp[0:4, 0:ST]),
                            waits=[(ve, e2)], inc=ve)
                    prev_dv = dv2 = ve.n
                    qt_done = []
                    bc_ev = {}
                    for c in range(KC):
                        bank = c % 2
                        w = [(ve, dv2)]
                        if bank in bc_ev:
                            w.append((ve, bc_ev[bank]))
                        P.t(lambda e, c=c, bank=bank: e.matmul(
                            ps_t[:, bank * 512: bank * 512 + ST], sel4[0:4, :],
                            dnvb[0:4, c * ST:(c + 1) * ST], start=True, stop=True),
                            waits=w, inc=pe)
                        bcp = pe.n
                        bc_ev[bank] = P.v(lambda e, c=c, bank=bank: e.tensor_mul(
                            qt[:, c * ST: c * ST + ST], qf[:, c * ST: c * ST + ST],
                            ps_t[:, bank * 512: bank * 512 + ST]),
                            waits=[(pe, bcp), (ac, qf_ac), (ve, ve.n)], inc=ve)
                        qt_done.append(bc_ev[bank])
                    qt_all = ve.n
                    prev_qt_all = qt_all
                    af_ev = {}
                    for c in range(KC):
                        bank = c % 2
                        w = [(ve, qt_all)]
                        if bank in af_ev:
                            w.append((ac, af_ev[bank]))
                        P.t(lambda e, c=c, bank=bank: e.matmul(
                            ps_mm[:, bank * 512: bank * 512 + ST], s1t[:, c * 128:(c + 1) * 128],
                            qt[:, c * ST: c * ST + ST], start=True, stop=True),
                            waits=w, inc=pe)
                        nmp = pe.n
                        af_ev[bank] = P.s(lambda e, c=c, bank=bank: e.activation(
                            af[:, c * ST: c * ST + ST], ps_mm[:, bank * 512: bank * 512 + ST], AF.Copy),
                            waits=[(pe, nmp), (ac, ac.n),
                                   (k0_st[0], k0_sts.get(0, 0)), (k0_st[1], k0_sts.get(1, 0))], inc=ac)
                    af_all = ac.n
                    for t2 in range(STT):
                        for kc in range(KC):
                            for n2 in range(2):
                                w = []
                                if kc == 0 and n2 == 0:
                                    w = [(ac, af_all), (P.w[1], prj_ld), (ve, prev_proj_free)]
                                P.t(lambda e, kc=kc, n2=n2, t2=t2: e.matmul(
                                    ps_mm[:, n2 * 512:(n2 + 1) * 512],
                                    af[:, kc * ST + t2 * 128: kc * ST + (t2 + 1) * 128],
                                    wq[:, 8192 + kc * 1024 + n2 * 512: 8192 + kc * 1024 + (n2 + 1) * 512],
                                    start=(kc == 0), stop=(kc == KC - 1)),
                                    waits=w, inc=(pe if (kc == KC - 1 and n2 == 1) else None))
                        prp = pe.n
                        gt = sidx * STT + t2
                        ya = P.v(lambda e: e.tensor_add(ysc[:, :], ps_mm[:, :], bpj[:, :]),
                                 waits=[(pe, prp), (P.sems["wa"], bpj_ld), (ve, ve.n)], inc=ve)
                        prev_proj_free = ya
                        vr, ar = ln_small(ysc[:, :], SMX, [(ve, ya)])
                        v6 = P.v(lambda e, gt=gt: e.tensor_scalar(
                            xr[:, gt * 1024:(gt + 1) * 1024], xr[:, gt * 1024:(gt + 1) * 1024],
                            stat[:, SMX + 4:SMX + 5], None, ALU.add),
                            waits=[(ve, vr)], inc=ve)
                        P.v(lambda e, gt=gt: e.scalar_tensor_tensor(
                            xr[:, gt * 1024:(gt + 1) * 1024], ysc[:, :], stat[:, SMX + 3:SMX + 4],
                            xr[:, gt * 1024:(gt + 1) * 1024], ALU.mult, ALU.add),
                            waits=[(ve, v6)], inc=ve)
                P.barrier()


        if "l0f" in blocks:
            P.mark("l0f")
            es_f = contextlib.ExitStack()
            with es_f:
                xhs = es_f.enter_context(nc.sbuf_tensor("xhs", [128, 2 * KC * ST], BF16))
                Ab = es_f.enter_context(nc.sbuf_tensor("Ab", [128, FC * ST], BF16))
                Bb = es_f.enter_context(nc.sbuf_tensor("Bb", [128, FC * ST], BF16))
                wfc = es_f.enter_context(nc.sbuf_tensor("wfc", [128, 4 * 1024], BF16))
                rshl = es_f.enter_context(nc.sbuf_tensor("rshl", [32, 2 * 512], BF16))
                ps_t2 = es_f.enter_context(nc.psum_tensor("ps_t2", [128, 1024], F32))
                ps_m = es_f.enter_context(nc.psum_tensor("ps_m", [128, 2048], F32))
                ps_o = es_f.enter_context(nc.psum_tensor("ps_o", [128, 1024], F32))
                wl1 = (P.w[6], P.w[7])
                wl2 = (P.w[4], P.w[5])
                wl3 = (P.w[0], P.w[1])
                mm1_free = {}
                mm2_free = {}
                mm3_free = {}
                mm1_ld = {}
                mm2_ld = {}
                mm3_ld = {}

                def load_mm1_slab(gi, j):
                    par = gi % 2
                    w = [(pe, mm1_free.get(par, 0))]
                    if gi < 2:
                        w += [(k0_st[0], k0_sts.get(0, 0)), (k0_st[1], k0_sts.get(1, 0))]
                    for kc in range(KC):
                        P.sy(lambda e, j=j, kc=kc, par=par: e.dma_start(
                            out=wq[:, par * 4096 + kc * 512: par * 4096 + (kc + 1) * 512],
                            in_=w0s[128 * kc:128 * (kc + 1), 512 * j:512 * (j + 1)]),
                            waits=(w if kc == 0 else []), inc=wl1[par])
                    mm1_ld[gi] = (wl1[par], wl1[par].n)
                    return par

                def load_mm2_slab(gi, mc):
                    par = gi % 2
                    w = [(pe, mm2_free.get(par, 0))]
                    if gi < 2:
                        w += [(sto, k1t_total)]
                    P.gp(lambda e, mc=mc, par=par: e.dma_start(
                        out=wq[:, 8192 + par * 4096: 8192 + (par + 1) * 4096].rearrange(
                            "p (k c) -> p k c", c=128),
                        in_=w1s_d[mc * F: (mc + 1) * F, :].rearrange("(k p) c -> p k c", p=128)),
                        waits=w, inc=wl2[par])
                    mm2_ld[gi] = (wl2[par], wl2[par].n)
                    return par

                def load_mm3_slab(gi, kc2):
                    par = gi % 2
                    w = [(pe, mm3_free.get(par, 0))]
                    P.gp(lambda e, kc2=kc2, par=par: e.dma_start(
                        out=wfc[:, par * 2048:(par + 1) * 2048].rearrange(
                            "p (k c) -> p k c", c=1024),
                        in_=k2_in[128 * kc2:128 * (kc2 + 2), :].rearrange(
                            "(k p) c -> p k c", p=128)),
                        waits=w, inc=wl3[par])
                    mm3_ld[gi] = (wl3[par], wl3[par].n)
                    return par

                g1 = g2 = g3 = 0
                prev_eva = 0

                def l0f_ln(sidx):
                    par = sidx % 2
                    wv = wa = 0
                    for t2 in range(STT):
                        gt = sidx * STT + t2
                        wv, wa = ln_tile_stats(xr[:, gt * 1024:(gt + 1) * 1024], t2)
                    ln_finalize(STT, wv, wa)
                    mbv, rca = ve.n, ac.n
                    tpa = {}
                    for t2 in range(STT):
                        half = t2 % 2
                        gt = sidx * STT + t2
                        w = [(ve, mbv), (ac, rca), (ac, ac.n)]
                        if t2 >= 2:
                            w.append((pe, tpa[t2 - 2]))
                        aa = P.s(lambda e, gt=gt, half=half, t2=t2: e.activation(
                            nrm[:, half * 1024:(half + 1) * 1024], xr[:, gt * 1024:(gt + 1) * 1024],
                            AF.Identity, scale=stat[:, RC + t2:RC + t2 + 1],
                            bias=stat[:, MB + t2:MB + t2 + 1]),
                            waits=w, inc=ac)

                        def evacx(g, psap, ws, t2=t2, par=par):
                            dap = xhs[:, par * KC * ST:(par + 1) * KC * ST].rearrange(
                                "p (c t3) -> p c t3", t3=ST)[
                                :, 4 * (g % 2):4 * (g % 2) + 4, t2 * 128:(t2 + 1) * 128]
                            return (ve, P.v(lambda e, psap=psap, dap=dap: e.tensor_copy(dap, psap),
                                            waits=ws, inc=ve))

                        tpa[t2] = transpose_run(
                            [nrm[:, half * 1024 + c * 128: half * 1024 + (c + 1) * 128] for c in range(KC)],
                            ps_t2, 2, evacx, extra_first=[(ac, aa)])
                    return ve.n

                p2cv = []
                if "l1a" in blocks:
                    for wnm in ("q", "k", "v", "p"):
                        for srcw, dstw in ((kTm[wnm], kTb[wnm]), (vVm[wnm], vVb[wnm])):
                            for r in range(KC):
                                p2cv.append(lambda e, r=r, srcw=srcw, dstw=dstw: e.dma_start(
                                    out=dstw[128 * r:128 * (r + 1), :],
                                    in_=srcw[128 * r:128 * (r + 1), :]))
                p2cv_i = [0]

                def drain_p2cv(k):
                    while k > 0 and p2cv_i[0] < len(p2cv):
                        P.gp(p2cv[p2cv_i[0]], inc=P.w[2 + p2cv_i[0] % 2])
                        p2cv_i[0] += 1
                        k -= 1

                xhs_ve_n = {0: l0f_ln(0)}
                prev_rc_ve = 0
                prev_mm2_ac = 0
                prev_mm3_ac = 0
                for sidx in range(NS):
                    par0 = sidx % 2
                    xhs_ve = xhs_ve_n[sidx]

                    def xs_ap(kc, par0=par0):
                        return xhs[:, par0 * KC * ST + kc * ST: par0 * KC * ST + (kc + 1) * ST]
                    ev1 = {}
                    for j in range(8):
                        if sidx > 0:
                            drain_p2cv(3)
                        par = load_mm1_slab(g1, j)
                        for m4 in range(4):
                            mc = j * 4 + m4
                            bank = mc % 4
                            for kc in range(KC):
                                w = []
                                if kc == 0:
                                    w = [mm1_ld[g1], (ve, xhs_ve), (ve, prev_rc_ve),
                                         (ac, prev_mm2_ac), (ac, prev_mm3_ac)]
                                    if bank in ev1:
                                        w.append(ev1[bank])
                                P.t(lambda e, par=par, kc=kc, m4=m4, bank=bank, xs_ap=xs_ap: e.matmul(
                                    ps_m[:, bank * 512: bank * 512 + ST],
                                    wq[:, par * 4096 + kc * 512 + m4 * 128: par * 4096 + kc * 512 + (m4 + 1) * 128],
                                    xs_ap(kc),
                                    start=(kc == 0), stop=(kc == KC - 1)),
                                    waits=w, inc=(pe if kc == KC - 1 else None))
                            if m4 == 3:
                                mm1_free[par] = pe.n
                            if mc % 2 == 0:
                                ev1[bank] = (ac, P.s(lambda e, mc=mc, bank=bank: e.activation(
                                    Ab[:, mc * ST:(mc + 1) * ST], ps_m[:, bank * 512: bank * 512 + ST],
                                    AF.Relu, scale=smv[:, 4:5]),
                                    waits=[(pe, pe.n), (ac, ac.n)], inc=ac))
                            else:
                                ev1[bank] = (ve, P.v(lambda e, mc=mc, bank=bank: e.tensor_scalar(
                                    Ab[:, mc * ST:(mc + 1) * ST], ps_m[:, bank * 512: bank * 512 + ST],
                                    smv[:, 4:5], 0.0, ALU.mult, ALU.max),
                                    waits=[(pe, pe.n), (ve, ve.n)], inc=ve))
                        g1 += 1
                    mm1_ac, mm1_ve = ac.n, ve.n
                    if sidx + 1 < NS:
                        xhs_ve_n[sidx + 1] = l0f_ln(sidx + 1)
                    for kc2 in range(FC):
                        P.t(lambda e, kc2=kc2: e.matmul(
                            ps_o[0:32, 0:ST], onec[:, 0:32], Ab[:, kc2 * ST:(kc2 + 1) * ST],
                            start=(kc2 == 0), stop=(kc2 == FC - 1)),
                            waits=([(ac, mm1_ac), (ve, mm1_ve)] if kc2 == 0 else []),
                            inc=(pe if kc2 == FC - 1 else None))
                    rs_pe = pe.n
                    rv1 = P.v(lambda e: e.tensor_copy(rshl[0:32, 0:ST], ps_o[0:32, 0:ST]),
                              waits=[(pe, rs_pe), (ve, ve.n)], inc=ve)
                    rv2 = P.v(lambda e: e.tensor_sub(rshl[0:32, 512:512 + ST], ps_o[0:32, 0:ST],
                                                     rshl[0:32, 0:ST]),
                              waits=[(ve, rv1)], inc=ve)
                    ev2 = {}
                    for mc in range(FC):
                        par = load_mm2_slab(g2, mc)
                        bank = mc % 4
                        for kc2 in range(FC):
                            w = []
                            if kc2 == 0:
                                w = [mm2_ld[g2], (ac, mm1_ac), (ve, mm1_ve)]
                                if bank in ev2:
                                    w.append((ac, ev2[bank]))
                            P.t(lambda e, par=par, kc2=kc2, bank=bank: e.matmul(
                                ps_m[:, bank * 512: bank * 512 + ST],
                                wq[:, 8192 + par * 4096 + kc2 * 128: 8192 + par * 4096 + (kc2 + 1) * 128],
                                Ab[:, kc2 * ST: (kc2 + 1) * ST],
                                start=(kc2 == 0), stop=False),
                                waits=w)
                        for hl in range(2):
                            P.t(lambda e, hl=hl, bank=bank: e.matmul(
                                ps_m[:, bank * 512: bank * 512 + ST],
                                nhalf[0:32, :], rshl[0:32, hl * 512: hl * 512 + ST],
                                start=False, stop=(hl == 1)),
                                waits=([(ve, rv2)] if hl == 0 else []),
                                inc=(pe if hl == 1 else None))
                        mm2_free[par] = pe.n
                        ev2[bank] = P.s(lambda e, mc=mc, bank=bank: e.activation(
                            Bb[:, mc * ST:(mc + 1) * ST], ps_m[:, bank * 512: bank * 512 + ST],
                            AF.Relu, scale=smv[:, 7:8]),
                            waits=[(pe, pe.n), (ac, ac.n)], inc=ac)
                        g2 += 1
                    mm2_ac = ac.n
                    prev_mm2_ac = mm2_ac
                    for hf in range(STT // 2):
                        for kc2 in range(FC):
                            if kc2 % 2 == 0:
                                par = load_mm3_slab(g3, kc2)
                            sub = (kc2 % 2) * 1024
                            for ti in range(2):
                                t2 = hf * 2 + ti
                                for n2 in range(2):
                                    w = []
                                    if kc2 == 0 and ti == 0 and n2 == 0:
                                        w = [(ac, mm2_ac), (ve, prev_eva), (ac, prev_mm3_ac)]
                                    if kc2 % 2 == 0 and ti == 0 and n2 == 0:
                                        w = w + [mm3_ld[g3]]
                                    P.t(lambda e, par=par, kc2=kc2, t2=t2, ti=ti, n2=n2, sub=sub: e.matmul(
                                        ps_m[:, ti * 1024 + n2 * 512: ti * 1024 + (n2 + 1) * 512],
                                        Bb[:, kc2 * ST + t2 * 128: kc2 * ST + (t2 + 1) * 128],
                                        wfc[:, par * 2048 + sub + n2 * 512: par * 2048 + sub + (n2 + 1) * 512],
                                        start=(kc2 == 0), stop=(kc2 == FC - 1)),
                                        waits=w)
                                P.t(lambda e, kc2=kc2, t2=t2, ti=ti: e.matmul(
                                    ps_o[:, ti * 512: ti * 512 + 1],
                                    Bb[:, kc2 * ST + t2 * 128: kc2 * ST + (t2 + 1) * 128],
                                    onec[:, 0:1],
                                    start=(kc2 == 0), stop=(kc2 == FC - 1)),
                                    inc=(pe if ti == 1 else None))
                            if kc2 % 2 == 1:
                                mm3_free[par] = pe.n
                                g3 += 1
                        mmp = pe.n
                        for ti in range(2):
                            t2 = hf * 2 + ti
                            gt = sidx * STT + t2
                            rc1 = P.v(lambda e, ti=ti: e.tensor_scalar(
                                stat[:, SMX + 6:SMX + 7], ps_o[:, ti * 512: ti * 512 + 1],
                                smv[:, 2:3], -1.0, ALU.mult, ALU.mult),
                                waits=[(pe, mmp), (ve, ve.n)], inc=ve)
                            prev_rc_ve = rc1
                            ya = P.s(lambda e, ti=ti: e.activation(
                                ysc[:, :], ps_m[:, ti * 1024:(ti + 1) * 1024],
                                AF.Identity, scale=smv[:, 8:9], bias=stat[:, SMX + 6:SMX + 7]),
                                waits=[(pe, mmp), (ac, ac.n), (ve, prev_eva), (ve, rc1)], inc=ac)
                            prev_mm3_ac = ya
                            vr, ar = ln_small(ysc[:, :], SMX, [(ac, ya)])
                            v6 = P.v(lambda e, gt=gt: e.tensor_scalar(
                                xr[:, gt * 1024:(gt + 1) * 1024], xr[:, gt * 1024:(gt + 1) * 1024],
                                stat[:, SMX + 4:SMX + 5], None, ALU.add),
                                waits=[(ve, vr)], inc=ve)
                            prev_eva = P.v(lambda e, gt=gt: e.scalar_tensor_tensor(
                                xr[:, gt * 1024:(gt + 1) * 1024], ysc[:, :], stat[:, SMX + 3:SMX + 4],
                                xr[:, gt * 1024:(gt + 1) * 1024], ALU.mult, ALU.add),
                                waits=[(ve, v6)], inc=ve)
                drain_p2cv(len(p2cv))
                es.p2cv_done = (P.w[2].n, P.w[3].n)
            P.barrier()


        if "l1a" in blocks:
            P.mark("l1a.N1")
            H1, D1 = 16, 64
            es_h = contextlib.ExitStack()
            hfm = es_h.enter_context(nc.sbuf_tensor("hfm", [128, KC * T], BF16))
            es_n1 = contextlib.ExitStack()
            with es_n1:
                ps_n1 = es_n1.enter_context(nc.psum_tensor("ps_n1", [128, 1024], F32))
                wv = wa = 0
                for tt in range(NT):
                    wv, wa = ln_tile_stats(xr[:, tt * 1024:(tt + 1) * 1024], tt)
                ln_finalize(NT, wv, wa, mu_c=MU1, r_c=R1, mb_c=MB1)
                mbv, rca = ve.n, ac.n
                tpa = {}
                prev_ai = 0
                for tt in range(NT):
                    half = tt % 2
                    w = [(ve, mbv), (ac, rca), (ac, ac.n), (ve, prev_ai)]
                    ai = P.s(lambda e, tt=tt: e.activation(
                        ysc[:, :], xr[:, tt * 1024:(tt + 1) * 1024], AF.Identity,
                        scale=stat[:, R1 + tt:R1 + tt + 1], bias=stat[:, MB1 + tt:MB1 + tt + 1]),
                        waits=w, inc=ac)
                    vr, ar = ln_small(ysc[:, :], SMX, [(ac, ai)])
                    w2_ = [(ve, vr), (ac, ar), (ac, ac.n)]
                    if tt >= 2:
                        w2_.append((pe, tpa[tt - 2]))
                    aa = P.s(lambda e, half=half: e.activation(
                        nrm[:, half * 1024:(half + 1) * 1024], ysc[:, :],
                        AF.Identity, scale=stat[:, SMX + 3:SMX + 4], bias=stat[:, SMX + 4:SMX + 5]),
                        waits=w2_, inc=ac)

                    def evach(g, psap, ws, tt=tt):
                        dap = hfm[:, :].rearrange("p (c t2) -> p c t2", t2=T)[
                            :, 4 * (g % 2):4 * (g % 2) + 4, tt * 128:(tt + 1) * 128]
                        return (ve, P.v(lambda e, psap=psap, dap=dap: e.tensor_copy(dap, psap),
                                        waits=ws, inc=ve))

                    tpa[tt] = transpose_run(
                        [nrm[:, half * 1024 + c * 128: half * 1024 + (c + 1) * 128] for c in range(KC)],
                        ps_n1, 2, evach, extra_first=[(ac, aa)])
                    prev_ai = ve.n
                    P.sy(lambda e, tt=tt: e.dma_start(out=xstash[128 * tt:128 * (tt + 1), :],
                                                      in_=xr[:, tt * 1024:(tt + 1) * 1024]),
                         inc=sto)
                hfm_ve = ve.n
                stash_total = sto.n
            es1.close()
            P.barrier()
            P.s(None, waits=[(sto, stash_total)])
            P.v(None, waits=[(sto, stash_total)])
            P.t(None, waits=[(sto, stash_total)])

            P.mark("l1a.P2")
            es2 = contextlib.ExitStack()
            qfl = es2.enter_context(nc.sbuf_tensor("qfl", [128, KC * T], BF16, side="right"))
            kfl = es2.enter_context(nc.sbuf_tensor("kfl", [128, KC * T], BF16, side="right"))
            vtl = es2.enter_context(nc.sbuf_tensor("vtl", [128, NT * 1024], BF16, side="right"))
            es_p2 = contextlib.ExitStack()
            with es_p2:
                ptile = es_p2.enter_context(nc.sbuf_tensor("ptile", [128, STT * 1024], BF16))
                pf = es_p2.enter_context(nc.sbuf_tensor("pf", [128, KC * ST], BF16))
                ps2 = es_p2.enter_context(nc.psum_tensor("ps2", [128, 4096], F32))
                L2 = SMX + 16
                wlg = (P.sems["wa"], P.sems["wb"])
                wlg2 = (P.sems["wc"], P.sems["wd"])
                kt_ld = {}
                v_ld = {}
                exp_done = {}
                reg_free = {}
                prev_scores_pe = 0
                prev_av_pe = 0
                for widx, wnm in enumerate(("q", "k", "v")):
                    par = widx % 2
                    w = [(pe, prev_scores_pe)]
                    if widx < 2:
                        w += [(P.w[2], es.p2cv_done[0]), (P.w[3], es.p2cv_done[1])]
                    for kc in range(KC):
                        P.sy(lambda e, wnm=wnm, kc=kc: e.dma_start(
                            out=wq[:, kc * 1024:(kc + 1) * 1024],
                            in_=kTb[wnm][128 * kc:128 * (kc + 1), :]),
                            waits=(w if kc == 0 else []), inc=wlg[par])
                    kt_ld[widx] = (wlg[par], wlg[par].n)
                    w = [(pe, prev_av_pe)]
                    if widx < 2:
                        w += [(P.w[2], es.p2cv_done[0]), (P.w[3], es.p2cv_done[1])]
                    for nc_ in range(KC):
                        P.sy(lambda e, wnm=wnm, nc_=nc_: e.dma_start(
                            out=wq[:, 8192 + nc_ * 1024: 8192 + (nc_ + 1) * 1024],
                            in_=vVb[wnm][128 * nc_:128 * (nc_ + 1), :]),
                            waits=(w if nc_ == 0 else []), inc=wlg2[par])
                    v_ld[widx] = (wlg2[par], wlg2[par].n)
                    if widx == 2:
                        cp = es.conv_pool = []
                        for r in range(KC):
                            cp.append(lambda e, r=r: e.dma_start(
                                out=fKb[128 * r:128 * (r + 1), :],
                                in_=fKT[128 * r:128 * (r + 1), :]))
                        for r in range(FC):
                            cp.append(lambda e, r=r: e.dma_start(
                                out=fVb[128 * r:128 * (r + 1), :],
                                in_=fV[128 * r:128 * (r + 1), :]))
                        if "l1f" in blocks:
                            for srcw, dstw in ((w1T, w1b), (w2T, w2b)):
                                for r in range(KC):
                                    cp.append(lambda e, r=r, srcw=srcw, dstw=dstw: e.dma_start(
                                        out=dstw[128 * r:128 * (r + 1), :],
                                        in_=srcw[128 * r:128 * (r + 1), :]))
                            for r in range(FC):
                                cp.append(lambda e, r=r: e.dma_start(
                                    out=w3b[128 * r:128 * (r + 1), :],
                                    in_=w3T[128 * r:128 * (r + 1), :]))
                            for srcw, dstw in ((h1T, h1s_t), (h2T, h2s_t)):
                                for r in range(FC):
                                    for gq in range(4):
                                        cp.append(lambda e, r=r, gq=gq, srcw=srcw, dstw=dstw: e.dma_start(
                                            out=dstw.rearrange("(m k) c -> k m c", k=F)[
                                                128 * r:128 * (r + 1), 8 * gq:8 * gq + 8, :],
                                            in_=srcw[128 * r:128 * (r + 1), :].rearrange(
                                                "p (m c) -> p m c", c=128)[:, 8 * gq:8 * gq + 8, :]))
                    for sidx in range(NS):
                        sc_done = {}
                        for t2 in range(STT):
                            for nh in range(2):
                                for kc in range(KC):
                                    w = []
                                    if nh == 0 and kc == 0:
                                        w = [kt_ld[widx], (ve, hfm_ve),
                                             exp_done.get(t2, (ac, 0)),
                                             reg_free.get(t2, (ac, 0))]
                                    P.t(lambda e, kc=kc, nh=nh, sidx=sidx, t2=t2: e.matmul(
                                        ps2[:, t2 * 1024 + nh * 512: t2 * 1024 + (nh + 1) * 512],
                                        hfm[:, kc * T + sidx * ST + t2 * 128: kc * T + sidx * ST + (t2 + 1) * 128],
                                        wq[:, kc * 1024 + nh * 512: kc * 1024 + (nh + 1) * 512],
                                        start=(kc == 0), stop=(kc == KC - 1)),
                                        waits=w, inc=(pe if (nh == 1 and kc == KC - 1) else None))
                            sc_done[t2] = pe.n
                        for t2 in range(STT):
                            rm = P.v(lambda e, t2=t2: e.reduce_max(
                                stat[:, SMX:SMX + 1], ps2[:, t2 * 1024:(t2 + 1) * 1024], axis=AX.X),
                                waits=[(pe, sc_done[t2]), (ve, ve.n)], inc=ve)
                            rb = P.v(lambda e: e.tensor_scalar(
                                stat[:, SMX + 1:SMX + 2], stat[:, SMX:SMX + 1],
                                -1.0 / 32.0, None, ALU.mult),
                                waits=[(ve, rm)], inc=ve)
                            exq = P.s(lambda e, t2=t2: e.activation(
                                ptile[:, t2 * 1024:(t2 + 1) * 1024], ps2[:, t2 * 1024:(t2 + 1) * 1024],
                                AF.Exp, scale=1.0 / 32.0, bias=stat[:, SMX + 1:SMX + 2],
                                accum_out=stat[:, SMX + 2:SMX + 3]),
                                waits=[(ve, rb), (ac, ac.n)], inc=ac)
                            exp_done[t2] = (ac, exq)
                            rc = P.v(lambda e, t2=t2: e.reciprocal(stat[:, L2 + t2:L2 + t2 + 1],
                                                                   stat[:, SMX + 2:SMX + 3]),
                                     waits=[(ac, exq), (ve, ve.n)], inc=ve)
                            nmv = rc
                            if wnm in ("q", "k"):
                                nmv = P.v(lambda e, t2=t2: e.tensor_scalar_mul(
                                    ptile[:, t2 * 1024:(t2 + 1) * 1024],
                                    ptile[:, t2 * 1024:(t2 + 1) * 1024],
                                    stat[:, L2 + t2:L2 + t2 + 1]),
                                    waits=[(ve, rc), (ac, exq)], inc=ve)
                            for g2_ in range(2):
                                w = [(ve, nmv), (ac, exq)]
                                for jj in range(4):
                                    P.t(lambda e, t2=t2, g2_=g2_, jj=jj: e.matmul(
                                        ps2[:, t2 * 1024 + g2_ * 512 + jj * 128:
                                            t2 * 1024 + g2_ * 512 + (jj + 1) * 128],
                                        ptile[:, t2 * 1024 + (g2_ * 4 + jj) * 128:
                                              t2 * 1024 + (g2_ * 4 + jj + 1) * 128],
                                        ident[:, :], start=True, stop=True),
                                        waits=(w if jj == 0 else []),
                                        inc=(pe if jj == 3 else None))
                                if g2_ == 0:
                                    P.v(lambda e, t2=t2: e.tensor_copy(
                                        pf[:, :].rearrange("p (c t3) -> p c t3", t3=ST)[
                                            :, 0:4, t2 * 128:(t2 + 1) * 128],
                                        ps2[:, t2 * 1024: t2 * 1024 + 512]),
                                        waits=[(pe, pe.n), (ve, ve.n)], inc=ve)
                                else:
                                    P.s(lambda e, t2=t2: e.copy(
                                        pf[:, :].rearrange("p (c t3) -> p c t3", t3=ST)[
                                            :, 4:8, t2 * 128:(t2 + 1) * 128],
                                        ps2[:, t2 * 1024 + 512: t2 * 1024 + 1024]),
                                        waits=[(pe, pe.n), (ac, ac.n)], inc=ac)
                        pf_ve, pf_ac = ve.n, ac.n
                        if sidx == NS - 1:
                            prev_scores_pe = pe.n
                        if wnm in ("q", "k"):
                            dstb = qfl if wnm == "q" else kfl
                            ev = {}
                            for cc in range(KC):
                                reg = (cc % 4) * 512
                                for nc_ in range(KC):
                                    w = []
                                    if nc_ == 0:
                                        w = [(ve, pf_ve), (ac, pf_ac), v_ld[widx]]
                                        if (cc % 4) in ev:
                                            w.append((ac, ev[cc % 4]))
                                    P.t(lambda e, cc=cc, nc_=nc_, reg=reg: e.matmul(
                                        ps2[:, reg: reg + ST],
                                        wq[:, 8192 + nc_ * 1024 + cc * 128: 8192 + nc_ * 1024 + (cc + 1) * 128],
                                        pf[:, nc_ * ST:(nc_ + 1) * ST],
                                        start=(nc_ == 0), stop=(nc_ == KC - 1)),
                                        waits=w, inc=(pe if nc_ == KC - 1 else None))
                                avp2 = pe.n
                                ev[cc % 4] = P.s(lambda e, cc=cc, reg=reg, dstb=dstb, sidx=sidx: e.activation(
                                    dstb[:, cc * T + sidx * ST: cc * T + sidx * ST + ST],
                                    ps2[:, reg: reg + ST], AF.Copy),
                                    waits=[(pe, avp2), (ac, ac.n)], inc=ac)
                            for t2 in range(STT):
                                reg_free[t2] = (ac, ac.n)
                            if sidx == NS - 1:
                                prev_av_pe = pe.n
                        else:
                            ev = {}
                            for t2 in range(STT):
                                reg = t2 * 1024
                                for nc_ in range(KC):
                                    for nh in range(2):
                                        w = []
                                        if nc_ == 0 and nh == 0:
                                            w = [(ve, pf_ve), (ac, pf_ac), v_ld[widx]]
                                        P.t(lambda e, t2=t2, nc_=nc_, nh=nh, reg=reg: e.matmul(
                                            ps2[:, reg + nh * 512: reg + (nh + 1) * 512],
                                            pf[:, nc_ * ST + t2 * 128: nc_ * ST + (t2 + 1) * 128],
                                            wq[:, 8192 + nc_ * 1024 + nh * 512: 8192 + nc_ * 1024 + (nh + 1) * 512],
                                            start=(nc_ == 0), stop=(nc_ == KC - 1)),
                                            waits=w, inc=(pe if (nc_ == KC - 1 and nh == 1) else None))
                                avp2 = pe.n
                                gt = sidx * STT + t2
                                ev[t2] = P.s(lambda e, t2=t2, reg=reg, gt=gt: e.activation(
                                    vtl[:, gt * 1024:(gt + 1) * 1024], ps2[:, reg:reg + 1024],
                                    AF.Identity, scale=stat[:, L2 + t2:L2 + t2 + 1]),
                                    waits=[(pe, avp2), (ac, ac.n)], inc=ac)
                                reg_free[t2] = (ac, ac.n)
                            if sidx == NS - 1:
                                prev_av_pe = pe.n
                qkv_ac = ac.n
            P.barrier()

            P.mark("l1a.P3")
            es_h.close()
            es3 = contextlib.ExitStack()
            ofm = es3.enter_context(nc.sbuf_tensor("ofm", [128, KC * T], BF16))
            es_p3 = contextlib.ExitStack()
            with es_p3:
                sraw = es_p3.enter_context(nc.sbuf_tensor("sraw", [128, 6144], F32))
                ptr = wq[:, 0:2048]
                prow = wq[:, 2048:6144]
                ofin = es_p3.enter_context(nc.sbuf_tensor("ofin", [128, 2 * 64], BF16))
                ps3 = es_p3.enter_context(nc.psum_tensor("ps3", [128, 4096], F32))
                SROFF = (0, 2048, 4096)
                PROFF = (0, 2048)
                po = P.po
                CM0, CB0, CL0, CLI0, CW0 = SMX, SMX + 3, SMX + 6, SMX + 9, SMX + 12
                rows = [(hh, qi) for hh in range(H1) for qi in range(NT)]
                nrows = len(rows)
                slot_free = {}
                sraw_free = {}
                prow_free = {}
                tp_free = {}
                pv_free = {}
                ptr_busy = 0
                ofin_busy = {}
                wmax_done = {}
                gwa = 0

                def passA(r):
                    nonlocal gwa
                    hh, qi = rows[r]
                    hp = (hh % 2) * 64
                    qc = hh // 2
                    ncols = (qi + 1) * 128
                    rb = r % 3
                    nw = (ncols + 511) // 512
                    for wd in range(nw):
                        c0 = wd * 512
                        wcols = min(512, ncols - c0)
                        s4 = gwa % 4
                        w = list(slot_free.get(s4, [(pe, 0)]))
                        if r == 0 and wd == 0:
                            w.append((ac, qkv_ac))
                        P.t(lambda e, hp=hp, qc=qc, qi=qi, c0=c0, wcols=wcols, s4=s4: e.matmul(
                            ps3[:, s4 * 512: s4 * 512 + wcols],
                            qfl[hp:hp + 64, qc * T + qi * 128: qc * T + (qi + 1) * 128],
                            kfl[hp:hp + 64, qc * T + c0: qc * T + c0 + wcols],
                            start=True, stop=True),
                            waits=w, inc=pe)
                        scp = pe.n
                        ww = [(pe, scp)]
                        if wd == 0 and r >= 3:
                            ww.append(sraw_free.get(rb, (ac, 0)))
                        wwk = list(ww)
                        dstart = qi * 128 - c0
                        pre_evs = None
                        if 0 <= dstart < wcols:
                            if dstart > 0:
                                if gwa % 2 == 0:
                                    pre_evs = (ac, P.s(
                                        lambda e, rb=rb, c0=c0, s4=s4, dstart=dstart: e.copy(
                                            sraw[:, SROFF[rb] + c0: SROFF[rb] + c0 + dstart],
                                            ps3[:, s4 * 512: s4 * 512 + dstart]),
                                        waits=ww + [(ac, ac.n)], inc=ac))
                                else:
                                    pre_evs = (ve, P.v(
                                        lambda e, rb=rb, c0=c0, s4=s4, dstart=dstart: e.tensor_copy(
                                            sraw[:, SROFF[rb] + c0: SROFF[rb] + c0 + dstart],
                                            ps3[:, s4 * 512: s4 * 512 + dstart]),
                                        waits=ww + [(ve, ve.n)], inc=ve))
                                ww = wwk
                            evd = P.v(lambda e, rb=rb, c0=c0, s4=s4, dstart=dstart: e.tensor_add(
                                sraw[:, SROFF[rb] + c0 + dstart: SROFF[rb] + c0 + dstart + 128],
                                ps3[:, s4 * 512 + dstart: s4 * 512 + dstart + 128], cmask[:, :]),
                                waits=ww + [(ve, ve.n)], inc=ve)
                            evs = (ve, evd)
                        elif gwa % 2 == 0:
                            evs = (ac, P.s(lambda e, rb=rb, c0=c0, s4=s4, wcols=wcols: e.copy(
                                sraw[:, SROFF[rb] + c0: SROFF[rb] + c0 + wcols],
                                ps3[:, s4 * 512: s4 * 512 + wcols]),
                                waits=ww + [(ac, ac.n)], inc=ac))
                        else:
                            evs = (ve, P.v(lambda e, rb=rb, c0=c0, s4=s4, wcols=wcols: e.tensor_copy(
                                sraw[:, SROFF[rb] + c0: SROFF[rb] + c0 + wcols],
                                ps3[:, s4 * 512: s4 * 512 + wcols]),
                                waits=ww + [(ve, ve.n)], inc=ve))
                        slot_free[s4] = [evs] + ([pre_evs] if pre_evs else [])
                        wmax_done[r] = P.v(lambda e, rb=rb, c0=c0, wcols=wcols, wd=wd: e.reduce_max(
                            stat[:, CW0 + rb * 4 + wd: CW0 + rb * 4 + wd + 1],
                            sraw[:, SROFF[rb] + c0: SROFF[rb] + c0 + wcols], axis=AX.X),
                            waits=[evs, (ve, ve.n)] + ([pre_evs] if pre_evs else []), inc=ve)
                        gwa += 1
                    return nw

                def passB1(r, nw):
                    hh, qi = rows[r]
                    ncols = (qi + 1) * 128
                    rb = r % 3
                    rbp = r % 2
                    cmb = P.v(lambda e, rb=rb, nw=nw: e.reduce_max(
                        stat[:, CM0 + rb:CM0 + rb + 1], stat[:, CW0 + rb * 4: CW0 + rb * 4 + nw],
                        axis=AX.X),
                        waits=[(ve, wmax_done[r]), (ve, ve.n)], inc=ve)
                    bs = P.gp(lambda e, rb=rb: e.tensor_scalar(
                        stat[:, CB0 + rb:CB0 + rb + 1], stat[:, CM0 + rb:CM0 + rb + 1],
                        -0.125, None, ALU.mult),
                        waits=[(ve, cmb), (po, po.n)], inc=po, amt=1)
                    w = [(po, bs), (ac, ac.n), prow_free.get(rbp, (pe, 0))]
                    exq = P.s(lambda e, rb=rb, rbp=rbp, ncols=ncols: e.activation(
                        prow[:, PROFF[rbp]: PROFF[rbp] + ncols], sraw[:, SROFF[rb]: SROFF[rb] + ncols],
                        AF.Exp, scale=0.125, bias=stat[:, CB0 + rb:CB0 + rb + 1],
                        accum_out=stat[:, CL0 + rb:CL0 + rb + 1]),
                        waits=w, inc=ac)
                    sraw_free[rb] = (ac, exq)
                    return exq

                def passB2(r, exq):
                    nonlocal ptr_busy
                    hh, qi = rows[r]
                    hp = (hh % 2) * 64
                    qc = hh // 2
                    ntk = qi + 1
                    rb = r % 3
                    rbp = r % 2
                    vli = P.v(lambda e, rb=rb: e.reciprocal(stat[:, CLI0 + rb:CLI0 + rb + 1],
                                                            stat[:, CL0 + rb:CL0 + rb + 1]),
                              waits=[(ac, exq), (ve, ve.n)], inc=ve)
                    tev = {}
                    for j0 in range(0, ntk, 4):
                        jn = min(4, ntk - j0)
                        slot = (j0 // 4) % 2
                        w = [(ac, exq), (pe, ptr_busy), tp_free.get(slot, (pe, 0))]
                        for jj in range(jn):
                            P.t(lambda e, rbp=rbp, j0=j0, jj=jj, slot=slot: e.matmul(
                                ps3[:, 2048 + slot * 512 + jj * 128: 2048 + slot * 512 + (jj + 1) * 128],
                                prow[:, PROFF[rbp] + (j0 + jj) * 128: PROFF[rbp] + (j0 + jj + 1) * 128],
                                ident[:, :], start=True, stop=True),
                                waits=(w if jj == 0 else []), inc=(pe if jj == jn - 1 else None))
                        if (j0 // 4) % 2 == 0:
                            tp_free[slot] = (ve, P.v(lambda e, j0=j0, jn=jn, slot=slot: e.tensor_copy(
                                ptr[:, j0 * 128: (j0 + jn) * 128],
                                ps3[:, 2048 + slot * 512: 2048 + slot * 512 + jn * 128]),
                                waits=[(pe, pe.n), (ve, ve.n)], inc=ve))
                        else:
                            tp_free[slot] = (ac, P.s(lambda e, j0=j0, jn=jn, slot=slot: e.copy(
                                ptr[:, j0 * 128: (j0 + jn) * 128],
                                ps3[:, 2048 + slot * 512: 2048 + slot * 512 + jn * 128]),
                                waits=[(pe, pe.n), (ac, ac.n)], inc=ac))
                    prow_free[rbp] = (pe, pe.n)
                    pvslot = r % 2
                    w = [tp_free.get(0, (pe, 0)), tp_free.get(1, (pe, 0)),
                         pv_free.get(pvslot, (ac, 0))]
                    for j in range(ntk):
                        P.t(lambda e, j=j, hh=hh, pvslot=pvslot, ntk=ntk: e.matmul(
                            ps3[:, 3072 + pvslot * 512: 3072 + pvslot * 512 + 64],
                            ptr[:, j * 128:(j + 1) * 128],
                            vtl[:, j * 1024 + hh * 64: j * 1024 + (hh + 1) * 64],
                            start=(j == 0), stop=(j == ntk - 1)),
                            waits=(w if j == 0 else []), inc=(pe if j == ntk - 1 else None))
                    ptr_busy = pvp = pe.n
                    ob = r % 2
                    fin = P.s(lambda e, pvslot=pvslot, ob=ob, rb=rb: e.activation(
                        ofin[:, ob * 64:(ob + 1) * 64],
                        ps3[:, 3072 + pvslot * 512: 3072 + pvslot * 512 + 64],
                        AF.Identity, scale=stat[:, CLI0 + rb:CLI0 + rb + 1]),
                        waits=[(pe, pvp), (ve, vli), (ac, ac.n),
                               ofin_busy.get(ob, (pe, 0))], inc=ac)
                    return fin

                def finalize(r, fin):
                    hh, qi = rows[r]
                    hp = (hh % 2) * 64
                    qc = hh // 2
                    ntk = qi + 1
                    ob = r % 2
                    pvslot = r % 2
                    slot = (ntk + 3) // 4 % 2
                    P.t(lambda e, slot=slot, ob=ob: e.matmul(
                        ps3[0:64, 2048 + slot * 512: 2048 + slot * 512 + 128],
                        ofin[:, ob * 64:(ob + 1) * 64], ident[:, :], start=True, stop=True),
                        waits=[(ac, fin), tp_free.get(slot, (pe, 0))], inc=pe)
                    ofin_busy[ob] = (pe, pe.n)
                    oevq = P.s(lambda e, hp=hp, qc=qc, qi=qi, slot=slot: e.activation(
                        ofm[hp:hp + 64, qc * T + qi * 128: qc * T + (qi + 1) * 128],
                        ps3[0:64, 2048 + slot * 512: 2048 + slot * 512 + 128], AF.Copy),
                        waits=[(pe, pe.n), (ac, ac.n)], inc=ac)
                    tp_free[slot] = (ac, oevq)
                    pv_free[pvslot] = (ac, oevq)

                conv_pool = getattr(es, "conv_pool", [])
                conv_i = [0]

                def drain_conv(k):
                    while k > 0 and conv_i[0] < len(conv_pool):
                        P.gp(conv_pool[conv_i[0]],
                             inc=P.w[conv_i[0] % 2])
                        conv_i[0] += 1
                        k -= 1

                LKA = 2
                nwl = {}
                for rr in range(min(LKA, nrows)):
                    nwl[rr] = passA(rr)
                fins = {}
                for r in range(nrows):
                    exq = passB1(r, nwl.pop(r))
                    if r + LKA < nrows:
                        nwl[r + LKA] = passA(r + LKA)
                    if r - 1 in fins:
                        finalize(r - 1, fins.pop(r - 1))
                    fins[r] = passB2(r, exq)
                    drain_conv(2)
                finalize(nrows - 1, fins.pop(nrows - 1))
                drain_conv(len(conv_pool))
                es.h12_tile_done = es.fkv_done = (P.w[0].n, P.w[1].n)
                ofm_ac = ac.n
            es2.close()
            P.barrier()

            P.mark("l1a.P4")
            xb = es.enter_context(nc.sbuf_tensor("xb", [128, NT * 1024], F32, side="right"))
            es_p4 = contextlib.ExitStack()
            with es_p4:
                ptile4 = es_p4.enter_context(nc.sbuf_tensor("ptile4", [128, STT * 1024], BF16))
                pf4 = es_p4.enter_context(nc.sbuf_tensor("pf4", [128, KC * ST], BF16))
                xld = es_p4.enter_context(nc.sbuf_tensor("xld", [128, 2 * 1024], F32))
                ps4 = es_p4.enter_context(nc.psum_tensor("ps4", [128, 4096], F32))
                wlx = (P.w[6], P.w[7])
                for kc in range(KC):
                    P.sy(lambda e, kc=kc: e.dma_start(
                        out=wq[:, kc * 1024:(kc + 1) * 1024],
                        in_=kTb["p"][128 * kc:128 * (kc + 1), :]), inc=P.sems["wa"])
                for nc_ in range(KC):
                    P.sy(lambda e, nc_=nc_: e.dma_start(
                        out=wq[:, 8192 + nc_ * 1024: 8192 + (nc_ + 1) * 1024],
                        in_=vVb["p"][128 * nc_:128 * (nc_ + 1), :]), inc=P.sems["wa"])
                pw_ld = P.sems["wa"].n
                L2 = SMX + 16
                tpp4 = {}
                exp_pe4 = {}
                xld_free = {}
                xld_ld = {}
                prev_xi = 0
                def load_xld(gt):
                    par = gt % 2
                    w = [(ve, xld_free.get(par, 0))]
                    if gt < 2:
                        w.append((sto, stash_total))
                    P.sy(lambda e, gt=gt, par=par: e.dma_start(
                        out=xld[:, par * 1024:(par + 1) * 1024],
                        in_=xstash[128 * gt:128 * (gt + 1), :]),
                        waits=w, inc=wlx[par])
                    xld_ld[gt] = (wlx[par], wlx[par].n)

                for sidx in range(NS):
                    for t2 in range(min(2, STT)):
                        load_xld(sidx * STT + t2)
                    for t2 in range(STT):
                        bk = (t2 % 2) * 1024
                        for nh in range(2):
                            for kc in range(KC):
                                w = []
                                if nh == 0 and kc == 0:
                                    w = [(P.sems["wa"], pw_ld), (ac, ofm_ac), (ac, exp_pe4.get(t2 % 2, 0))]
                                P.t(lambda e, kc=kc, nh=nh, bk=bk, sidx=sidx, t2=t2: e.matmul(
                                    ps4[:, bk + nh * 512: bk + (nh + 1) * 512],
                                    ofm[:, kc * T + sidx * ST + t2 * 128: kc * T + sidx * ST + (t2 + 1) * 128],
                                    wq[:, kc * 1024 + nh * 512: kc * 1024 + (nh + 1) * 512],
                                    start=(kc == 0), stop=(kc == KC - 1)),
                                    waits=w, inc=(pe if (nh == 1 and kc == KC - 1) else None))
                        scp = pe.n
                        rm = P.v(lambda e, bk=bk: e.reduce_max(stat[:, SMX:SMX + 1],
                                                               ps4[:, bk:bk + 1024], axis=AX.X),
                                 waits=[(pe, scp), (ve, ve.n)], inc=ve)
                        rb = P.v(lambda e: e.tensor_scalar(stat[:, SMX + 1:SMX + 2], stat[:, SMX:SMX + 1],
                                                           -1.0 / 32.0, None, ALU.mult),
                                 waits=[(ve, rm)], inc=ve)
                        exq = P.s(lambda e, bk=bk, t2=t2: e.activation(
                            ptile4[:, t2 * 1024:(t2 + 1) * 1024], ps4[:, bk:bk + 1024],
                            AF.Exp, scale=1.0 / 32.0, bias=stat[:, SMX + 1:SMX + 2],
                            accum_out=stat[:, SMX + 2:SMX + 3]),
                            waits=[(ve, rb), (ac, ac.n), (pe, tpp4.get(t2, 0))], inc=ac)
                        exp_pe4[t2 % 2] = exq
                        P.v(lambda e, t2=t2: e.reciprocal(stat[:, L2 + t2:L2 + t2 + 1],
                                                          stat[:, SMX + 2:SMX + 3]),
                            waits=[(ac, exq), (ve, ve.n)], inc=ve)

                        def evp4(g, psap, ws, t2=t2):
                            dap = pf4[:, :].rearrange("p (c t3) -> p c t3", t3=ST)[
                                :, 4 * (g % 2):4 * (g % 2) + 4, t2 * 128:(t2 + 1) * 128]
                            return (ve, P.v(lambda e, psap=psap, dap=dap: e.tensor_copy(dap, psap),
                                            waits=ws, inc=ve))

                        tpp4[t2] = transpose_run(
                            [ptile4[:, t2 * 1024 + c * 128: t2 * 1024 + (c + 1) * 128] for c in range(KC)],
                            ps4[:, 2048:3072], 2, evp4, extra_first=[(ac, exq)])
                    pf_ve = ve.n
                    ev4 = {}
                    for t2 in range(STT):
                        gt = sidx * STT + t2
                        bk2 = 2048 + (t2 % 2) * 1024
                        for nc_ in range(KC):
                            for nh in range(2):
                                w = []
                                if nc_ == 0 and nh == 0:
                                    w = [(ve, pf_ve)]
                                    if (t2 % 2) in ev4:
                                        w.append((ve, ev4[t2 % 2]))
                                P.t(lambda e, t2=t2, nc_=nc_, nh=nh, bk2=bk2: e.matmul(
                                    ps4[:, bk2 + nh * 512: bk2 + (nh + 1) * 512],
                                    pf4[:, nc_ * ST + t2 * 128: nc_ * ST + (t2 + 1) * 128],
                                    wq[:, 8192 + nc_ * 1024 + nh * 512: 8192 + nc_ * 1024 + (nh + 1) * 512],
                                    start=(nc_ == 0), stop=(nc_ == KC - 1)),
                                    waits=w, inc=(pe if (nc_ == KC - 1 and nh == 1) else None))
                        avp4 = pe.n
                        ai = P.s(lambda e, gt=gt: e.activation(
                            xb[:, gt * 1024:(gt + 1) * 1024], xld[:, (gt % 2) * 1024:((gt % 2) + 1) * 1024],
                            AF.Identity, scale=stat[:, R1 + gt:R1 + gt + 1],
                            bias=stat[:, MB1 + gt:MB1 + gt + 1]),
                            waits=[xld_ld[gt], (ac, ac.n)], inc=ac)
                        ev4[t2 % 2] = P.v(lambda e, gt=gt, bk2=bk2, t2=t2: e.scalar_tensor_tensor(
                            xb[:, gt * 1024:(gt + 1) * 1024], ps4[:, bk2:bk2 + 1024],
                            stat[:, L2 + t2:L2 + t2 + 1], xb[:, gt * 1024:(gt + 1) * 1024],
                            ALU.mult, ALU.add),
                            waits=[(pe, avp4), (ac, ai), (ve, ve.n)], inc=ve)
                        xld_free[gt % 2] = ev4[t2 % 2]
                        if t2 + 2 < STT:
                            load_xld(gt + 2)
                xb_ve = ve.n
            es3.close()
            P.barrier()


            P.mark("l1a.P5")
            es_p5 = contextlib.ExitStack()
            with es_p5:
                ptile5 = es_p5.enter_context(nc.sbuf_tensor("ptile5", [128, 2 * 2048], BF16))
                ptr5 = es_p5.enter_context(nc.sbuf_tensor("ptr5", [128, STT * 16 * 128], BF16))
                h2fs = es_p5.enter_context(nc.sbuf_tensor("h2fs", [128, STT * 1024], BF16))
                pvh = es_p5.enter_context(nc.sbuf_tensor("pvh", [128, STT * 1024], F32))
                fvs = es_p5.enter_context(nc.sbuf_tensor("fvs", [128, 2 * 1024], BF16))
                xld2 = es_p5.enter_context(nc.sbuf_tensor("xld2", [128, 2 * 1024], F32))
                ps5 = es_p5.enter_context(nc.psum_tensor("ps5", [128, 4096], F32))
                wlf = (P.sems["wa"], P.sems["wb"])
                wlv = (P.sems["wc"], P.sems["wd"])
                wlx2 = (P.w[6], P.w[7])
                FH = SMX + 24
                fk_free = {}
                fk_ld = {}
                fv_free = {}
                fv_ld = {}
                x2_free = {}
                x2_ld = {}
                gfk = 0
                gfv = 0
                prev_sc5 = 0
                prev_tp5 = 0
                prev_ep5 = 0
                h2_tp = {}
                for sidx in range(NS):
                    for t2 in range(STT):
                        gt = sidx * STT + t2
                        half2 = t2 % 2
                        w = [(ve, xb_ve), (ac, ac.n), (ve, prev_ep5)]
                        vr, ar = ln_small(xb[:, gt * 1024:(gt + 1) * 1024], SMX)
                        w2_ = [(ve, vr), (ac, ar), (ac, ac.n)]
                        if t2 >= 2:
                            w2_.append((pe, h2_tp[t2 - 2]))
                        aa = P.s(lambda e, gt=gt, half2=half2: e.activation(
                            nrm[:, half2 * 1024:(half2 + 1) * 1024], xb[:, gt * 1024:(gt + 1) * 1024],
                            AF.Identity, scale=stat[:, SMX + 3:SMX + 4], bias=stat[:, SMX + 4:SMX + 5]),
                            waits=w2_, inc=ac)

                        def evh2(g, psap, ws, t2=t2):
                            dap = h2fs[:, :].rearrange("p (t3 c) -> p t3 c", c=1024)[
                                :, t2, 4 * (g % 2) * 128:(4 * (g % 2) + 4) * 128]
                            return (ve, P.v(lambda e, psap=psap, dap=dap: e.tensor_copy(dap, psap),
                                            waits=ws, inc=ve))

                        h2_tp[t2] = transpose_run(
                            [nrm[:, half2 * 1024 + c * 128: half2 * 1024 + (c + 1) * 128]
                             for c in range(KC)],
                            ps5[:, 0:1024], 2, evh2, extra_first=[(ac, aa), (ve, prev_sc5)])
                    h2_ve = ve.n
                    for half in range(2):
                        parf = gfk % 2
                        w = [(pe, fk_free.get(parf, 0))]
                        if gfk < 2:
                            w += [(P.w[0], es.fkv_done[0]), (P.w[1], es.fkv_done[1])]
                        for kc in range(KC):
                            P.sy(lambda e, kc=kc, half=half: e.dma_start(
                                out=wq[:, kc * 2048:(kc + 1) * 2048],
                                in_=fKb[128 * kc:128 * (kc + 1), half * 2048:(half + 1) * 2048]),
                                waits=(w if kc == 0 else []), inc=wlf[parf])
                        fk_ld[gfk] = (wlf[parf], wlf[parf].n)
                        half_free = {}
                        ep_half = {}
                        tpd_half = {}

                        def p5_scores(t2, half=half, gfk=gfk):
                            hp5 = t2 % 2
                            for wd in range(4):
                                for kc in range(KC):
                                    w = []
                                    if wd == 0 and kc == 0:
                                        w = [fk_ld[gfk], (ve, h2_ve), (ve, prev_ep5),
                                             ep_half.get(hp5, (ac, 0)),
                                             half_free.get(hp5, (ve, 0))]
                                    P.t(lambda e, t2=t2, wd=wd, kc=kc, hp5=hp5: e.matmul(
                                        ps5[:, wd * 512 + hp5 * 2048: wd * 512 + hp5 * 2048 + 512],
                                        h2fs[:, t2 * 1024 + kc * 128: t2 * 1024 + (kc + 1) * 128],
                                        wq[:, kc * 2048 + wd * 512: kc * 2048 + (wd + 1) * 512],
                                        start=(kc == 0), stop=(kc == KC - 1)),
                                        waits=w, inc=(pe if kc == KC - 1 else None))
                                P.v(lambda e, wd=wd, hp5=hp5: e.reduce_max(
                                    stat[:, SMX + 8 + hp5 * 4 + wd:SMX + 9 + hp5 * 4 + wd],
                                    ps5[:, wd * 512 + hp5 * 2048: wd * 512 + hp5 * 2048 + 512],
                                    axis=AX.X),
                                    waits=[(pe, pe.n), (ve, ve.n)], inc=ve)
                            return ve.n

                        def p5_soft(t2, wmx, half=half):
                            nonlocal prev_tp5
                            hp5 = t2 % 2
                            vm = P.v(lambda e, t2=t2, half=half, hp5=hp5: e.reduce_max(
                                stat[:, FH + half * 16 + t2:FH + half * 16 + t2 + 1],
                                stat[:, SMX + 8 + hp5 * 4:SMX + 12 + hp5 * 4], axis=AX.X),
                                waits=[(ve, wmx), (ve, ve.n)], inc=ve)
                            vb = P.v(lambda e, t2=t2, half=half, hp5=hp5: e.tensor_scalar(
                                stat[:, SMX + 16 + hp5:SMX + 17 + hp5],
                                stat[:, FH + half * 16 + t2:FH + half * 16 + t2 + 1],
                                -1.0 / 32.0, None, ALU.mult),
                                waits=[(ve, vm)], inc=ve)
                            for wd in range(4):
                                P.s(lambda e, wd=wd, hp5=hp5: e.activation(
                                    ptile5[:, hp5 * 2048 + wd * 512: hp5 * 2048 + (wd + 1) * 512],
                                    ps5[:, wd * 512 + hp5 * 2048: wd * 512 + hp5 * 2048 + 512],
                                    AF.Exp, scale=1.0 / 32.0, bias=stat[:, SMX + 16 + hp5:SMX + 17 + hp5],
                                    accum_out=stat[:, SMX + 8 + hp5 * 4 + wd:SMX + 9 + hp5 * 4 + wd]),
                                    waits=[(ve, vb), (ac, ac.n), tpd_half.get(hp5, (pe, 0))], inc=ac)
                            ex5 = ac.n
                            ep_half[hp5] = (ac, ex5)
                            P.v(lambda e, t2=t2, half=half, hp5=hp5: e.reduce_sum(
                                stat[:, FH + 32 + half * 16 + t2:FH + 32 + half * 16 + t2 + 1],
                                stat[:, SMX + 8 + hp5 * 4:SMX + 12 + hp5 * 4], axis=AX.X),
                                waits=[(ac, ex5), (ve, ve.n)], inc=ve)
                            tslot = {}
                            for j0 in range(0, 16, 4):
                                slot = (j0 // 4) % 2
                                w = [(ac, ex5)]
                                if slot in tslot:
                                    w.append((ve, tslot[slot]))
                                for jj in range(4):
                                    P.t(lambda e, j0=j0, jj=jj, slot=slot, t2=t2, hp5=hp5: e.matmul(
                                        ps5[:, 1024 + hp5 * 2048 + slot * 512 + jj * 128:
                                            1024 + hp5 * 2048 + slot * 512 + (jj + 1) * 128],
                                        ptile5[:, hp5 * 2048 + (j0 + jj) * 128: hp5 * 2048 + (j0 + jj + 1) * 128],
                                        ident[:, :], start=True, stop=True),
                                        waits=(w if jj == 0 else []),
                                        inc=(pe if jj == 3 else None))
                                tslot[slot] = P.v(lambda e, j0=j0, slot=slot, t2=t2, hp5=hp5: e.tensor_copy(
                                    ptr5[:, t2 * 2048 + j0 * 128: t2 * 2048 + (j0 + 4) * 128],
                                    ps5[:, 1024 + hp5 * 2048 + slot * 512:
                                        1024 + hp5 * 2048 + slot * 512 + 512]),
                                    waits=[(pe, pe.n), (ve, ve.n)], inc=ve)
                            tpd_half[hp5] = (pe, pe.n)
                            prev_tp5 = pe.n
                            half_free[hp5] = (ve, ve.n)

                        wmxs = {0: p5_scores(0)}
                        for t2 in range(STT):
                            if t2 + 1 < STT:
                                wmxs[t2 + 1] = p5_scores(t2 + 1)
                            p5_soft(t2, wmxs.pop(t2))
                        ptr_ve5 = ve.n
                        prev_sc5 = ptr_ve5
                        pv_done = {}
                        for nc2 in range(16):
                            parv = gfv % 2
                            w = [(pe, fv_free.get(parv, 0))]
                            if gfv < 2:
                                w += [(P.w[0], es.fkv_done[0]), (P.w[1], es.fkv_done[1])]
                            P.sy(lambda e, nc2=nc2, half=half, parv=parv: e.dma_start(
                                out=fvs[:, parv * 1024:(parv + 1) * 1024],
                                in_=fVb[half * 2048 + 128 * nc2: half * 2048 + 128 * (nc2 + 1), :]),
                                waits=w, inc=wlv[parv])
                            fv_ld[gfv] = (wlv[parv], wlv[parv].n)
                            for t2 in range(STT):
                                for nh in range(2):
                                    w = []
                                    if nh == 0:
                                        w = [fv_ld[gfv]]
                                        if nc2 == 0:
                                            w += [(ve, ptr_ve5), (ve, prev_ep5)]
                                    P.t(lambda e, nc2=nc2, t2=t2, nh=nh, parv=parv: e.matmul(
                                        ps5[:, t2 * 1024 + nh * 512: t2 * 1024 + (nh + 1) * 512],
                                        ptr5[:, t2 * 2048 + nc2 * 128: t2 * 2048 + (nc2 + 1) * 128],
                                        fvs[:, parv * 1024 + nh * 512: parv * 1024 + (nh + 1) * 512],
                                        start=(nc2 == 0), stop=(nc2 == 15)),
                                        waits=w,
                                        inc=(pe if (t2 == STT - 1 and nh == 1) else None))
                            fv_free[parv] = pe.n
                            gfv += 1
                        pv_pe = pe.n
                        fk_free[parf] = pe.n
                        gfk += 1
                        if half == 0:
                            stv = 0
                            for t2 in range(STT):
                                stv = P.v(lambda e, t2=t2: e.tensor_copy(
                                    pvh[:, t2 * 1024:(t2 + 1) * 1024], ps5[:, t2 * 1024:(t2 + 1) * 1024]),
                                    waits=[(pe, pv_pe), (ve, ve.n)], inc=ve)
                            prev_ep5 = stv
                        else:
                            def load_xld2(gt):
                                par2 = gt % 2
                                w = [(ve, x2_free.get(par2, 0))]
                                if gt < 2:
                                    w.append((sto, stash_total))
                                P.sy(lambda e, gt=gt, par2=par2: e.dma_start(
                                    out=xld2[:, par2 * 1024:(par2 + 1) * 1024],
                                    in_=xstash[128 * gt:128 * (gt + 1), :]),
                                    waits=w, inc=wlx2[par2])
                                x2_ld[gt] = (wlx2[par2], wlx2[par2].n)

                            for t2 in range(min(2, STT)):
                                load_xld2(sidx * STT + t2)
                            ep_last = 0
                            for t2 in range(STT):
                                gt = sidx * STT + t2
                                c0 = FH + t2
                                c1 = FH + 16 + t2
                                c2 = FH + 32 + t2
                                c3 = FH + 48 + t2
                                v1 = P.v(lambda e, c0=c0, c1=c1: e.tensor_max(
                                    stat[:, SMX:SMX + 1], stat[:, c0:c0 + 1], stat[:, c1:c1 + 1]),
                                    waits=[(ve, ve.n)], inc=ve)
                                v2 = P.v(lambda e, c0=c0: e.tensor_sub(
                                    stat[:, SMX + 1:SMX + 2], stat[:, c0:c0 + 1], stat[:, SMX:SMX + 1]),
                                    waits=[(ve, v1)], inc=ve)
                                v3 = P.v(lambda e, c1=c1: e.tensor_sub(
                                    stat[:, SMX + 2:SMX + 3], stat[:, c1:c1 + 1], stat[:, SMX:SMX + 1]),
                                    waits=[(ve, v2)], inc=ve)
                                a1_ = P.s(lambda e: e.activation(
                                    stat[:, SMX + 1:SMX + 3], stat[:, SMX + 1:SMX + 3],
                                    AF.Exp, scale=1.0 / 32.0),
                                    waits=[(ve, v3), (ac, ac.n)], inc=ac)
                                v4 = P.v(lambda e, c2=c2, c3=c3: e.tensor_mul(
                                    stat[:, SMX + 4:SMX + 5], stat[:, c2:c2 + 1], stat[:, SMX + 1:SMX + 2]),
                                    waits=[(ac, a1_), (ve, ve.n)], inc=ve)
                                v5 = P.v(lambda e, c3=c3: e.scalar_tensor_tensor(
                                    stat[:, SMX + 4:SMX + 5], stat[:, c3:c3 + 1], 1.0,
                                    stat[:, SMX + 4:SMX + 5], ALU.bypass, ALU.mult)
                                    if False else e.tensor_mul(
                                    stat[:, SMX + 5:SMX + 6], stat[:, c3:c3 + 1], stat[:, SMX + 2:SMX + 3]),
                                    waits=[(ve, v4)], inc=ve)
                                v6 = P.v(lambda e: e.tensor_add(
                                    stat[:, SMX + 4:SMX + 5], stat[:, SMX + 4:SMX + 5],
                                    stat[:, SMX + 5:SMX + 6]),
                                    waits=[(ve, v5)], inc=ve)
                                v7 = P.v(lambda e: e.reciprocal(stat[:, SMX + 6:SMX + 7],
                                                                stat[:, SMX + 4:SMX + 5]),
                                         waits=[(ve, v6)], inc=ve)
                                v8 = P.v(lambda e: e.tensor_mul(stat[:, SMX + 1:SMX + 2],
                                                                stat[:, SMX + 1:SMX + 2],
                                                                stat[:, SMX + 6:SMX + 7]),
                                         waits=[(ve, v7)], inc=ve)
                                v9 = P.v(lambda e: e.tensor_mul(stat[:, SMX + 2:SMX + 3],
                                                                stat[:, SMX + 2:SMX + 3],
                                                                stat[:, SMX + 6:SMX + 7]),
                                         waits=[(ve, v8)], inc=ve)
                                y1 = P.v(lambda e, t2=t2, gt=gt: e.scalar_tensor_tensor(
                                    ysc[:, :], pvh[:, t2 * 1024:(t2 + 1) * 1024],
                                    stat[:, SMX + 1:SMX + 2], xb[:, gt * 1024:(gt + 1) * 1024],
                                    ALU.mult, ALU.add),
                                    waits=[(ve, v9), (ve, ep_last)], inc=ve)
                                y2 = P.v(lambda e, t2=t2: e.scalar_tensor_tensor(
                                    ysc[:, :], ps5[:, t2 * 1024:(t2 + 1) * 1024],
                                    stat[:, SMX + 2:SMX + 3], ysc[:, :],
                                    ALU.mult, ALU.add),
                                    waits=[(ve, y1), (pe, pv_pe)], inc=ve)
                                vr, ar = ln_small(ysc[:, :], SMX + 8, [(ve, y2)])
                                v10 = P.v(lambda e, gt=gt, par2=gt % 2: e.tensor_scalar(
                                    xld2[:, par2 * 1024:(par2 + 1) * 1024],
                                    xld2[:, par2 * 1024:(par2 + 1) * 1024],
                                    stat[:, SMX + 8 + 4:SMX + 8 + 5], None, ALU.add),
                                    waits=[(ve, vr), x2_ld[gt]], inc=ve)
                                ep_last = P.v(lambda e, gt=gt, par2=gt % 2: e.scalar_tensor_tensor(
                                    xb[:, gt * 1024:(gt + 1) * 1024], ysc[:, :],
                                    stat[:, SMX + 8 + 3:SMX + 8 + 4],
                                    xld2[:, par2 * 1024:(par2 + 1) * 1024],
                                    ALU.mult, ALU.add),
                                    waits=[(ve, v10)], inc=ve)
                                x2_free[gt % 2] = ep_last
                                if t2 + 2 < STT:
                                    load_xld2(gt + 2)
                            prev_ep5 = ep_last
                xb_ve2 = ve.n
            P.barrier()


        if "l1f" in blocks:
            P.mark("l1f")
            es_g = contextlib.ExitStack()
            with es_g:
                xhs2 = es_g.enter_context(nc.sbuf_tensor("xhs2", [128, 2 * KC * ST], BF16))
                w3sb = es_g.enter_context(nc.sbuf_tensor("w3sb", [128, C], F32))
                P.sy(lambda e: e.dma_start(out=w3sb[:, :], in_=w3sb_in[:, :]), inc=P.sems["ld"])
                w3sb_ld = P.sems["ld"].n
                Ab2 = es_g.enter_context(nc.sbuf_tensor("Ab2", [128, FC * ST], BF16))
                Bb2 = es_g.enter_context(nc.sbuf_tensor("Bb2", [128, FC * ST], BF16))
                wfc2 = es_g.enter_context(nc.sbuf_tensor("wfc2", [128, 4 * 1024], BF16))
                ps_t3 = es_g.enter_context(nc.psum_tensor("ps_t3", [128, 1024], F32))
                ps_m2 = es_g.enter_context(nc.psum_tensor("ps_m2", [128, 2048], F32))
                ps_o2 = es_g.enter_context(nc.psum_tensor("ps_o2", [128, 1024], F32))
                gl1 = (P.w[6], P.w[7])
                gl2 = (P.sems["wa"], P.sems["wb"])
                gl3 = (P.sems["wc"], P.sems["wd"])
                m1_free = {}
                m2_free = {}
                m3_free = {}
                m1_ld = {}
                m2_ld = {}
                m3_ld = {}

                def l1f_mm1_slab(gi, j, srcw):
                    par = gi % 2
                    w = [(pe, m1_free.get(par, 0))]
                    if gi < 2 and hasattr(es, "h12_tile_done"):
                        w += [(P.w[0], es.h12_tile_done[0]), (P.w[1], es.h12_tile_done[1])]
                    for kc in range(KC):
                        P.sy(lambda e, j=j, kc=kc, par=par, srcw=srcw: e.dma_start(
                            out=wq[:, par * 4096 + kc * 512: par * 4096 + (kc + 1) * 512],
                            in_=srcw[128 * kc:128 * (kc + 1), 512 * j:512 * (j + 1)]),
                            waits=(w if kc == 0 else []), inc=gl1[par])
                    m1_ld[gi] = (gl1[par], gl1[par].n)
                    return par

                def l1f_mm2_slab(gi, mc, srcw):
                    par = gi % 2
                    w = [(pe, m2_free.get(par, 0))]
                    if gi < 2 and hasattr(es, "h12_tile_done"):
                        w += [(P.w[0], es.h12_tile_done[0]), (P.w[1], es.h12_tile_done[1])]
                    P.sy(lambda e, mc=mc, par=par, srcw=srcw: e.dma_start(
                        out=wq[:, 8192 + par * 4096: 8192 + (par + 1) * 4096].rearrange(
                            "p (k c) -> p k c", c=128),
                        in_=srcw[mc * F: (mc + 1) * F, :].rearrange("(k p) c -> p k c", p=128)),
                        waits=w, inc=gl2[par])
                    m2_ld[gi] = (gl2[par], gl2[par].n)
                    return par

                def l1f_mm3_slab(gi, kc2):
                    par = gi % 2
                    w = [(pe, m3_free.get(par, 0))]
                    if gi < 2 and hasattr(es, "h12_tile_done"):
                        w += [(P.w[0], es.h12_tile_done[0]), (P.w[1], es.h12_tile_done[1])]
                    P.sy(lambda e, kc2=kc2, par=par: e.dma_start(
                        out=wfc2[:, par * 2048:(par + 1) * 2048].rearrange(
                            "p (k c) -> p k c", c=1024),
                        in_=w3b[128 * kc2:128 * (kc2 + 2), :].rearrange(
                            "(k p) c -> p k c", p=128)),
                        waits=w, inc=gl3[par])
                    m3_ld[gi] = (gl3[par], gl3[par].n)
                    return par

                g1 = g2 = g3 = 0
                prev_eva = 0

                def l1f_ln(sidx):
                    par = sidx % 2
                    wv = wa = 0
                    for t2 in range(STT):
                        gt = sidx * STT + t2
                        wv, wa = ln_tile_stats(xb[:, gt * 1024:(gt + 1) * 1024], t2)
                    ln_finalize(STT, wv, wa)
                    mbv, rca = ve.n, ac.n
                    tpa = {}
                    for t2 in range(STT):
                        half = t2 % 2
                        gt = sidx * STT + t2
                        w = [(ve, mbv), (ac, rca), (ac, ac.n)]
                        if t2 >= 2:
                            w.append((pe, tpa[t2 - 2]))
                        aa = P.s(lambda e, gt=gt, half=half, t2=t2: e.activation(
                            nrm[:, half * 1024:(half + 1) * 1024], xb[:, gt * 1024:(gt + 1) * 1024],
                            AF.Identity, scale=stat[:, RC + t2:RC + t2 + 1],
                            bias=stat[:, MB + t2:MB + t2 + 1]),
                            waits=w, inc=ac)

                        def evacx2(g, psap, ws, t2=t2, par=par):
                            dap = xhs2[:, par * KC * ST:(par + 1) * KC * ST].rearrange(
                                "p (c t3) -> p c t3", t3=ST)[
                                :, 4 * (g % 2):4 * (g % 2) + 4, t2 * 128:(t2 + 1) * 128]
                            return (ve, P.v(lambda e, psap=psap, dap=dap: e.tensor_copy(dap, psap),
                                            waits=ws, inc=ve))

                        tpa[t2] = transpose_run(
                            [nrm[:, half * 1024 + c * 128: half * 1024 + (c + 1) * 128] for c in range(KC)],
                            ps_t3, 2, evacx2, extra_first=[(ac, aa)])
                    return ve.n

                xh2_ve_n = {0: l1f_ln(0)}
                prev_l1f_mm2b = 0
                for sidx in range(NS):
                    par0 = sidx % 2
                    xhs_ve = xh2_ve_n[sidx]

                    def xs2_ap(kc, par0=par0):
                        return xhs2[:, par0 * KC * ST + kc * ST: par0 * KC * ST + (kc + 1) * ST]
                    ev1 = {}
                    for j in range(8):
                        par = l1f_mm1_slab(g1, j, w1b)
                        for m4 in range(4):
                            mc = j * 4 + m4
                            bank = mc % 4
                            for kc in range(KC):
                                w = []
                                if kc == 0:
                                    w = [m1_ld[g1], (ve, xhs_ve), (ac, prev_l1f_mm2b), (ve, prev_eva)]
                                    if bank in ev1:
                                        w.append((ac, ev1[bank]))
                                P.t(lambda e, par=par, kc=kc, m4=m4, bank=bank, xs2_ap=xs2_ap: e.matmul(
                                    ps_m2[:, bank * 512: bank * 512 + ST],
                                    wq[:, par * 4096 + kc * 512 + m4 * 128: par * 4096 + kc * 512 + (m4 + 1) * 128],
                                    xs2_ap(kc),
                                    start=(kc == 0), stop=(kc == KC - 1)),
                                    waits=w, inc=(pe if kc == KC - 1 else None))
                            if m4 == 3:
                                m1_free[par] = pe.n
                            ev1[bank] = P.s(lambda e, mc=mc, bank=bank: e.activation(
                                Ab2[:, mc * ST:(mc + 1) * ST], ps_m2[:, bank * 512: bank * 512 + ST],
                                AF.Silu, scale=scv[:, mc:mc + 1]),
                                waits=[(pe, pe.n), (ac, ac.n), (ld, init_ld)], inc=ac)
                        g1 += 1
                    mm1a_ac = ac.n
                    ev1b = {}
                    for j in range(8):
                        par = l1f_mm1_slab(g1, j, w2b)
                        for m4 in range(4):
                            mc = j * 4 + m4
                            bank = mc % 4
                            for kc in range(KC):
                                w = []
                                if kc == 0:
                                    w = [m1_ld[g1], (ve, xhs_ve), (ac, mm1a_ac)]
                                    if bank in ev1b:
                                        w.append((ve, ev1b[bank]))
                                P.t(lambda e, par=par, kc=kc, m4=m4, bank=bank, xs2_ap=xs2_ap: e.matmul(
                                    ps_m2[:, bank * 512: bank * 512 + ST],
                                    wq[:, par * 4096 + kc * 512 + m4 * 128: par * 4096 + kc * 512 + (m4 + 1) * 128],
                                    xs2_ap(kc),
                                    start=(kc == 0), stop=(kc == KC - 1)),
                                    waits=w, inc=(pe if kc == KC - 1 else None))
                            if m4 == 3:
                                m1_free[par] = pe.n
                            ev1b[bank] = P.v(lambda e, mc=mc, bank=bank: e.scalar_tensor_tensor(
                                Bb2[:, mc * ST:(mc + 1) * ST], ps_m2[:, bank * 512: bank * 512 + ST],
                                scv[:, 32 + mc:33 + mc], Ab2[:, mc * ST:(mc + 1) * ST],
                                ALU.mult, ALU.mult),
                                waits=[(pe, pe.n), (ac, mm1a_ac), (ve, ve.n)], inc=ve)
                        g1 += 1
                    mm1b_ve = ve.n
                    if sidx + 1 < NS:
                        xh2_ve_n[sidx + 1] = l1f_ln(sidx + 1)
                    ev2 = {}
                    for mc in range(FC):
                        par = l1f_mm2_slab(g2, mc, h1s_t)
                        bank = mc % 4
                        for kc2 in range(FC):
                            w = []
                            if kc2 == 0:
                                w = [m2_ld[g2], (ve, mm1b_ve)]
                                if bank in ev2:
                                    w.append((ac, ev2[bank]))
                            P.t(lambda e, par=par, kc2=kc2, bank=bank: e.matmul(
                                ps_m2[:, bank * 512: bank * 512 + ST],
                                wq[:, 8192 + par * 4096 + kc2 * 128: 8192 + par * 4096 + (kc2 + 1) * 128],
                                Bb2[:, kc2 * ST: (kc2 + 1) * ST],
                                start=(kc2 == 0), stop=(kc2 == FC - 1)),
                                waits=w, inc=(pe if kc2 == FC - 1 else None))
                        m2_free[par] = pe.n
                        ev2[bank] = P.s(lambda e, mc=mc, bank=bank: e.activation(
                            Ab2[:, mc * ST:(mc + 1) * ST], ps_m2[:, bank * 512: bank * 512 + ST],
                            AF.Gelu, scale=scv[:, 64 + mc:65 + mc]),
                            waits=[(pe, pe.n), (ac, ac.n), (ve, mm1b_ve)], inc=ac)
                        g2 += 1
                    mm2a_ac = ac.n
                    ev2b = {}
                    for mc in range(FC):
                        par = l1f_mm2_slab(g2, mc, h2s_t)
                        bank = mc % 4
                        for kc2 in range(FC):
                            w = []
                            if kc2 == 0:
                                w = [m2_ld[g2], (ac, mm2a_ac)]
                                if bank in ev2b:
                                    w.append((ac, ev2b[bank]))
                            P.t(lambda e, par=par, kc2=kc2, bank=bank: e.matmul(
                                ps_m2[:, bank * 512: bank * 512 + ST],
                                wq[:, 8192 + par * 4096 + kc2 * 128: 8192 + par * 4096 + (kc2 + 1) * 128],
                                Ab2[:, kc2 * ST: (kc2 + 1) * ST],
                                start=(kc2 == 0), stop=(kc2 == FC - 1)),
                                waits=w, inc=(pe if kc2 == FC - 1 else None))
                        m2_free[par] = pe.n
                        ev2b[bank] = P.s(lambda e, mc=mc, bank=bank: e.activation(
                            Bb2[:, mc * ST:(mc + 1) * ST], ps_m2[:, bank * 512: bank * 512 + ST],
                            AF.Gelu, scale=scv[:, 96 + mc:97 + mc]),
                            waits=[(pe, pe.n), (ac, ac.n)], inc=ac)
                        g2 += 1
                    mm2b_ac = ac.n
                    prev_l1f_mm2b = mm2b_ac
                    for hf in range(STT // 2):
                        for kc2 in range(FC):
                            if kc2 % 2 == 0:
                                par = l1f_mm3_slab(g3, kc2)
                            sub = (kc2 % 2) * 1024
                            for ti in range(2):
                                t2 = hf * 2 + ti
                                for n2 in range(2):
                                    w = []
                                    if kc2 == 0 and ti == 0 and n2 == 0:
                                        w = [(ac, mm2b_ac), (ve, prev_eva)]
                                    if kc2 % 2 == 0 and ti == 0 and n2 == 0:
                                        w = w + [m3_ld[g3]]
                                    P.t(lambda e, par=par, kc2=kc2, t2=t2, ti=ti, n2=n2, sub=sub: e.matmul(
                                        ps_m2[:, ti * 1024 + n2 * 512: ti * 1024 + (n2 + 1) * 512],
                                        Bb2[:, kc2 * ST + t2 * 128: kc2 * ST + (t2 + 1) * 128],
                                        wfc2[:, par * 2048 + sub + n2 * 512: par * 2048 + sub + (n2 + 1) * 512],
                                        start=(kc2 == 0), stop=(kc2 == FC - 1)),
                                        waits=w,
                                        inc=(pe if (ti == 1 and n2 == 1) else None))
                            if kc2 % 2 == 1:
                                m3_free[par] = pe.n
                                g3 += 1
                        mmp = pe.n
                        for ti in range(2):
                            t2 = hf * 2 + ti
                            gt = sidx * STT + t2
                            ya = P.v(lambda e, ti=ti: e.tensor_mul(
                                ysc[:, :], ps_m2[:, ti * 1024:(ti + 1) * 1024], w3sb[:, :]),
                                waits=[(pe, mmp), (P.sems["ld"], w3sb_ld), (ve, prev_eva), (ve, ve.n)], inc=ve)
                            vr, ar = ln_small(ysc[:, :], SMX, [(ve, ya)])
                            v6 = P.v(lambda e, gt=gt: e.tensor_scalar(
                                xb[:, gt * 1024:(gt + 1) * 1024], xb[:, gt * 1024:(gt + 1) * 1024],
                                stat[:, SMX + 4:SMX + 5], None, ALU.add),
                                waits=[(ve, vr)], inc=ve)
                            prev_eva = P.v(lambda e, gt=gt: e.scalar_tensor_tensor(
                                xb[:, gt * 1024:(gt + 1) * 1024], ysc[:, :], stat[:, SMX + 3:SMX + 4],
                                xb[:, gt * 1024:(gt + 1) * 1024], ALU.mult, ALU.add),
                                waits=[(ve, v6)], inc=ve)
            P.barrier()

        def store_residual(buf):
            for tt in range(NT):
                P.sy(lambda e, tt=tt: e.dma_start(out=yout[128 * tt:128 * (tt + 1), :],
                                                  in_=buf[:, tt * 1024:(tt + 1) * 1024]),
                     waits=[(P.ve, P.ve.n), (P.ac, P.ac.n), (P.pe, P.pe.n)], inc=sto)
            P.sy(None, waits=[(sto, sto.n)])

        store_residual(xb if "l1a" in blocks else xr)

        semh = {nm: es.enter_context(nc.semaphore(nm)) for nm in P.sems}
        P.replay(nc, semh)
    build_nc.last_marks = P.marks
    return nc



def prep_weights(i):
    C, F = 1024, 4096
    d = {}
    d["wqkvT"] = np.ascontiguousarray(np.asarray(i["L0_attn_W_qkv"]).T)
    d["wprojT"] = np.ascontiguousarray(np.asarray(i["L0_attn_W_proj"]).T)
    d["bpj_d"] = np.ascontiguousarray(np.broadcast_to(np.asarray(i["L0_attn_b_proj"])[None, :], (128, C)))
    d["k0"] = np.ascontiguousarray(np.asarray(i["L0_ffn_k0"]).view(np.uint8))
    d["k1"] = np.ascontiguousarray(np.asarray(i["L0_ffn_k1"]).view(np.uint8))
    d["k2"] = np.ascontiguousarray(np.asarray(i["L0_ffn_k2"]).view(np.uint8))
    sm = np.zeros((128, 8), np.float32)
    sm[:, 0] = float(i["L0_ffn_s0"])
    sm[:, 1] = float(i["L0_ffn_s1"])
    sm[:, 2] = float(i["L0_ffn_s2"])
    d["smallv_d"] = sm
    for w, (kk, vv) in {"q": ("L1_attn_qK", "L1_attn_qV"), "k": ("L1_attn_kK", "L1_attn_kV"),
                        "v": ("L1_attn_vK", "L1_attn_vV"), "p": ("L1_attn_pK", "L1_attn_pV")}.items():
        d[f"{w}KT"] = np.ascontiguousarray(np.asarray(i[kk]).T)
        d[f"{w}V"] = np.ascontiguousarray(np.asarray(i[vv]))
    d["fKT"] = np.ascontiguousarray(np.asarray(i["L1_attn_fK"]).T)
    d["fV"] = np.ascontiguousarray(np.asarray(i["L1_attn_fV"]))
    d["w1T"] = np.ascontiguousarray(np.asarray(i["L1_ffn_w1_w"]).T)
    d["w2T"] = np.ascontiguousarray(np.asarray(i["L1_ffn_w2_w"]).T)
    d["h1T"] = np.ascontiguousarray(np.asarray(i["L1_ffn_h1_w"]).T)
    d["h2T"] = np.ascontiguousarray(np.asarray(i["L1_ffn_h2_w"]).T)
    d["w3T"] = np.ascontiguousarray(np.asarray(i["L1_ffn_w3_w"]).T)
    sc = np.zeros((128, 128), np.float32)
    sc[:, 0:32] = np.asarray(i["L1_ffn_w1_s"]).reshape(32, 128).T
    sc[:, 32:64] = np.asarray(i["L1_ffn_w2_s"]).reshape(32, 128).T
    sc[:, 64:96] = np.asarray(i["L1_ffn_h1_s"]).reshape(32, 128).T
    sc[:, 96:128] = np.asarray(i["L1_ffn_h2_s"]).reshape(32, 128).T
    d["sc32_d"] = sc
    d["w3sb_d"] = np.ascontiguousarray(np.broadcast_to(np.asarray(i["L1_ffn_w3_s"])[None, :], (128, C)))
    d["idn_d"] = np.eye(128, dtype=np.float32)
    bd = np.zeros((128, 128), np.float32)
    for g in range(4):
        bd[32 * g:32 * (g + 1), 32 * g:32 * (g + 1)] = 1.0
    d["bdiag_d"] = bd
    s4 = np.zeros((4, 128), np.float32)
    for g in range(4):
        s4[g, 32 * g:32 * (g + 1)] = 1.0
    d["sel4_d"] = s4
    cm = np.zeros((128, 128), np.float32)
    cm[np.triu_indices(128, 1)] = NEG
    d["cmask_d"] = cm
    nh = np.zeros((32, 128), np.float32)
    nh[0, :] = -0.5
    d["nh_d"] = nh
    return {k: np.ascontiguousarray(np.asarray(v)) for k, v in d.items()}


def kernel(**inputs):
    inputs = {k: np.asarray(v) for k, v in inputs.items()}
    x = inputs["L0_attn_x"]
    B, T, C = x.shape
    shared = prep_weights(inputs)
    nc = build_nc(T=T)
    in_maps = []
    for b in range(B):
        m = dict(shared)
        m["x"] = np.ascontiguousarray(x[b])
        in_maps.append(m)
    res = run_bass_kernel_spmd(nc, in_maps, core_ids=list(range(8)))
    out = np.stack([np.asarray(r["y"]) for r in res.results], axis=0)
    return out.astype(np.float32)
